# version 06
# speedup vs baseline: 1.0641x; 1.0456x over previous
import os
import numpy as np
import ml_dtypes
from contextlib import ExitStack

import concourse.bass as bass
import concourse.mybir as mybir
from concourse.bass_utils import run_bass_kernel_spmd

F32 = mybir.dt.float32
BF16 = mybir.dt.bfloat16
I32 = mybir.dt.int32
U32 = mybir.dt.uint32
AF = mybir.ActivationFunctionType
ALU = mybir.AluOpType
AX = mybir.AxisListType
BF = ml_dtypes.bfloat16

B, T, D = 8, 2048, 1024
NCH = D // 128
DS = 64
LSC = 64
NSC = T // LSC
E = 8
H0 = 4096
NH0 = H0 // 128
CAP = 3072
CLOC = 640
NU = T // 128
NV = T // 512
LN2 = float(np.log(2.0))
PSCL = 20.0 * LN2


class Sem:
    def __init__(self, es, nc, name):
        self.h = es.enter_context(nc.semaphore(name))
        self.n = 0

    def inc(self, instr, k=1):
        instr.then_inc(self.h, k)
        self.n += k
        return instr

    def tok(self):
        return (self, self.n)


def host_consts():
    return {
        "identf": np.eye(128, dtype=np.float32),
        "identb": np.eye(128, dtype=BF),
        "tril_s": np.triu(np.ones((128, 128), np.float32), 1).astype(BF),
        "mask_ul": np.triu(np.ones((LSC, LSC), np.float32), 0),
        "ecb": np.broadcast_to(np.arange(E, dtype=np.float32)[None, :], (128, E)).copy(),
    }


def cpref_for_core(b):
    m = np.zeros((E,), np.float32)
    m[:b] = 1.0
    return m


def build_l0(dbgkeys=(), upto=9):
    nc = bass.Bass("TRN2", target_bir_lowering=False)
    dt = nc.dram_tensor
    x_in = dt("x", [T, D], F32, kind="ExternalInput").ap()
    Wq = dt("Wq", [DS, D], F32, kind="ExternalInput").ap()
    bq = dt("bq", [DS], F32, kind="ExternalInput").ap()
    Wk = dt("Wk", [DS, D], F32, kind="ExternalInput").ap()
    bk = dt("bk", [DS], F32, kind="ExternalInput").ap()
    Wv = dt("Wv", [DS, D], F32, kind="ExternalInput").ap()
    bv = dt("bv", [DS], F32, kind="ExternalInput").ap()
    dec = dt("dec", [DS], F32, kind="ExternalInput").ap()
    Wo = dt("Wo", [D, DS], F32, kind="ExternalInput").ap()
    bo = dt("bo", [D], F32, kind="ExternalInput").ap()
    Wr = dt("Wr", [D, E], F32, kind="ExternalInput").ap()
    w1 = dt("w1", [E, D, H0], F32, kind="ExternalInput").ap()
    b1 = dt("b1", [E, H0], F32, kind="ExternalInput").ap()
    w2 = dt("w2", [E, H0, D], F32, kind="ExternalInput").ap()
    b2 = dt("b2", [E, D], F32, kind="ExternalInput").ap()
    identf = dt("identf", [128, 128], F32, kind="ExternalInput").ap()
    identb = dt("identb", [128, 128], BF16, kind="ExternalInput").ap()
    tril_s = dt("tril_s", [128, 128], BF16, kind="ExternalInput").ap()
    mask_ul = dt("mask_ul", [LSC, LSC], F32, kind="ExternalInput").ap()
    ecb_in = dt("ecb", [128, E], F32, kind="ExternalInput").ap()
    cpref = dt("cpref", [E], F32, kind="ExternalInput").ap()
    x1_out = dt("x1", [D, T], F32, kind="ExternalOutput").ap()
    dbg = {}
    if "xn" in dbgkeys:
        dbg["xn"] = dt("dbg_xn", [D, T], F32, kind="ExternalOutput").ap()
    if "qkv" in dbgkeys:
        for nm in ("q", "k", "v", "scan"):
            dbg[nm] = dt("dbg_" + nm, [DS, T], F32, kind="ExternalOutput").ap()
    if "route" in dbgkeys:
        for nm in ("e1", "e2", "g1", "g2", "lp1", "lp2", "id1", "id2"):
            dbg[nm] = dt("dbg_" + nm, [128, NU], F32, kind="ExternalOutput").ap()

    buckets = dt("buckets", [E * CLOC, D], BF16)
    exp_out_d = dt("exp_out_d", [E * CLOC, D], BF16)
    cc_in = dt("cc_in", [1, E], F32)
    cc_out = dt("cc_out", [8, E], F32)

    es = ExitStack()

    def sb(name, shape, dtp):
        return es.enter_context(nc.sbuf_tensor(name, shape, dtp))

    def pst(name, shape, dtp=F32):
        return es.enter_context(nc.psum_tensor(name, shape, dtp))

    resid = sb("resid", [128, NCH * T], F32)
    idf = sb("idf", [128, 128], F32)
    idb = sb("idb", [128, 128], BF16)
    trilS = sb("trilS", [128, 128], BF16)
    mUL = sb("mUL", [LSC, LSC], F32)
    ecbS = sb("ecbS", [128, E], F32)
    onesc = sb("onesc", [1, 128], F32)
    onescb = sb("onescb", [1, 128], BF16)
    onesrb = sb("onesrb", [1, 512], BF16)
    onesr = sb("onesr", [1, 512], F32)
    onescol = sb("onescol", [128, 1], BF16)
    srow_m = sb("srow_m", [1, 512], F32)
    srow_r = sb("srow_r", [1, 512], F32)
    vrow = sb("vrow", [1, 512], F32)
    vrow2 = sb("vrow2", [1, 512], F32)
    offs = sb("offs", [1, E], F32)
    coffs = sb("coffs", [1, E], F32)
    coffb = sb("coffb", [128, E], F32)
    cmask = sb("cmask", [E, 1], F32)
    cntrow = sb("cntrow", [1, E], F32)
    agrows = sb("agrows", [8, E], F32)
    g1c = sb("g1c", [128, NU], F32)
    g2c = sb("g2c", [128, NU], F32)
    e1c = sb("e1c", [128, NU], F32)
    e2c = sb("e2c", [128, NU], F32)
    mcolT = sb("mcolT", [128, NU], F32)
    rcolT = sb("rcolT", [128, NU], F32)
    csb = sb("csb", [128, E], F32)
    onescolf = sb("onescolf", [128, 1], F32)
    idx1 = sb("idx1", [128, NU], U32)
    idx2 = sb("idx2", [128, NU], U32)
    dbgstage = sb("dbgstage", [DS, 512], F32) if dbgkeys else None
    dbg_cols = {}
    if "route" in dbgkeys:
        for nm in ("e1", "e2", "g1", "g2", "lp1", "lp2"):
            dbg_cols[nm] = sb("dbgc_" + nm, [128, NU], F32)

    pb = [pst(f"pb{i}", [128, 512], F32) for i in range(6)]
    pbf = pst("pbf", [128, 1024], BF16)
    pbf2 = pst("pbf2", [128, 1024], BF16)

    sd = Sem(es, nc, "sd")
    sw = Sem(es, nc, "sw")
    sx = [Sem(es, nc, "sx0"), Sem(es, nc, "sx1")]
    sgt = [Sem(es, nc, "sgt0"), Sem(es, nc, "sgt1")]
    sw1 = [Sem(es, nc, "sw1a"), Sem(es, nc, "sw1b")]
    sw2 = [Sem(es, nc, "sw2a"), Sem(es, nc, "sw2b")]
    setm = [Sem(es, nc, "set0"), Sem(es, nc, "set1")]
    scc = Sem(es, nc, "scc")
    sgd = Sem(es, nc, "sgd")
    sv = Sem(es, nc, "sv")
    sp = Sem(es, nc, "sp")
    sa = Sem(es, nc, "sa")
    sg = Sem(es, nc, "sg")
    st = Sem(es, nc, "st")
    ALLSEMS = [sd, sw, sx[0], sx[1], sgt[0], sgt[1], sw1[0], sw1[1], sw2[0], sw2[1],
               setm[0], setm[1], scc, sgd, sv, sp, sa, sg, st]

    block = es.enter_context(nc.Block())
    ENG = {"S": block.sync, "V": block.vector, "A": block.scalar,
           "P": block.tensor, "G": block.gpsimd}

    def emit(eng, deps=(), inc=None, k=1):
        reg = ENG[eng]
        deps = [d for d in deps if d is not None]

        def deco(body):
            def _f(e):
                seen = {}
                for sem, val in deps:
                    if seen.get(id(sem), -1) < val:
                        seen[id(sem)] = val
                        e.wait_ge(sem.h, val)
                last = body(e)
                if inc is not None and last is not None:
                    inc.inc(last, k)
            reg(_f)
        return deco

    def chain(e, sem):
        e.wait_ge(sem.h, sem.n)

    pr = {}

    def war(key):
        return pr.get(key)

    def setread(key, tok):
        pr[key] = tok

    x_t = x_in.rearrange("(n p) d -> n p d", p=128)

    esA = ExitStack()

    def sbA(name, shape, dtp):
        return esA.enter_context(nc.sbuf_tensor(name, shape, dtp))

    xstage = [sbA("xstage0", [128, D], F32), sbA("xstage1", [128, D], F32)]
    wq_f = sbA("wq_f", [DS, D], F32)
    wk_f = sbA("wk_f", [DS, D], F32)
    wv_f = sbA("wv_f", [DS, D], F32)
    wo_f = sbA("wo_f", [128, NCH * DS], F32)
    wr_f = sbA("wr_f", [128, NCH * E], F32)
    wq_b = sbA("wq_b", [DS, D], BF16)
    wk_b = sbA("wk_b", [DS, D], BF16)
    wv_b = sbA("wv_b", [DS, D], BF16)
    wo_b = sbA("wo_b", [128, NCH * DS], BF16)
    wqT = sbA("wqT", [128, NCH * DS], BF16)
    wkT = sbA("wkT", [128, NCH * DS], BF16)
    wvT = sbA("wvT", [128, NCH * DS], BF16)
    woT = sbA("woT", [DS, D], BF16)
    wrS = sbA("wrS", [128, NCH * E], BF16)
    brow_f = sbA("brow_f", [1, 3 * DS + D], F32)
    brow = sbA("brow", [1, 3 * DS + D], BF16)
    bqkvc = sbA("bqkvc", [DS, 3], F32)
    dcol = sbA("dcol", [DS, 1], F32)
    dsig = sbA("dsig", [DS, 1], F32)
    lndc = sbA("lndc", [DS, 1], F32)
    lndr = sbA("lndr", [1, DS], F32)
    iotar = sbA("iotar", [1, LSC], F32)
    iotarr = sbA("iotarr", [1, LSC], F32)
    iotai = sbA("iotai", [1, LSC], I32)
    dpm = sbA("dpm", [LSC, DS], BF16)
    dpp = sbA("dpp", [LSC, DS], F32)
    dpr = sbA("dpr", [LSC, DS], BF16)
    drow = sbA("drow", [DS, DS], F32)
    dLrow = sbA("dLrow", [DS, DS], F32)
    xn = sbA("xn", [128, NCH * T], BF16)
    xbp = [sbA("xb0", [128, 512], BF16), sbA("xb1", [128, 512], BF16)]
    xsqp = [sbA("xsq0", [128, 512], BF16), sbA("xsq1", [128, 512], BF16)]
    xtmp = sbA("xtmp", [128, 512], F32)
    qf = sbA("qf", [DS, T], BF16)
    kf = sbA("kf", [DS, T], BF16)
    vf = sbA("vf", [DS, T], BF16)
    kT = sbA("kT", [DS, NSC * DS], BF16)
    vT = sbA("vT", [DS, NSC * DS], BF16)
    of = sbA("of", [DS, T], BF16)
    St = sbA("St", [DS, DS], F32)
    St_b = sbA("St_b", [DS, DS], BF16)
    uta = sbA("uta", [LSC, DS], BF16)
    utb = sbA("utb", [LSC, DS], BF16)
    Ata = sbA("Ata", [LSC, LSC], BF16)
    otok = sbA("otok", [LSC, DS], BF16)
    lgs = sbA("lgs", [E, 512], F32)
    lgT = sbA("lgT", [128, NU * E], F32)
    qexp = sbA("qexp", [128, E], F32)
    oh1f = sbA("oh1f", [128, E], F32)
    oh2f = sbA("oh2f", [128, E], F32)
    ohs_all = sbA("ohs_all", [128, NU * E], BF16)
    oh1_all = sbA("oh1_all", [128, NU * E], F32)
    oh2_all = sbA("oh2_all", [128, NU * E], F32)
    cum_u = sbA("cum_u", [128, E], F32)
    rc = {nm: sbA("rc_" + nm, [128, 1], F32) for nm in
          ("nmax", "sum", "rec", "m1", "m2", "e1", "e2", "lp1", "lp2",
           "gp1", "gp2", "k1", "k2", "ob1", "ob2", "t1", "t2", "if1", "if2")}
    rtmp = sbA("rtmp", [128, E], F32)
    rtmp2 = sbA("rtmp2", [128, E], F32)
    xtok = [sbA("xtok0", [128, D], BF16), sbA("xtok1", [128, D], BF16)]

    @emit("S")
    def _(s):
        sw.inc(s.dma_start(out=idf[:], in_=identf), 16)
        sw.inc(s.dma_start(out=idb[:], in_=identb), 16)
        sw.inc(s.dma_start(out=trilS[:], in_=tril_s), 16)
        sw.inc(s.dma_start(out=mUL[:], in_=mask_ul), 16)
        sw.inc(s.dma_start(out=ecbS[:], in_=ecb_in), 16)
        sw.inc(s.dma_start(out=cmask[:], in_=cpref[:, None]), 16)
        sw.inc(s.dma_start(out=dcol[:], in_=dec[:, None]), 16)
        sw.inc(s.dma_start(out=bqkvc[:, 0:1], in_=bq[:, None]), 16)
        sw.inc(s.dma_start(out=bqkvc[:, 1:2], in_=bk[:, None]), 16)
        sw.inc(s.dma_start(out=bqkvc[:, 2:3], in_=bv[:, None]), 16)
        sw.inc(s.dma_start(out=wq_f[:], in_=Wq), 16)
        sw.inc(s.dma_start(out=wk_f[:], in_=Wk), 16)
        sw.inc(s.dma_start(out=wv_f[:], in_=Wv), 16)
        sw.inc(s.dma_start(out=wo_f[:].rearrange("p (c s) -> p c s", c=NCH), in_=Wo.rearrange("(c p) s -> p c s", p=128)), 16)
        sw.inc(s.dma_start(out=wr_f[:].rearrange("p (c e) -> p c e", c=NCH), in_=Wr.rearrange("(c p) e -> p c e", p=128)), 16)
        sw.inc(s.dma_start(out=brow_f[0:1, 0:DS], in_=bq[None, :]), 16)
        sw.inc(s.dma_start(out=brow_f[0:1, DS:2 * DS], in_=bk[None, :]), 16)
        sw.inc(s.dma_start(out=brow_f[0:1, 2 * DS:3 * DS], in_=bv[None, :]), 16)
        sw.inc(s.dma_start(out=brow_f[0:1, 3 * DS:], in_=bo[None, :]), 16)
        return None
    t_wload = sw.tok()

    @emit("G", inc=sg)
    def _(g):
        g.memset(onesc[:], 1.0)
        g.memset(onescb[:], 1.0)
        g.memset(onesrb[:], 1.0)
        g.memset(onesr[:], 1.0)
        g.memset(onescol[:], 1.0)
        g.memset(onescolf[:], 1.0)
        g.memset(offs[:], 0.0)
        return g.iota(out=iotai[:], base=0, pattern=[[1, LSC]], channel_multiplier=0)
    t_ones = sg.tok()

    @emit("V", deps=[t_wload, t_ones], inc=sv)
    def _(v):
        v.tensor_copy(out=iotar[:], in_=iotai[:])
        v.tensor_copy(out=wq_b[:], in_=wq_f[:])
        v.tensor_copy(out=wk_b[:], in_=wk_f[:])
        v.tensor_copy(out=wv_b[:], in_=wv_f[:])
        v.tensor_copy(out=wo_b[:], in_=wo_f[:])
        v.tensor_copy(out=wrS[:], in_=wr_f[:])
        return v.tensor_copy(out=brow[:], in_=brow_f[:])
    t_wcast = sv.tok()

    @emit("G", deps=[t_wcast], inc=sg)
    def _(g):
        return g.iota(out=iotai[:], base=LSC - 1, pattern=[[-1, LSC]], channel_multiplier=0)
    t_iota2 = sg.tok()

    @emit("V", deps=[t_iota2], inc=sv)
    def _(v):
        return v.tensor_copy(out=iotarr[:], in_=iotai[:])
    t_iotarr = sv.tok()

    @emit("P", deps=[t_wcast], inc=sp)
    def _(p):
        for c in range(NCH):
            p.transpose(pbf[:, c * 64:(c + 1) * 64], wq_b[:, c * 128:(c + 1) * 128], idb[0:DS, 0:DS])
        i = None
        for c in range(NCH):
            i = p.transpose(pbf[:, 512 + c * 64: 512 + (c + 1) * 64], wk_b[:, c * 128:(c + 1) * 128], idb[0:DS, 0:DS])
        return i
    t_trqk = sp.tok()

    @emit("V", deps=[t_trqk], inc=sv)
    def _(v):
        v.tensor_copy(out=wqT[:], in_=pbf[:, 0:512])
        return v.tensor_copy(out=wkT[:], in_=pbf[:, 512:1024])
    t_evqk = sv.tok()
    setread("pbf", t_evqk)

    @emit("P", deps=[t_wcast, war("pbf")], inc=sp)
    def _(p):
        i = None
        for c in range(NCH):
            i = p.transpose(pbf[:, c * 64:(c + 1) * 64], wv_b[:, c * 128:(c + 1) * 128], idb[0:DS, 0:DS])
        return i
    t_trv = sp.tok()

    @emit("V", deps=[t_trv], inc=sv)
    def _(v):
        return v.tensor_copy(out=wvT[:], in_=pbf[:, 0:512])
    t_evv = sv.tok()
    setread("pbf", t_evv)

    @emit("P", deps=[t_evv], inc=sp)
    def _(p):
        i = None
        for c in range(NCH):
            i = p.transpose(pbf[0:DS, c * 128:(c + 1) * 128], wo_b[:, c * DS:(c + 1) * DS], idb[:])
        return i
    t_tro = sp.tok()

    @emit("V", deps=[t_tro], inc=sv)
    def _(v):
        return v.tensor_copy(out=woT[:], in_=pbf[0:DS, 0:1024])
    t_evvo = sv.tok()
    setread("pbf", t_evvo)

    @emit("A", deps=[t_wload], inc=sa)
    def _(a):
        sa.inc(a.activation(out=dsig[:], in_=dcol[:], func=AF.Sigmoid), 1)
        chain(a, sa)
        return a.activation(out=lndc[:], in_=dsig[:], func=AF.Ln)
    t_lnd = sa.tok()

    @emit("P", deps=[t_lnd, war("pb4")], inc=sp)
    def _(p):
        return p.transpose(pb[4][0:1, 0:DS], lndc[:], idf[0:DS, 0:DS])
    t_lndT = sp.tok()

    @emit("V", deps=[t_lndT], inc=sv)
    def _(v):
        return v.tensor_copy(out=lndr[:], in_=pb[4][0:1, 0:DS])
    t_lndr = sv.tok()
    setread("pb4", t_lndr)

    @emit("P", deps=[t_lndr, t_iotarr, war("pb4")], inc=sp)
    def _(p):
        p.matmul(pb[4][0:LSC, 0:DS], iotar[:], lndr[:], start=True, stop=True)
        p.matmul(pb[4][0:LSC, DS:2 * DS], iotarr[:], lndr[:], start=True, stop=True)
        return p.matmul(pb[4][0:DS, 2 * DS:3 * DS], onesc[0:1, 0:DS], lndr[:], start=True, stop=True)
    t_outer = sp.tok()

    @emit("A", deps=[t_outer], inc=sa)
    def _(a):
        a.activation(out=dpm[:], in_=pb[4][0:LSC, 0:DS], func=AF.Exp, scale=np.float32(-1.0), bias=np.float32(-PSCL))
        a.activation(out=dpp[:], in_=pb[4][0:LSC, 0:DS], func=AF.Exp, scale=np.float32(1.0), bias=np.float32(PSCL))
        a.activation(out=dpr[:], in_=pb[4][0:LSC, DS:2 * DS], func=AF.Exp)
        a.activation(out=drow[:], in_=pb[4][0:DS, 2 * DS:3 * DS], func=AF.Exp, scale=np.float32(1.0), bias=np.float32(-PSCL))
        return a.activation(out=dLrow[:], in_=pb[4][0:DS, 2 * DS:3 * DS], func=AF.Exp, scale=np.float32(LSC))
    t_tables = sa.tok()
    setread("pb4", t_tables)

    t_xtr = [None] * 16
    t_xev = [None] * 16
    for i in range(16):
        @emit("S", deps=([t_xtr[i - 2]] if i >= 2 else []), inc=sx[i % 2], k=16)
        def _(s, i=i):
            return s.dma_start(out=xstage[i % 2][:], in_=x_t[i])
        t_xload = sx[i % 2].tok()

        b0k, b1k = "pb%d" % (2 * (i % 2)), "pb%d" % (2 * (i % 2) + 1)

        @emit("P", deps=[t_xload, war(b0k), war(b1k)], inc=sp)
        def _(p, i=i):
            b0, b1 = pb[2 * (i % 2)], pb[2 * (i % 2) + 1]
            for c in range(4):
                p.transpose(b0[:, c * 128:(c + 1) * 128], xstage[i % 2][:, c * 128:(c + 1) * 128], idf[:])
            j = None
            for c in range(4):
                j = p.transpose(b1[:, c * 128:(c + 1) * 128], xstage[i % 2][:, (4 + c) * 128:(5 + c) * 128], idf[:])
            return j
        t_xtr[i] = sp.tok()

        @emit("V", deps=[t_xtr[i]], inc=sv)
        def _(v, i=i):
            b0, b1 = pb[2 * (i % 2)], pb[2 * (i % 2) + 1]
            rv = resid[:].rearrange("p (c t) -> p c t", c=NCH)[:, :, i * 128:(i + 1) * 128]
            v.tensor_copy(out=rv[:, 0:4, :], in_=b0[:].rearrange("p (c t) -> p c t", c=4))
            return v.tensor_copy(out=rv[:, 4:8, :], in_=b1[:].rearrange("p (c t) -> p c t", c=4))
        t_xev[i] = sv.tok()
        setread(b0k, t_xev[i])
        setread(b1k, t_xev[i])

    if upto >= 2:
        t_xn = [[None] * NV for _ in range(NCH)]
        t_residln = [[None] * NV for _ in range(NCH)]
        t_norm_prev = None
        for v_i in range(NV):
            dep_x = [t_xev[4 * v_i + c] for c in range(4)]
            t_mmprev = None
            t_cast = [None] * NCH
            for c in range(NCH):
                pp = c % 2

                @emit("V", deps=dep_x + [t_mmprev], inc=sv)
                def _(v, c=c, v_i=v_i, pp=pp):
                    return v.tensor_copy(out=xbp[pp][:], in_=resid[:, c * T + v_i * 512: c * T + v_i * 512 + 512])
                t_cast[c] = sv.tok()

                @emit("V", deps=[t_cast[c]], inc=sv)
                def _(v, pp=pp):
                    return v.tensor_tensor(out=xsqp[pp][:], in0=xbp[pp][:], in1=xbp[pp][:], op=ALU.mult)
                t_sq = sv.tok()

                @emit("P", deps=[t_cast[c], t_sq, t_ones, war("pb4"), war("pb5")], inc=sp)
                def _(p, c=c, pp=pp):
                    p.matmul(pb[4][0:1, 0:512], onescol[:], xbp[pp][:], start=(c == 0), stop=(c == NCH - 1))
                    return p.matmul(pb[5][0:1, 0:512], onescol[:], xsqp[pp][:], start=(c == 0), stop=(c == NCH - 1))
                t_mmprev = sp.tok()
            t_stats = t_mmprev

            @emit("V", deps=[t_stats], inc=sv)
            def _(v, v_i=v_i):
                v.tensor_scalar(out=srow_m[:], in0=pb[4][0:1, 0:512], scalar1=1.0 / D, scalar2=None, op0=ALU.mult)
                sv.inc(v.tensor_scalar(out=vrow2[:], in0=pb[5][0:1, 0:512], scalar1=1.0 / D, scalar2=None, op0=ALU.mult), 1)
                chain(v, sv)
                sv.inc(v.tensor_tensor(out=vrow[:], in0=srow_m[:], in1=srow_m[:], op=ALU.mult), 1)
                chain(v, sv)
                return v.tensor_tensor(out=vrow[:], in0=vrow2[:], in1=vrow[:], op=ALU.subtract)
            t_var = sv.tok()
            setread("pb4", t_var)
            setread("pb5", t_var)

            @emit("A", deps=[t_var], inc=sa)
            def _(a, v_i=v_i):
                return a.activation(out=vrow[:], in_=vrow[:], func=AF.Sqrt, bias=np.float32(1e-5))
            t_sqrt = sa.tok()

            @emit("V", deps=[t_sqrt], inc=sv)
            def _(v, v_i=v_i):
                return v.reciprocal(out=srow_r[:], in_=vrow[:])
            t_rstd = sv.tok()

            @emit("P", deps=[t_rstd, t_ones, war("pb4"), war("pb5")], inc=sp)
            def _(p, v_i=v_i):
                p.matmul(pb[4][:, 0:512], onesc[:], srow_m[:], start=True, stop=True)
                return p.matmul(pb[5][:, 0:512], onesc[:], srow_r[:], start=True, stop=True)
            t_bcast = sp.tok()

            @emit("P", deps=dep_x + [t_wload, war("pb3")], inc=sp)
            def _(p, v_i=v_i):
                i = None
                for c in range(NCH):
                    i = p.matmul(pb[3][0:E, :], wr_f[:, c * E:(c + 1) * E],
                                 resid[:, c * T + v_i * 512: c * T + v_i * 512 + 512],
                                 start=(c == 0), stop=(c == NCH - 1))
                return i
            t_Ax = sp.tok()

            @emit("V", deps=[t_Ax], inc=sv)
            def _(v):
                return v.tensor_copy(out=lgs[:], in_=pb[3][0:E, :])
            t_Axs = sv.tok()
            setread("pb3", t_Axs)

            @emit("P", deps=[t_Axs, t_rstd, war("pb2")], inc=sp)
            def _(p, v_i=v_i):
                for j in range(4):
                    p.transpose(pb[2][:, j * E:(j + 1) * E], lgs[:, j * 128:(j + 1) * 128], idf[0:E, 0:E])
                for j in range(4):
                    p.matmul(pb[2][:, 32 + j: 33 + j], srow_m[0:1, j * 128:(j + 1) * 128],
                             onesr[0:1, 0:1], start=True, stop=True)
                i = None
                for j in range(4):
                    i = p.matmul(pb[2][:, 40 + j: 41 + j], srow_r[0:1, j * 128:(j + 1) * 128],
                                 onesr[0:1, 0:1], start=True, stop=True)
                return i
            t_Axt = sp.tok()

            @emit("V", deps=[t_Axt], inc=sv)
            def _(v, v_i=v_i):
                v.tensor_copy(out=lgT[:, v_i * 4 * E:(v_i + 1) * 4 * E], in_=pb[2][:, 0:4 * E])
                v.tensor_copy(out=mcolT[:, v_i * 4:(v_i + 1) * 4], in_=pb[2][:, 32:36])
                return v.tensor_copy(out=rcolT[:, v_i * 4:(v_i + 1) * 4], in_=pb[2][:, 40:44])
            t_AxT = sv.tok()
            setread("pb2", t_AxT)

            for c in range(NCH):
                @emit("V", deps=[t_bcast, t_norm_prev], inc=sv)
                def _(v, c=c, v_i=v_i):
                    sl = slice(c * T + v_i * 512, c * T + v_i * 512 + 512)
                    sv.inc(v.tensor_tensor(out=xtmp[:], in0=resid[:, sl], in1=pb[4][:, 0:512], op=ALU.subtract), 1)
                    chain(v, sv)
                    return v.tensor_tensor(out=xn[:, sl], in0=xtmp[:], in1=pb[5][:, 0:512], op=ALU.mult)
                t_xn[c][v_i] = sv.tok()
                t_norm_prev = t_xn[c][v_i]

                @emit("V", deps=[t_xn[c][v_i], t_Ax], inc=sv)
                def _(v, c=c, v_i=v_i):
                    sl = slice(c * T + v_i * 512, c * T + v_i * 512 + 512)
                    return v.tensor_tensor(out=resid[:, sl], in0=resid[:, sl], in1=xn[:, sl], op=ALU.add)
                t_residln[c][v_i] = sv.tok()
            setread("pb4", t_norm_prev)
            setread("pb5", t_norm_prev)

        if "xn" in dbg:
            for c in range(NCH):
                @emit("G", deps=[t_xn[c][v] for v in range(NV)], inc=st, k=16)
                def _(g, c=c):
                    return g.dma_start(out=dbg["xn"][c * 128:(c + 1) * 128, :], in_=xn[:, c * T:(c + 1) * T])

    if upto >= 3:
        t_q = [None] * NV
        t_k = [None] * NV
        t_v = [None] * NV
        for v_i in range(NV):
            dep_xn = [t_xn[c][v_i] for c in range(NCH)]
            for pi, (wT, broff, dst, tlist) in enumerate(
                    ((wqT, 0, qf, t_q), (wkT, DS, kf, t_k), (wvT, 2 * DS, vf, t_v))[:int(os.environ.get("L0QKV", "3"))]):
                bk_ = "pb%d" % (pi % 2)

                @emit("P", deps=dep_xn + [t_evqk, t_evvo, war(bk_)], inc=sp)
                def _(p, v_i=v_i, wT=wT, broff=broff, pi=pi):
                    bank = pb[pi % 2]
                    i = None
                    for c in range(NCH):
                        i = p.matmul(bank[0:DS, :], wT[:, c * DS:(c + 1) * DS],
                                     xn[:, c * T + v_i * 512: c * T + v_i * 512 + 512],
                                     start=(c == 0), stop=(c == NCH - 1))
                    return i
                t_mm = sp.tok()

                @emit("V", deps=[t_mm, t_wload], inc=sv)
                def _(v, v_i=v_i, dst=dst, pi=pi):
                    return v.tensor_scalar(out=dst[:, v_i * 512:(v_i + 1) * 512], in0=pb[pi % 2][0:DS, :],
                                           scalar1=bqkvc[:, pi:pi + 1], scalar2=None, op0=ALU.add)
                tlist[v_i] = sv.tok()
                setread(bk_, tlist[v_i])

        t_kvT = [None] * NSC
        for ci in range(int(os.environ.get("L0KVT", str(NSC)))):
            v_i = ci // 8
            bk = pbf if ci % 2 == 0 else pbf2
            bkk = "pbf" if ci % 2 == 0 else "pbf2"

            @emit("P", deps=[t_k[v_i], t_v[v_i], war(bkk)], inc=sp)
            def _(p, ci=ci, bk=bk):
                p.transpose(bk[0:DS, 0:64], kf[:, ci * 64:(ci + 1) * 64], idb[0:DS, 0:DS])
                return p.transpose(bk[0:DS, 64:128], vf[:, ci * 64:(ci + 1) * 64], idb[0:DS, 0:DS])
            t_tr = sp.tok()

            @emit("V", deps=[t_tr], inc=sv)
            def _(v, ci=ci, bk=bk):
                v.tensor_copy(out=kT[:, ci * DS:(ci + 1) * DS], in_=bk[0:DS, 0:64])
                return v.tensor_copy(out=vT[:, ci * DS:(ci + 1) * DS], in_=bk[0:DS, 64:128])
            t_kvT[ci] = sv.tok()
            setread(bkk, t_kvT[ci])

    if upto >= 4:
        @emit("G", deps=[t_tables], inc=sg)
        def _(g):
            return g.memset(St[:], 0.0)
        t_St = sg.tok()

        t_Ata_prev = None
        t_ops_prev = None
        t_otok_prev = None
        t_otr_prev = None
        t_oev_prev = None
        t_G_prev = None
        t_of = [None] * NSC
        for ci in range(NSC):
            u = ci // 2
            half = ci % 2
            v_i = ci // 8
            q_sl = qf[:, ci * LSC:(ci + 1) * LSC]
            k_sl = kf[:, ci * LSC:(ci + 1) * LSC]
            kT_sl = kT[:, ci * DS:(ci + 1) * DS]
            vT_sl = vT[:, ci * DS:(ci + 1) * DS]

            @emit("P", deps=[t_q[v_i], t_k[v_i], t_Ata_prev or t_tables, war("pb0")], inc=sp)
            def _(p, q_sl=q_sl, k_sl=k_sl):
                return p.matmul(pb[0][0:LSC, 0:LSC], k_sl, q_sl, start=True, stop=True)
            t_A = sp.tok()

            @emit("V", deps=[t_A, t_ops_prev], inc=sv)
            def _(v):
                return v.tensor_tensor(out=Ata[:], in0=pb[0][0:LSC, 0:LSC], in1=mUL[:], op=ALU.mult)
            t_Ata = sv.tok()
            setread("pb0", t_Ata)

            @emit("V", deps=[t_kvT[ci], t_St, t_ops_prev, t_G_prev, t_tables], inc=sv)
            def _(v, vT_sl=vT_sl):
                v.tensor_tensor(out=uta[:], in0=vT_sl, in1=dpm[:], op=ALU.mult)
                v.tensor_tensor(out=utb[:], in0=vT_sl, in1=dpr[:], op=ALU.mult)
                return v.tensor_tensor(out=St_b[:], in0=St[:], in1=drow[:], op=ALU.mult)
            t_u = sv.tok()

            @emit("P", deps=[t_Ata, t_u, t_otok_prev, war("pb1")], inc=sp)
            def _(p, q_sl=q_sl):
                p.matmul(pb[1][0:LSC, 0:DS], Ata[:], uta[:], start=True, stop=False)
                return p.matmul(pb[1][0:LSC, 0:DS], q_sl, St_b[:], start=False, stop=True)
            t_ops = sp.tok()

            @emit("V", deps=[t_ops, t_otr_prev], inc=sv)
            def _(v):
                return v.tensor_tensor(out=otok[:], in0=pb[1][0:LSC, 0:DS], in1=dpp[:], op=ALU.mult)
            t_otok = sv.tok()
            setread("pb1", t_otok)

            @emit("P", deps=[t_otok, t_oev_prev, war("pbf2")], inc=sp)
            def _(p):
                return p.transpose(pbf2[0:DS, 896:960], otok[:], idb[0:LSC, 0:LSC])
            t_otr = sp.tok()

            @emit("A", deps=[t_otr], inc=sa)
            def _(a, ci=ci):
                return a.copy(out=of[:, ci * LSC:(ci + 1) * LSC], in_=pbf2[0:DS, 896:960])
            t_oev = sa.tok()
            setread("pbf2", t_oev)
            t_of[ci] = t_oev

            @emit("P", deps=[t_u, t_kvT[ci], t_St, war("pb2")], inc=sp)
            def _(p, kT_sl=kT_sl):
                return p.matmul(pb[2][0:DS, 0:DS], kT_sl, utb[:], start=True, stop=True)
            t_G = sp.tok()
            t_G_prev = t_G

            @emit("V", deps=[t_G, t_u], inc=sv)
            def _(v):
                return v.tensor_tensor(out=St[:], in0=St[:], in1=dLrow[:], op=ALU.mult)
            t_sd1 = sv.tok()

            @emit("V", deps=[t_sd1], inc=sv)
            def _(v):
                return v.tensor_tensor(out=St[:], in0=St[:], in1=pb[2][0:DS, 0:DS], op=ALU.add)
            t_St = sv.tok()
            setread("pb2", t_St)

            t_Ata_prev, t_ops_prev, t_otok_prev = t_Ata, t_ops, t_otok
            t_otr_prev, t_oev_prev = t_otr, t_oev

        if "q" in dbg:
            t_dprev = None
            for nm, src in (("scan", of), ("q", qf), ("k", kf), ("v", vf)):
                for v_i in range(NV):
                    dep = [t_of[NSC - 1]] if nm == "scan" else [t_q[v_i], t_k[v_i], t_v[v_i]]

                    @emit("V", deps=dep + [t_dprev], inc=sv)
                    def _(v, src=src, v_i=v_i):
                        return v.tensor_copy(out=dbgstage[:], in_=src[:, v_i * 512:(v_i + 1) * 512])
                    t_c = sv.tok()

                    @emit("S", deps=[t_c], inc=st, k=16)
                    def _(s, nm=nm, v_i=v_i):
                        return s.dma_start(out=dbg[nm][:, v_i * 512:(v_i + 1) * 512], in_=dbgstage[:])
                    t_dprev = st.tok()

    if upto >= 5:
        t_op = [[None] * NV for _ in range(NCH)]
        for v_i in range(NV):
            for c in range(NCH):
                bk_ = "pb%d" % (c % 2)

                @emit("P", deps=[t_of[8 * v_i + 7], t_evvo, war(bk_)], inc=sp)
                def _(p, c=c, v_i=v_i):
                    bank = pb[c % 2]
                    p.matmul(bank[:, :], woT[:, c * 128:(c + 1) * 128], of[:, v_i * 512:(v_i + 1) * 512],
                             start=True, stop=False)
                    return p.matmul(bank[:, :], brow[0:1, 3 * DS + c * 128: 3 * DS + (c + 1) * 128],
                                    onesrb[:], start=False, stop=True)
                t_mm = sp.tok()

                @emit("V", deps=[t_mm, t_residln[c][v_i]], inc=sv)
                def _(v, c=c, v_i=v_i):
                    sl = slice(c * T + v_i * 512, c * T + v_i * 512 + 512)
                    return v.tensor_tensor(out=resid[:, sl], in0=resid[:, sl], in1=pb[c % 2][:, :], op=ALU.add)
                t_op[c][v_i] = sv.tok()
                setread(bk_, t_op[c][v_i])

    if upto >= 6:
        @emit("P", deps=[t_wload, t_ones, war("pb3")], inc=sp)
        def _(p):
            i = None
            for c in range(NCH):
                i = p.matmul(pb[3][0:E, 0:1], wr_f[:, c * E:(c + 1) * E], onescolf[:],
                             start=(c == 0), stop=(c == NCH - 1))
            return i
        t_cs1 = sp.tok()

        @emit("V", deps=[t_cs1], inc=sv)
        def _(v):
            return v.tensor_copy(out=lgs[0:E, 0:1], in_=pb[3][0:E, 0:1])
        t_cs2 = sv.tok()
        setread("pb3", t_cs2)

        @emit("P", deps=[t_cs2, war("pb3")], inc=sp)
        def _(p):
            return p.transpose(pb[3][0:1, 0:E], lgs[0:E, 0:1], idf[0:E, 0:E])
        t_cstr = sp.tok()

        @emit("V", deps=[t_cstr], inc=sv)
        def _(v):
            return v.tensor_copy(out=cntrow[:], in_=pb[3][0:1, 0:E])
        t_cs3 = sv.tok()
        setread("pb3", t_cs3)

        @emit("P", deps=[t_cs3, war("pb2")], inc=sp)
        def _(p):
            return p.matmul(pb[2][:, 0:E], onesc[:], cntrow[:], start=True, stop=True)
        t_cs4 = sp.tok()

        @emit("V", deps=[t_cs4], inc=sv)
        def _(v):
            return v.tensor_copy(out=csb[:], in_=pb[2][:, 0:E])
        t_csb = sv.tok()
        setread("pb2", t_csb)

        t_lgT = [None] * NU
        t_lgfix_prev = None
        for u in range(NU):
            v_i = u // 4

            @emit("V", deps=[t_AxT, t_csb, t_lgfix_prev], inc=sv)
            def _(v, u=u):
                sv.inc(v.tensor_scalar(out=rtmp[:], in0=csb[:], scalar1=mcolT[:, u:u + 1],
                                       scalar2=None, op0=ALU.mult), 1)
                chain(v, sv)
                sv.inc(v.tensor_tensor(out=lgT[:, u * E:(u + 1) * E], in0=lgT[:, u * E:(u + 1) * E],
                                       in1=rtmp[:], op=ALU.subtract), 1)
                chain(v, sv)
                return v.tensor_scalar(out=lgT[:, u * E:(u + 1) * E], in0=lgT[:, u * E:(u + 1) * E],
                                       scalar1=rcolT[:, u:u + 1], scalar2=None, op0=ALU.mult)
            t_lgT[u] = sv.tok()
            t_lgfix_prev = t_lgT[u]

        t_route = [None] * NU
        t_ru_prev = None
        for u in range(NU):
            @emit("V", deps=[t_lgT[u], t_ru_prev], inc=sv)
            def _(v, u=u):
                sl = lgT[:, u * E:(u + 1) * E]
                return v.tensor_reduce(out=rc["nmax"][:], in_=sl, op=ALU.max, axis=AX.X, negate=True)
            t1 = sv.tok()

            @emit("A", deps=[t1], inc=sa)
            def _(a, u=u):
                return a.activation(out=qexp[:], in_=lgT[:, u * E:(u + 1) * E], func=AF.Exp, bias=rc["nmax"][:])
            t2 = sa.tok()

            @emit("V", deps=[t2], inc=sv)
            def _(v, u=u):
                sv.inc(v.tensor_reduce(out=rc["m1"][:], in_=qexp[:], op=ALU.max, axis=AX.X), 1)
                chain(v, sv)
                sv.inc(v.tensor_scalar(out=oh1f[:], in0=qexp[:], scalar1=rc["m1"][:], scalar2=None, op0=ALU.is_ge), 1)
                chain(v, sv)
                sv.inc(v.tensor_scalar(out=rtmp[:], in0=oh1f[:], scalar1=1e30, scalar2=None, op0=ALU.mult), 1)
                chain(v, sv)
                sv.inc(v.tensor_tensor(out=rtmp[:], in0=qexp[:], in1=rtmp[:], op=ALU.subtract), 1)
                chain(v, sv)
                sv.inc(v.tensor_reduce(out=rc["m2"][:], in_=rtmp[:], op=ALU.max, axis=AX.X), 1)
                chain(v, sv)
                sv.inc(v.tensor_scalar(out=oh2f[:], in0=rtmp[:], scalar1=rc["m2"][:], scalar2=None, op0=ALU.is_ge), 1)
                chain(v, sv)
                sv.inc(v.tensor_tensor(out=rc["sum"][:], in0=rc["m1"][:], in1=rc["m2"][:], op=ALU.add), 1)
                chain(v, sv)
                sv.inc(v.reciprocal(out=rc["rec"][:], in_=rc["sum"][:]), 1)
                chain(v, sv)
                sv.inc(v.tensor_tensor(out=rc["t1"][:], in0=rc["m1"][:], in1=rc["rec"][:], op=ALU.mult), 1)
                chain(v, sv)
                sv.inc(v.tensor_tensor(out=rc["t2"][:], in0=rc["m2"][:], in1=rc["rec"][:], op=ALU.mult), 1)
                chain(v, sv)
                sv.inc(v.tensor_tensor(out=rtmp2[:], in0=oh1f[:], in1=ecbS[:], op=ALU.mult), 1)
                chain(v, sv)
                sv.inc(v.tensor_reduce(out=rc["e1"][:], in_=rtmp2[:], op=ALU.add, axis=AX.X), 1)
                chain(v, sv)
                sv.inc(v.tensor_tensor(out=rtmp2[:], in0=oh2f[:], in1=ecbS[:], op=ALU.mult), 1)
                chain(v, sv)
                sv.inc(v.tensor_reduce(out=rc["e2"][:], in_=rtmp2[:], op=ALU.add, axis=AX.X), 1)
                chain(v, sv)
                sv.inc(v.tensor_copy(out=oh1_all[:, u * E:(u + 1) * E], in_=oh1f[:]), 1)
                chain(v, sv)
                sv.inc(v.tensor_copy(out=oh2_all[:, u * E:(u + 1) * E], in_=oh2f[:]), 1)
                chain(v, sv)
                sv.inc(v.tensor_tensor(out=rtmp[:], in0=oh1f[:], in1=oh2f[:], op=ALU.add), 1)
                chain(v, sv)
                sv.inc(v.tensor_copy(out=ohs_all[:, u * E:(u + 1) * E], in_=rtmp[:]), 1)
                chain(v, sv)
                v.tensor_copy(out=g1c[:, u:u + 1], in_=rc["t1"][:])
                v.tensor_copy(out=e1c[:, u:u + 1], in_=rc["e1"][:])
                v.tensor_copy(out=e2c[:, u:u + 1], in_=rc["e2"][:])
                return v.tensor_copy(out=g2c[:, u:u + 1], in_=rc["t2"][:])
            t_route[u] = sv.tok()
            t_ru_prev = t_route[u]

        @emit("P", deps=[t_route[u] for u in range(NU)] + [war("pb3")], inc=sp)
        def _(p):
            i = None
            for u in range(NU):
                i = p.matmul(pb[3][0:1, 0:E], onescol[:], ohs_all[:, u * E:(u + 1) * E],
                             start=(u == 0), stop=(u == NU - 1))
            return i
        t_cnt = sp.tok()

        @emit("V", deps=[t_cnt, t_cs4], inc=sv)
        def _(v):
            return v.tensor_copy(out=cntrow[:], in_=pb[3][0:1, 0:E])
        t_cntrow = sv.tok()
        setread("pb3", t_cntrow)

        @emit("S", deps=[t_cntrow], inc=scc, k=16)
        def _(s):
            return s.dma_start(out=cc_in.ap(), in_=cntrow[:])
        t_ccin = scc.tok()

        @emit("G", deps=[t_ccin], inc=sg)
        def _(g):
            return g.collective_compute("AllGather", ALU.bypass,
                                        replica_groups=[[0, 1, 2, 3, 4, 5, 6, 7]],
                                        ins=[cc_in.ap().opt()], outs=[cc_out.ap().opt()])
        t_ag = sg.tok()

        @emit("G", deps=[t_ag], inc=sgd, k=16)
        def _(g):
            return g.dma_start(out=agrows[:], in_=cc_out.ap())
        t_agl = sgd.tok()

        @emit("P", deps=[t_agl, t_wload, war("pb3")], inc=sp)
        def _(p):
            return p.matmul(pb[3][0:1, 0:E], cmask[:], agrows[:], start=True, stop=True)
        t_coffmm = sp.tok()

        @emit("V", deps=[t_coffmm], inc=sv)
        def _(v):
            return v.tensor_copy(out=coffs[:], in_=pb[3][0:1, 0:E])
        t_coffs = sv.tok()
        setread("pb3", t_coffs)

        @emit("P", deps=[t_coffs, war("pb3")], inc=sp)
        def _(p):
            return p.matmul(pb[3][:, 0:E], onesc[:], coffs[:], start=True, stop=True)
        t_cb = sp.tok()

        @emit("V", deps=[t_cb], inc=sv)
        def _(v):
            return v.tensor_copy(out=coffb[:], in_=pb[3][:, 0:E])
        t_coffb = sv.tok()
        setread("pb3", t_coffb)

        t_scat = [None] * NU
        t_idx_prev = None
        t_xtok_prev = [None, None]
        for u in range(NU):
            @emit("P", deps=[t_route[u], t_idx_prev, war("pb3")], inc=sp)
            def _(p, u=u):
                p.matmul(pb[3][:, 0:E], trilS[:], ohs_all[:, u * E:(u + 1) * E], start=True, stop=False)
                p.matmul(pb[3][:, 0:E], onesc[:], offs[:], start=False, stop=True)
                return p.matmul(pb[3][0:1, E:2 * E], onescol[:], ohs_all[:, u * E:(u + 1) * E],
                                start=True, stop=True)
            t_cum = sp.tok()

            @emit("V", deps=[t_cum], inc=sv)
            def _(v, u=u):
                v.tensor_copy(out=cum_u[:], in_=pb[3][:, 0:E])
                return v.tensor_tensor(out=offs[:], in0=offs[:], in1=pb[3][0:1, E:2 * E], op=ALU.add)
            t_cumev = sv.tok()
            setread("pb3", t_cumev)

            @emit("V", deps=[t_cumev, t_coffb], inc=sv)
            def _(v, u=u):
                sv.inc(v.tensor_tensor(out=rtmp[:], in0=oh1_all[:, u * E:(u + 1) * E], in1=cum_u[:], op=ALU.mult), 1)
                chain(v, sv)
                sv.inc(v.tensor_reduce(out=rc["lp1"][:], in_=rtmp[:], op=ALU.add, axis=AX.X), 1)
                chain(v, sv)
                sv.inc(v.tensor_tensor(out=rtmp[:], in0=oh2_all[:, u * E:(u + 1) * E], in1=cum_u[:], op=ALU.mult), 1)
                chain(v, sv)
                sv.inc(v.tensor_reduce(out=rc["lp2"][:], in_=rtmp[:], op=ALU.add, axis=AX.X), 1)
                chain(v, sv)
                sv.inc(v.tensor_tensor(out=rtmp[:], in0=oh1_all[:, u * E:(u + 1) * E], in1=coffb[:], op=ALU.mult), 1)
                chain(v, sv)
                sv.inc(v.tensor_reduce(out=rc["gp1"][:], in_=rtmp[:], op=ALU.add, axis=AX.X), 1)
                chain(v, sv)
                sv.inc(v.tensor_tensor(out=rc["gp1"][:], in0=rc["gp1"][:], in1=rc["lp1"][:], op=ALU.add), 1)
                chain(v, sv)
                sv.inc(v.tensor_tensor(out=rtmp[:], in0=oh2_all[:, u * E:(u + 1) * E], in1=coffb[:], op=ALU.mult), 1)
                chain(v, sv)
                sv.inc(v.tensor_reduce(out=rc["gp2"][:], in_=rtmp[:], op=ALU.add, axis=AX.X), 1)
                chain(v, sv)
                sv.inc(v.tensor_tensor(out=rc["gp2"][:], in0=rc["gp2"][:], in1=rc["lp2"][:], op=ALU.add), 1)
                chain(v, sv)
                sv.inc(v.tensor_scalar(out=rc["k1"][:], in0=rc["gp1"][:], scalar1=float(CAP), scalar2=None, op0=ALU.is_lt), 1)
                chain(v, sv)
                sv.inc(v.tensor_scalar(out=rc["k2"][:], in0=rc["gp2"][:], scalar1=float(CAP), scalar2=None, op0=ALU.is_lt), 1)
                chain(v, sv)
                sv.inc(v.tensor_scalar(out=rc["ob1"][:], in0=rc["lp1"][:], scalar1=float(CLOC), scalar2=None, op0=ALU.is_lt), 1)
                chain(v, sv)
                sv.inc(v.tensor_scalar(out=rc["ob2"][:], in0=rc["lp2"][:], scalar1=float(CLOC), scalar2=None, op0=ALU.is_lt), 1)
                chain(v, sv)
                sv.inc(v.tensor_tensor(out=rc["k1"][:], in0=rc["k1"][:], in1=rc["ob1"][:], op=ALU.mult), 1)
                chain(v, sv)
                sv.inc(v.tensor_tensor(out=rc["k2"][:], in0=rc["k2"][:], in1=rc["ob2"][:], op=ALU.mult), 1)
                chain(v, sv)
                sv.inc(v.tensor_tensor(out=g1c[:, u:u + 1], in0=g1c[:, u:u + 1], in1=rc["k1"][:], op=ALU.mult), 1)
                chain(v, sv)
                sv.inc(v.tensor_tensor(out=g2c[:, u:u + 1], in0=g2c[:, u:u + 1], in1=rc["k2"][:], op=ALU.mult), 1)
                chain(v, sv)
                sv.inc(v.tensor_scalar(out=rc["if1"][:], in0=e1c[:, u:u + 1], scalar1=float(CLOC), scalar2=None, op0=ALU.mult), 1)
                chain(v, sv)
                sv.inc(v.tensor_tensor(out=rc["if1"][:], in0=rc["if1"][:], in1=rc["lp1"][:], op=ALU.add), 1)
                chain(v, sv)
                sv.inc(v.tensor_scalar(out=rc["t1"][:], in0=rc["ob1"][:], scalar1=-1e9, scalar2=1e9, op0=ALU.mult, op1=ALU.add), 1)
                chain(v, sv)
                sv.inc(v.tensor_tensor(out=rc["if1"][:], in0=rc["if1"][:], in1=rc["t1"][:], op=ALU.add), 1)
                chain(v, sv)
                sv.inc(v.tensor_scalar(out=rc["if2"][:], in0=e2c[:, u:u + 1], scalar1=float(CLOC), scalar2=None, op0=ALU.mult), 1)
                chain(v, sv)
                sv.inc(v.tensor_tensor(out=rc["if2"][:], in0=rc["if2"][:], in1=rc["lp2"][:], op=ALU.add), 1)
                chain(v, sv)
                sv.inc(v.tensor_scalar(out=rc["t2"][:], in0=rc["ob2"][:], scalar1=-1e9, scalar2=1e9, op0=ALU.mult, op1=ALU.add), 1)
                chain(v, sv)
                sv.inc(v.tensor_tensor(out=rc["if2"][:], in0=rc["if2"][:], in1=rc["t2"][:], op=ALU.add), 1)
                chain(v, sv)
                sv.inc(v.tensor_copy(out=idx1[:, u:u + 1], in_=rc["if1"][:]), 1)
                chain(v, sv)
                return v.tensor_copy(out=idx2[:, u:u + 1], in_=rc["if2"][:])
            t_idx = sv.tok()
            t_idx_prev = t_idx

            if "e1" in dbg:
                @emit("V", deps=[t_idx], inc=sv)
                def _(v, u=u):
                    v.tensor_copy(out=dbg_cols["e1"][:, u:u + 1], in_=e1c[:, u:u + 1])
                    v.tensor_copy(out=dbg_cols["e2"][:, u:u + 1], in_=e2c[:, u:u + 1])
                    v.tensor_copy(out=dbg_cols["lp1"][:, u:u + 1], in_=rc["lp1"][:])
                    v.tensor_copy(out=dbg_cols["lp2"][:, u:u + 1], in_=rc["lp2"][:])
                    v.tensor_copy(out=dbg_cols["g1"][:, u:u + 1], in_=rc["gp1"][:])
                    return v.tensor_copy(out=dbg_cols["g2"][:, u:u + 1], in_=rc["gp2"][:])
                t_idx = sv.tok()
                t_idx_prev = t_idx

            pp = u % 2

            @emit("P", deps=[t_idx] + [t_xn[c][u // 4] for c in range(NCH)] +
                           [t_xtok_prev[pp], war("pbf")], inc=sp)
            def _(p, u=u):
                i = None
                for c in range(NCH):
                    i = p.transpose(pbf[:, c * 128:(c + 1) * 128],
                                    xn[:, c * T + u * 128: c * T + (u + 1) * 128], idb[:])
                return i
            t_xtr2 = sp.tok()

            @emit("V", deps=[t_xtr2], inc=sv)
            def _(v, pp=pp):
                return v.tensor_copy(out=xtok[pp][:], in_=pbf[:])
            t_xtokc = sv.tok()
            setread("pbf", t_xtokc)

            @emit("G", deps=[t_xtokc, t_scat[u - 1] if u > 0 else None])
            def _(g, u=u, pp=pp):
                with g.register("bcs%d" % u) as r:
                    g.reg_mov(r, E * CLOC - 1)
                    sgd.inc(g.indirect_dma_start(
                        out=buckets.ap(), out_offset=bass.IndirectOffsetOnAxis(ap=idx1[:, u:u + 1], axis=0),
                        in_=xtok[pp][:], in_offset=None,
                        bounds_check=r, oob_is_err=False), 16)
                    chain(g, sgd)
                    sgd.inc(g.indirect_dma_start(
                        out=buckets.ap(), out_offset=bass.IndirectOffsetOnAxis(ap=idx2[:, u:u + 1], axis=0),
                        in_=xtok[pp][:], in_offset=None,
                        bounds_check=r, oob_is_err=False), 16)
                return None
            t_scat[u] = sgd.tok()
            t_xtok_prev[pp] = t_scat[u]

        if "e1" in dbg:
            for nm in ("e1", "e2", "g1", "g2", "lp1", "lp2"):
                @emit("S", deps=[t_idx_prev], inc=st, k=16)
                def _(s, nm=nm):
                    return s.dma_start(out=dbg[nm], in_=dbg_cols[nm][:])
            @emit("G", deps=[t_idx_prev])
            def _(g):
                sgd.inc(g.dma_start(out=dbg["id1"], in_=idx1[:]), 16)
                sgd.inc(g.dma_start(out=dbg["id2"], in_=idx2[:]), 16)
                return None

    esA.close()

    toks = [s_.tok() for s_ in ALLSEMS]
    for engname in ENG:
        @emit(engname, deps=toks)
        def _(e):
            return None

    if upto >= 7:
        esB = ExitStack()

        def sbB(name, shape, dtp):
            return esB.enter_context(nc.sbuf_tensor(name, shape, dtp))

        ein = sbB("ein", [128, NCH * CLOC], BF16)
        hbuf = sbB("hbuf", [128, NH0 * CLOC], BF16)
        eoutF = sbB("eoutF", [128, NCH * CLOC], BF16)
        gtok = [sbB("gtok0", [128, D], BF16), sbB("gtok1", [128, D], BF16)]
        w1c = [sbB("w1c0", [128, 1024], F32), sbB("w1c1", [128, 1024], F32)]
        w1cb = [sbB("w1cb0", [128, 1024], BF16), sbB("w1cb1", [128, 1024], BF16)]
        w2c = [sbB("w2c0", [128, 2048], F32), sbB("w2c1", [128, 2048], F32)]
        w2cb = [sbB("w2cb0", [128, 2048], BF16), sbB("w2cb1", [128, 2048], BF16)]
        b1r = sbB("b1r", [1, H0], BF16)
        b2r = sbB("b2r", [1, D], BF16)
        b1c = sbB("b1c", [128, NH0], F32)
        b2c = sbB("b2c", [128, NCH], F32)
        eout_t = [sbB("eout_t0", [128, D], BF16), sbB("eout_t1", [128, D], BF16)]
        gat1 = sbB("gat1", [128, D], BF16)
        gat2 = sbB("gat2", [128, D], BF16)
        ycomb = sbB("ycomb", [128, D], F32)
        yc2 = sbB("yc2", [128, D], F32)

        NTT = [(0, 512), (512, 128)]

        @emit("G", inc=sg)
        def _(g):
            g.memset(gat1[:], 0.0)
            return g.memset(gat2[:], 0.0)
        t_gz = sg.tok()

        t_scat_all = t_scat[NU - 1]
        t_wprev = {}
        t_bias_reader = None
        t_ein_reader = None
        t_h_reader = None
        t_eoutF_reader = None
        t_store_prev = None
        t_estore = [None] * E
        for e in range(E):
            t_eintr = []
            for j in range(5):
                pp = j % 2

                @emit("S", deps=[t_scat_all, t_wprev.get(("gt", pp))], inc=sgt[pp], k=16)
                def _(s, e=e, j=j, pp=pp):
                    return s.dma_start(out=gtok[pp][:], in_=buckets.ap()[e * CLOC + j * 128: e * CLOC + (j + 1) * 128, :])
                t_gl = sgt[pp].tok()

                @emit("P", deps=[t_gl, war("pbf"), t_ein_reader], inc=sp)
                def _(p, pp=pp):
                    i = None
                    for c in range(NCH):
                        i = p.transpose(pbf[:, c * 128:(c + 1) * 128], gtok[pp][:, c * 128:(c + 1) * 128], idb[:])
                    return i
                t_tr = sp.tok()
                t_wprev[("gt", pp)] = t_tr

                @emit("V", deps=[t_tr, t_ein_reader], inc=sv)
                def _(v, j=j):
                    ev = ein[:].rearrange("p (c t) -> p c t", c=NCH)[:, :, j * 128:(j + 1) * 128]
                    return v.tensor_copy(out=ev[:, 0:NCH, :], in_=pbf[:].rearrange("p (c t) -> p c t", c=NCH))
                tk2 = sv.tok()
                setread("pbf", tk2)
                t_eintr.append(tk2)

            @emit("G", deps=[t_bias_reader])
            def _(g, e=e):
                sgd.inc(g.dma_start(out=b1r[:], in_=b1[e:e + 1, :]), 16)
                sgd.inc(g.dma_start(out=b2r[:], in_=b2[e:e + 1, :]), 16)
                return None
            t_bc0 = sgd.tok()

            @emit("P", deps=[t_bc0, war("pb4")], inc=sp)
            def _(p, e=e):
                for hc in range(NH0):
                    p.matmul(pb[4][:, hc:hc + 1], b1r[0:1, hc * 128:(hc + 1) * 128],
                             onesrb[0:1, 0:1], start=True, stop=True)
                i = None
                for dc in range(NCH):
                    i = p.matmul(pb[4][:, NH0 + dc:NH0 + dc + 1], b2r[0:1, dc * 128:(dc + 1) * 128],
                                 onesrb[0:1, 0:1], start=True, stop=True)
                return i
            t_bcm = sp.tok()

            @emit("V", deps=[t_bcm], inc=sv)
            def _(v):
                sv.inc(v.tensor_copy(out=b1c[:], in_=pb[4][:, 0:NH0]), 1)
                chain(v, sv)
                return v.tensor_copy(out=b2c[:], in_=pb[4][:, NH0:NH0 + NCH])
            t_bc = sv.tok()
            setread("pb4", t_bc)

            t_hev = [[None, None] for _ in range(NH0)]
            t_hmm_last = None
            for hc in range(NH0):
                pp = hc % 2

                @emit("S", deps=[t_wprev.get(("w1", pp))], inc=sw1[pp], k=16)
                def _(s, e=e, hc=hc, pp=pp):
                    src = w1[e:e + 1, :, hc * 128:(hc + 1) * 128].rearrange("o (c p) h -> p o c h", p=128)
                    return s.dma_start(out=w1c[pp][:].rearrange("p (o c h) -> p o c h", o=1, c=NCH), in_=src)
                t_wl = sw1[pp].tok()

                @emit("V", deps=[t_wl, t_wprev.get(("w1b", pp))], inc=sv)
                def _(v, pp=pp):
                    return v.tensor_copy(out=w1cb[pp][:], in_=w1c[pp][:])
                t_wb = sv.tok()
                t_wprev[("w1", pp)] = t_wb

                for ti, (toff, tlen) in enumerate(NTT):
                    bi = (hc % 2) * 2 + ti
                    bk_ = "pb%d" % bi

                    @emit("P", deps=t_eintr + [t_wb, t_bc, war(bk_)], inc=sp)
                    def _(p, hc=hc, pp=pp, toff=toff, tlen=tlen, bi=bi):
                        bank = pb[bi]
                        i = None
                        for c in range(NCH):
                            i = p.matmul(bank[:, 0:tlen], w1cb[pp][:, c * 128:(c + 1) * 128],
                                         ein[:, c * CLOC + toff: c * CLOC + toff + tlen],
                                         start=(c == 0), stop=(c == NCH - 1))
                        return i
                    t_hmm = sp.tok()
                    t_wprev[("w1b", pp)] = t_hmm
                    t_hmm_last = t_hmm

                    @emit("V", deps=[t_hmm, t_h_reader], inc=sv)
                    def _(v, hc=hc, toff=toff, tlen=tlen, bi=bi):
                        return v.tensor_scalar(out=hbuf[:, hc * CLOC + toff: hc * CLOC + toff + tlen],
                                               in0=pb[bi][:, 0:tlen], scalar1=b1c[:, hc:hc + 1],
                                               scalar2=0.0, op0=ALU.add, op1=ALU.max)
                    t_hev[hc][ti] = sv.tok()
                    setread(bk_, t_hev[hc][ti])
            t_ein_reader = t_hmm_last

            t_omm_last = None
            t_oev = [[None, None] for _ in range(NCH)]
            for dc in range(NCH):
                t_whalf = []
                for half in range(2):
                    pp = half

                    @emit("S", deps=[t_wprev.get(("w2", pp))], inc=sw2[pp], k=16)
                    def _(s, e=e, dc=dc, half=half, pp=pp):
                        src = w2[e:e + 1, half * 2048:(half + 1) * 2048, dc * 128:(dc + 1) * 128] \
                            .rearrange("o (c p) d -> p o c d", p=128)
                        return s.dma_start(out=w2c[pp][:].rearrange("p (o c d) -> p o c d", o=1, c=16), in_=src)
                    t_w2l = sw2[pp].tok()

                    @emit("V", deps=[t_w2l, t_wprev.get(("w2b", pp))], inc=sv)
                    def _(v, pp=pp):
                        return v.tensor_copy(out=w2cb[pp][:], in_=w2c[pp][:])
                    t_w2b = sv.tok()
                    t_wprev[("w2", pp)] = t_w2b
                    t_whalf.append(t_w2b)

                for ti, (toff, tlen) in enumerate(NTT):
                    bk_ = "pb%d" % (2 + ti % 2)
                    hdeps = [t_hev[hc][ti] for hc in range(NH0)]

                    @emit("P", deps=hdeps + t_whalf + [t_bc, war(bk_)], inc=sp)
                    def _(p, dc=dc, toff=toff, tlen=tlen, ti=ti):
                        bank = pb[2 + ti % 2]
                        i = None
                        for hc in range(NH0):
                            i = p.matmul(bank[:, 0:tlen], w2cb[hc // 16][:, (hc % 16) * 128:(hc % 16 + 1) * 128],
                                         hbuf[:, hc * CLOC + toff: hc * CLOC + toff + tlen],
                                         start=(hc == 0), stop=(hc == NH0 - 1))
                        return i
                    t_omm = sp.tok()
                    t_wprev[("w2b", 0)] = t_omm
                    t_wprev[("w2b", 1)] = t_omm
                    t_omm_last = t_omm

                    @emit("V", deps=[t_omm, t_eoutF_reader], inc=sv)
                    def _(v, dc=dc, toff=toff, tlen=tlen, ti=ti):
                        return v.tensor_scalar(out=eoutF[:, dc * CLOC + toff: dc * CLOC + toff + tlen],
                                               in0=pb[2 + ti % 2][:, 0:tlen],
                                               scalar1=b2c[:, dc:dc + 1], scalar2=None, op0=ALU.add)
                    t_oev[dc][ti] = sv.tok()
                    setread(bk_, t_oev[dc][ti])
            t_h_reader = t_omm_last
            t_bias_reader = t_omm_last

            t_str_tr_last = None
            for j in range(5):
                pp = j % 2
                ti = 0 if j < 4 else 1
                deps_ev = [t_oev[dc][0] for dc in range(NCH)] + [t_oev[dc][1] for dc in range(NCH)]

                @emit("P", deps=deps_ev + [war("pbf"), t_store_prev if pp == (t_store_prev is not None and pp) else t_store_prev], inc=sp)
                def _(p, j=j):
                    i = None
                    for c in range(NCH):
                        i = p.transpose(pbf[:, c * 128:(c + 1) * 128],
                                        eoutF[:, c * CLOC + j * 128: c * CLOC + (j + 1) * 128], idb[:])
                    return i
                t_str_tr = sp.tok()
                t_str_tr_last = t_str_tr

                @emit("V", deps=[t_str_tr, t_wprev.get(("eot", pp))], inc=sv)
                def _(v, pp=pp):
                    return v.tensor_copy(out=eout_t[pp][:], in_=pbf[:])
                t_eot = sv.tok()
                setread("pbf", t_eot)

                @emit("S", deps=[t_eot], inc=setm[pp], k=16)
                def _(s, e=e, j=j, pp=pp):
                    return s.dma_start(out=exp_out_d.ap()[e * CLOC + j * 128: e * CLOC + (j + 1) * 128, :],
                                       in_=eout_t[pp][:])
                t_wprev[("eot", pp)] = setm[pp].tok()
            t_eoutF_reader = t_str_tr_last
            t_estore[e] = t_wprev[("eot", (5 - 1) % 2)]
            t_store_prev = t_estore[e]

    if upto >= 8:
        t_comb_prev = None
        for u in range(NU):
            @emit("G", deps=[t_estore[ee] for ee in range(E)] + [t_gz, t_comb_prev])
            def _(g, u=u):
                with g.register("bcg%d" % u) as r:
                    g.reg_mov(r, E * CLOC - 1)
                    sg.inc(g.indirect_dma_start(
                        out=gat1[:], out_offset=None, in_=exp_out_d.ap(),
                        in_offset=bass.IndirectOffsetOnAxis(ap=idx1[:, u:u + 1], axis=0),
                        bounds_check=r, oob_is_err=False), 16)
                    sg.inc(g.indirect_dma_start(
                        out=gat2[:], out_offset=None, in_=exp_out_d.ap(),
                        in_offset=bass.IndirectOffsetOnAxis(ap=idx2[:, u:u + 1], axis=0),
                        bounds_check=r, oob_is_err=False), 16)
                return None
            t_gath = sg.tok()

            @emit("V", deps=[t_gath], inc=sv)
            def _(v, u=u):
                v.tensor_scalar(out=ycomb[:], in0=gat1[:], scalar1=g1c[:, u:u + 1], scalar2=None, op0=ALU.mult)
                sv.inc(v.tensor_scalar(out=yc2[:], in0=gat2[:], scalar1=g2c[:, u:u + 1], scalar2=None, op0=ALU.mult), 1)
                chain(v, sv)
                return v.tensor_tensor(out=ycomb[:], in0=ycomb[:], in1=yc2[:], op=ALU.add)
            t_yc = sv.tok()

            @emit("P", deps=[t_yc, war("pb4"), war("pb5")], inc=sp)
            def _(p, u=u):
                for c in range(4):
                    p.transpose(pb[4][:, c * 128:(c + 1) * 128], ycomb[:, c * 128:(c + 1) * 128], idf[:])
                i = None
                for c in range(4):
                    i = p.transpose(pb[5][:, c * 128:(c + 1) * 128], ycomb[:, (4 + c) * 128:(5 + c) * 128], idf[:])
                return i
            t_ytr = sp.tok()

            @emit("V", deps=[t_ytr, t_op[NCH - 1][u // 4]] + [t_op[c][u // 4] for c in range(NCH)], inc=sv)
            def _(v, u=u):
                rv = resid[:].rearrange("p (c t) -> p c t", c=NCH)[:, :, u * 128:(u + 1) * 128]
                v.tensor_tensor(out=rv[:, 0:4, :], in0=rv[:, 0:4, :],
                                in1=pb[4][:].rearrange("p (c t) -> p c t", c=4), op=ALU.add)
                return v.tensor_tensor(out=rv[:, 4:8, :], in0=rv[:, 4:8, :],
                                       in1=pb[5][:].rearrange("p (c t) -> p c t", c=4), op=ALU.add)
            t_comb = sv.tok()
            setread("pb4", t_comb)
            setread("pb5", t_comb)
            t_comb_prev = t_comb

        esB.close()

    toks = [s_.tok() for s_ in ALLSEMS]

    @emit("S", deps=toks)
    def _(s):
        for c in range(NCH):
            st.inc(s.dma_start(out=x1_out[c * 128:(c + 1) * 128, :], in_=resid[:, c * T:(c + 1) * T]), 16)
        return None

    @emit("S", deps=[st.tok()])
    def _(s):
        return None

    es.close()
    return nc


def _erows_const():
    e = np.zeros((E, E * 128), np.float32)
    for i in range(E):
        e[i, i * 128:(i + 1) * 128] = 1.0
    return e


def kernel(**inputs):
    import os as _os
    if _os.environ.get("KMODE", "fused") == "fused":
        return kernel_fused(**inputs)
    consts = host_consts()
    erows = _erows_const()
    z = inputs

    nc0 = build_l0()
    in_maps0 = []
    for b in range(8):
        m = {
            "x": np.ascontiguousarray(z["L0_attn_x"][b]),
            "Wq": z["L0_attn_Wq"], "bq": z["L0_attn_bq"],
            "Wk": z["L0_attn_Wk"], "bk": z["L0_attn_bk"],
            "Wv": z["L0_attn_Wv"], "bv": z["L0_attn_bv"],
            "dec": z["L0_attn_decay"], "Wo": z["L0_attn_Wo"], "bo": z["L0_attn_bo"],
            "Wr": z["L0_ffn_w_router"], "w1": z["L0_ffn_w1"], "b1": z["L0_ffn_b1"],
            "w2": z["L0_ffn_w2"], "b2": z["L0_ffn_b2"],
            "cpref": cpref_for_core(b),
        }
        m.update(consts)
        in_maps0.append(m)
    res0 = run_bass_kernel_spmd(nc0, in_maps0, core_ids=list(range(8)))

    nc1 = build_l1()
    in_maps1 = []
    for b in range(8):
        m = {
            "x1h": np.ascontiguousarray(res0.results[b]["x1"]),
            "pos": z["L1_attn_pos_emb"], "gm": np.ascontiguousarray(z["L1_attn_global_memory"][0]),
            "convw": z["L1_attn_conv_w"], "convb": z["L1_attn_conv_b"],
            "gatew": z["L1_attn_gate_w"], "gateb": z["L1_attn_gate_b"],
            "outw": z["L1_attn_out_w"], "outb": z["L1_attn_out_b"],
            "fgw": z["L1_ffn_gate_w"], "fgb": z["L1_ffn_gate_b"],
            "fw1": z["L1_ffn_w1"], "fb1": z["L1_ffn_b1"],
            "fw2": z["L1_ffn_w2"], "fb2": z["L1_ffn_b2"],
            "fw3": z["L1_ffn_w3"], "fb3": z["L1_ffn_b3"],
            "identf": consts["identf"], "identb": consts["identb"],
            "erows": erows,
        }
        in_maps1.append(m)
    res1 = run_bass_kernel_spmd(nc1, in_maps1, core_ids=list(range(8)))
    out = np.stack([np.asarray(res1.results[b]["x2"]) for b in range(8)], axis=0)
    return out.astype(np.float32)


def war_a_sq(pr, hp):
    return pr.get(("xpsq",))


def war_ab(pr, sub):
    return pr.get(("abr", sub))


def war_av(pr, sub):
    return pr.get("pb2")


def war_bc(pr, sub):
    return pr.get("pb4")


def build_l1(dbgkeys=(), upto=99):
    import os
    nc = bass.Bass("TRN2", target_bir_lowering=False)
    dt = nc.dram_tensor
    NH, HD, WIN, G = 16, 64, 512, 64
    TH = T // 2
    x1_in = dt("x1h", [D, T], F32, kind="ExternalInput").ap()
    pos_in = dt("pos", [8192, D], F32, kind="ExternalInput").ap()
    gm_in = dt("gm", [G, D], F32, kind="ExternalInput").ap()
    cw_in = dt("convw", [D, D, 4], F32, kind="ExternalInput").ap()
    cb_in = dt("convb", [D], F32, kind="ExternalInput").ap()
    gw_in = dt("gatew", [2 * D, D], F32, kind="ExternalInput").ap()
    gb_in = dt("gateb", [D], F32, kind="ExternalInput").ap()
    ow_in = dt("outw", [D, D], F32, kind="ExternalInput").ap()
    ob_in = dt("outb", [D], F32, kind="ExternalInput").ap()
    fgw = dt("fgw", [E, D], F32, kind="ExternalInput").ap()
    fgb = dt("fgb", [E], F32, kind="ExternalInput").ap()
    fw1 = dt("fw1", [E, D, D], F32, kind="ExternalInput").ap()
    fb1 = dt("fb1", [E, D], F32, kind="ExternalInput").ap()
    fw2 = dt("fw2", [E, D, D], F32, kind="ExternalInput").ap()
    fb2 = dt("fb2", [E, D], F32, kind="ExternalInput").ap()
    fw3 = dt("fw3", [E, D, D], F32, kind="ExternalInput").ap()
    fb3 = dt("fb3", [E, D], F32, kind="ExternalInput").ap()
    identf = dt("identf", [128, 128], F32, kind="ExternalInput").ap()
    identb = dt("identb", [128, 128], BF16, kind="ExternalInput").ap()
    erows_in = dt("erows", [E, E * 128], F32, kind="ExternalInput").ap()
    x2_out = dt("x2", [T, D], F32, kind="ExternalOutput").ap()
    dbg = {}
    if "l1" in dbgkeys:
        for nm in ("xn", "xp", "local", "glob"):
            dbg[nm] = dt("dbg_" + nm, [D, T], F32, kind="ExternalOutput").ap()

    es = ExitStack()

    def sb(name, shape, dtp):
        return es.enter_context(nc.sbuf_tensor(name, shape, dtp))

    def pst(name, shape, dtp=F32):
        return es.enter_context(nc.psum_tensor(name, shape, dtp))

    resid = sb("resid", [128, NCH * T], F32)
    xn1 = sb("xn1", [128, NCH * T], BF16)
    idf = sb("idf", [128, 128], F32)
    idb = sb("idb", [128, 128], BF16)
    onesc = sb("onesc", [1, 128], F32)
    onescb = sb("onescb", [1, 128], BF16)
    onesrb = sb("onesrb", [1, 512], BF16)
    onescol = sb("onescol", [128, 1], BF16)
    ones2d = sb("ones2d", [128, 128], F32)
    erowS = sb("erowS", [E, E * 128], BF16)
    srow_m = sb("srow_m", [1, 512], F32)
    srow_r = sb("srow_r", [1, 512], F32)
    vrow = sb("vrow", [1, 512], F32)
    vrow2 = sb("vrow2", [1, 512], F32)
    gt_s = sb("gt_s", [128, NCH * 128], BF16)
    gtT = sb("gtT", [128, NH * HD], BF16)
    cwTb = sb("cwTb", [E, T], BF16)
    b3S = sb("b3S", [E, D], BF16)
    gbrow = sb("gbrow", [1, D], BF16)
    obrow = sb("obrow", [1, D], BF16)
    gbcol = sb("gbcol", [128, NCH], F32)
    obcol = sb("obcol", [128, NCH], F32)
    cbrow = sb("cbrow", [1, D], BF16)
    fgbrow = sb("fgbrow", [1, E], BF16)
    _xs0 = sb("xstage0", [128, D], F32)
    xstage = [_xs0, _xs0]
    rc = {nm: sb("rc_" + nm, [128, 1], F32) for nm in
          ("nmax", "sum", "rec", "m1", "m2", "t1", "t2")}
    gwT = sb("gwT", [128, NCH * E], BF16)
    esG = ExitStack()

    def sbG(name, shape, dtp):
        return esG.enter_context(nc.sbuf_tensor(name, shape, dtp))

    xbp = [sbG("xb0", [128, 512], BF16), sbG("xb1", [128, 512], BF16)]
    xsqp = [sbG("xsq0", [128, 512], BF16), sbG("xsq1", [128, 512], BF16)]
    xtmp = sbG("xtmp", [128, 512], F32)
    lgs = sbG("lgs", [E, 512], F32)
    lgT = sbG("lgT", [128, NU * E], F32)
    qexp = sbG("qexp", [128, E], F32)
    oh1f = sbG("oh1f", [128, E], F32)
    oh2f = sbG("oh2f", [128, E], F32)
    cwmat = sbG("cwmat", [128, E], F32)
    rtmp = sbG("rtmp", [128, E], F32)

    pb = [pst(f"pb{i}", [128, 512], F32) for i in range(6)]
    pbf = pst("pbf", [128, 1024], BF16)
    pbf2 = pst("pbf2", [128, 1024], BF16)

    sd = Sem(es, nc, "sd")
    sx = [Sem(es, nc, "sxa"), Sem(es, nc, "sxb")]
    swl = [Sem(es, nc, "swla"), Sem(es, nc, "swlb")]
    sgd = Sem(es, nc, "sgd")
    sgr = Sem(es, nc, "sgr")
    scv = Sem(es, nc, "scv")
    sdbg = Sem(es, nc, "sdbg")
    sv = Sem(es, nc, "sv")
    sp = Sem(es, nc, "sp")
    sa = Sem(es, nc, "sa")
    sg = Sem(es, nc, "sg")
    st = Sem(es, nc, "st")
    ALLSEMS = [sd, sx[0], sx[1], swl[0], swl[1], sgd, sgr, scv, sdbg, sv, sp, sa, sg, st]

    block = es.enter_context(nc.Block())
    ENG = {"S": block.sync, "V": block.vector, "A": block.scalar,
           "P": block.tensor, "G": block.gpsimd}

    def emit(eng, deps=(), inc=None, k=1):
        reg = ENG[eng]
        deps = [d for d in deps if d is not None]

        def deco(body):
            def _f(e):
                seen = {}
                for sem, val in deps:
                    if seen.get(id(sem), -1) < val:
                        seen[id(sem)] = val
                        e.wait_ge(sem.h, val)
                last = body(e)
                if inc is not None and last is not None:
                    inc.inc(last, k)
            reg(_f)
        return deco

    def chain(e, sem):
        e.wait_ge(sem.h, sem.n)

    pr = {}

    def war(key):
        return pr.get(key)

    def setread(key, tok):
        pr[key] = tok

    @emit("S")
    def _(s):
        sd.inc(s.dma_start(out=idf[:], in_=identf), 16)
        sd.inc(s.dma_start(out=idb[:], in_=identb), 16)
        sd.inc(s.dma_start(out=xstage[0][0:E, 0:E * 128], in_=erows_in), 16)
        for c in range(NCH):
            sd.inc(s.dma_start(out=resid[:, c * T:(c + 1) * T], in_=x1_in[c * 128:(c + 1) * 128, :]), 16)
        return None
    t_xload = sd.tok()

    @emit("G")
    def _(g):
        g.memset(onesc[:], 1.0)
        g.memset(onescb[:], 1.0)
        g.memset(onesrb[:], 1.0)
        g.memset(onescol[:], 1.0)
        sg.inc(g.memset(ones2d[:], 1.0), 1)
        sgr.inc(g.dma_start(out=gbrow[:], in_=gb_in[None, :]), 16)
        sgr.inc(g.dma_start(out=obrow[:], in_=ob_in[None, :]), 16)
        sgr.inc(g.dma_start(out=fgbrow[:], in_=fgb[None, :]), 16)
        sgr.inc(g.dma_start(out=b3S[:], in_=fb3), 16)
        sgr.inc(g.dma_start(out=cbrow[:], in_=cb_in[None, :]), 16)
        return None
    t_ones = sg.tok()
    t_rows = sgr.tok()

    @emit("V", deps=[t_xload], inc=sv)
    def _(v):
        return v.tensor_copy(out=erowS[:], in_=xstage[0][0:E, 0:E * 128])
    t_erows = sv.tok()

    @emit("P", deps=[t_rows, t_ones, war("pb4")], inc=sp)
    def _(p):
        for dc in range(NCH):
            p.matmul(pb[4][:, dc:dc + 1], gbrow[0:1, dc * 128:(dc + 1) * 128],
                     onesrb[0:1, 0:1], start=True, stop=True)
        i = None
        for dc in range(NCH):
            i = p.matmul(pb[4][:, NCH + dc:NCH + dc + 1], obrow[0:1, dc * 128:(dc + 1) * 128],
                         onesrb[0:1, 0:1], start=True, stop=True)
        return i
    t_gbm = sp.tok()

    @emit("V", deps=[t_gbm], inc=sv)
    def _(v):
        sv.inc(v.tensor_copy(out=gbcol[:], in_=pb[4][:, 0:NCH]), 1)
        chain(v, sv)
        return v.tensor_copy(out=obcol[:], in_=pb[4][:, NCH:2 * NCH])
    t_gbc = sv.tok()
    setread("pb4", t_gbc)

    t_xn = [[None] * NV for _ in range(NCH)]
    t_norm_prev = None
    for v_i in range(NV):
        t_mmprev = None
        t_cast = [None] * NCH
        for c in range(NCH):
            pp = c % 2

            @emit("V", deps=[t_xload, t_mmprev], inc=sv)
            def _(v, c=c, v_i=v_i, pp=pp):
                return v.tensor_copy(out=xbp[pp][:], in_=resid[:, c * T + v_i * 512: c * T + v_i * 512 + 512])
            t_cast[c] = sv.tok()

            @emit("V", deps=[t_cast[c]], inc=sv)
            def _(v, pp=pp):
                return v.tensor_tensor(out=xsqp[pp][:], in0=xbp[pp][:], in1=xbp[pp][:], op=ALU.mult)
            t_sq = sv.tok()

            @emit("P", deps=[t_cast[c], t_sq, t_ones, war("pb4"), war("pb5")], inc=sp)
            def _(p, c=c, pp=pp):
                p.matmul(pb[4][0:1, 0:512], onescol[:], xbp[pp][:], start=(c == 0), stop=(c == NCH - 1))
                return p.matmul(pb[5][0:1, 0:512], onescol[:], xsqp[pp][:], start=(c == 0), stop=(c == NCH - 1))
            t_mmprev = sp.tok()
        t_stats = t_mmprev

        @emit("V", deps=[t_stats], inc=sv)
        def _(v, v_i=v_i):
            v.tensor_scalar(out=srow_m[:], in0=pb[4][0:1, 0:512], scalar1=1.0 / D, scalar2=None, op0=ALU.mult)
            sv.inc(v.tensor_scalar(out=vrow2[:], in0=pb[5][0:1, 0:512], scalar1=1.0 / D, scalar2=None, op0=ALU.mult), 1)
            chain(v, sv)
            sv.inc(v.tensor_tensor(out=vrow[:], in0=srow_m[:], in1=srow_m[:], op=ALU.mult), 1)
            chain(v, sv)
            return v.tensor_tensor(out=vrow[:], in0=vrow2[:], in1=vrow[:], op=ALU.subtract)
        t_var = sv.tok()
        setread("pb4", t_var)
        setread("pb5", t_var)

        @emit("A", deps=[t_var], inc=sa)
        def _(a):
            return a.activation(out=vrow[:], in_=vrow[:], func=AF.Sqrt, bias=np.float32(1e-5))
        t_sqrt = sa.tok()

        @emit("V", deps=[t_sqrt], inc=sv)
        def _(v):
            return v.reciprocal(out=srow_r[:], in_=vrow[:])
        t_rstd = sv.tok()

        @emit("P", deps=[t_rstd, t_ones, war("pb4"), war("pb5")], inc=sp)
        def _(p):
            p.matmul(pb[4][:, 0:512], onesc[:], srow_m[:], start=True, stop=True)
            return p.matmul(pb[5][:, 0:512], onesc[:], srow_r[:], start=True, stop=True)
        t_bcast = sp.tok()

        for c in range(NCH):
            @emit("V", deps=[t_bcast, t_norm_prev], inc=sv)
            def _(v, c=c, v_i=v_i):
                sl = slice(c * T + v_i * 512, c * T + v_i * 512 + 512)
                sv.inc(v.tensor_tensor(out=xtmp[:], in0=resid[:, sl], in1=pb[4][:, 0:512], op=ALU.subtract), 1)
                chain(v, sv)
                return v.tensor_tensor(out=xn1[:, sl], in0=xtmp[:], in1=pb[5][:, 0:512], op=ALU.mult)
            t_xn[c][v_i] = sv.tok()
            t_norm_prev = t_xn[c][v_i]
        setread("pb4", t_norm_prev)
        setread("pb5", t_norm_prev)

    if "xn" in dbg:
        for c in range(NCH):
            @emit("G", deps=[t_xn[c][v] for v in range(NV)], inc=sdbg, k=16)
            def _(g, c=c):
                return g.dma_start(out=dbg["xn"][c * 128:(c + 1) * 128, :], in_=xn1[:, c * T:(c + 1) * T])

    @emit("S", deps=[t_erows], inc=sx[0], k=16)
    def _(s):
        return s.dma_start(out=xstage[0][0:E, 0:D], in_=fgw)
    t_fgw = sx[0].tok()

    @emit("P", deps=[t_fgw, war("pb5")], inc=sp)
    def _(p):
        i = None
        for c in range(NCH):
            i = p.transpose(pb[5][:, c * E:(c + 1) * E], xstage[0][0:E, c * 128:(c + 1) * 128], idf[0:E, 0:E])
        return i
    t_gwT = sp.tok()

    @emit("V", deps=[t_gwT], inc=sv)
    def _(v):
        return v.tensor_copy(out=gwT[:], in_=pb[5][:, 0:NCH * E])
    t_gwTc = sv.tok()
    setread("pb5", t_gwTc)

    t_cwT = [None] * NV
    t_ru_prev = None
    for v_i in range(NV):
        @emit("P", deps=[t_xn[c][v_i] for c in range(NCH)] + [t_gwTc, t_rows, war("pb3")], inc=sp)
        def _(p, v_i=v_i):
            for c in range(NCH):
                p.matmul(pb[3][0:E, :], gwT[:, c * E:(c + 1) * E],
                         xn1[:, c * T + v_i * 512: c * T + v_i * 512 + 512],
                         start=(c == 0), stop=False)
            return p.matmul(pb[3][0:E, :], fgbrow[:], onesrb[:], start=False, stop=True)
        t_lg = sp.tok()

        @emit("V", deps=[t_lg], inc=sv)
        def _(v):
            return v.tensor_copy(out=lgs[:], in_=pb[3][0:E, :])
        t_lgs = sv.tok()
        setread("pb3", t_lgs)

        @emit("P", deps=[t_lgs, war("pb5")], inc=sp)
        def _(p):
            i = None
            for j in range(4):
                i = p.transpose(pb[5][:, j * E:(j + 1) * E], lgs[:, j * 128:(j + 1) * 128], idf[0:E, 0:E])
            return i
        t_lgtr = sp.tok()

        @emit("V", deps=[t_lgtr], inc=sv)
        def _(v, v_i=v_i):
            return v.tensor_copy(out=lgT[:, v_i * 4 * E:(v_i + 1) * 4 * E], in_=pb[5][:, 0:4 * E])
        tk = sv.tok()
        setread("pb5", tk)

        for j in range(4):
            u = 4 * v_i + j

            @emit("V", deps=[tk, t_ru_prev], inc=sv)
            def _(v, u=u):
                return v.tensor_reduce(out=rc["nmax"][:], in_=lgT[:, u * E:(u + 1) * E], op=ALU.max, axis=AX.X, negate=True)
            t1 = sv.tok()

            @emit("A", deps=[t1], inc=sa)
            def _(a, u=u):
                return a.activation(out=qexp[:], in_=lgT[:, u * E:(u + 1) * E], func=AF.Exp, bias=rc["nmax"][:],
                                    accum_out=rc["sum"][:])
            t2 = sa.tok()

            @emit("V", deps=[t2], inc=sv)
            def _(v, u=u):
                def q(i):
                    sv.inc(i, 1)
                    chain(v, sv)
                    return i
                q(v.reciprocal(out=rc["rec"][:], in_=rc["sum"][:]))
                q(v.tensor_scalar(out=qexp[:], in0=qexp[:], scalar1=rc["rec"][:], scalar2=None, op0=ALU.mult))
                q(v.tensor_reduce(out=rc["m1"][:], in_=qexp[:], op=ALU.max, axis=AX.X))
                q(v.tensor_scalar(out=oh1f[:], in0=qexp[:], scalar1=rc["m1"][:], scalar2=None, op0=ALU.is_ge))
                q(v.tensor_scalar(out=rtmp[:], in0=oh1f[:], scalar1=1e30, scalar2=None, op0=ALU.mult))
                q(v.tensor_tensor(out=rtmp[:], in0=qexp[:], in1=rtmp[:], op=ALU.subtract))
                q(v.tensor_reduce(out=rc["m2"][:], in_=rtmp[:], op=ALU.max, axis=AX.X))
                q(v.tensor_scalar(out=oh2f[:], in0=rtmp[:], scalar1=rc["m2"][:], scalar2=None, op0=ALU.is_ge))
                q(v.tensor_copy(out=rc["t1"][:], in_=qexp[:, 0:1]))
                q(v.tensor_copy(out=rc["t2"][:], in_=qexp[:, 1:2]))
                q(v.tensor_scalar(out=oh1f[:], in0=oh1f[:], scalar1=rc["t1"][:], scalar2=None, op0=ALU.mult))
                q(v.tensor_scalar(out=oh2f[:], in0=oh2f[:], scalar1=rc["t2"][:], scalar2=None, op0=ALU.mult))
                return v.tensor_tensor(out=cwmat[:], in0=oh1f[:], in1=oh2f[:], op=ALU.add)
            t3 = sv.tok()

            @emit("P", deps=[t3, war("pb2")], inc=sp)
            def _(p, u=u):
                return p.transpose(pb[2][0:E, 0:128], cwmat[:], idf[:])
            t4 = sp.tok()

            @emit("V", deps=[t4], inc=sv)
            def _(v, u=u):
                return v.tensor_copy(out=cwTb[0:E, u * 128:(u + 1) * 128], in_=pb[2][0:E, 0:128])
            t_ru_prev = sv.tok()
            setread("pb2", t_ru_prev)
    t_cw_done = t_ru_prev
    esG.close()
    t_scope = [sp.tok(), sv.tok(), sa.tok()]


    esC0 = ExitStack()
    xp256 = esC0.enter_context(nc.sbuf_tensor("xp256", [128, NCH * 256], BF16))
    t_x256 = [None, None]
    for j in range(2):
        @emit("S", deps=[t_x256[0], t_gwT], inc=sx[0], k=16)
        def _(s, j=j):
            return s.dma_start(out=xstage[0][:], in_=pos_in[j * 128:(j + 1) * 128, :])
        t_pl = sx[0].tok()

        @emit("P", deps=[t_pl, war("pb0"), war("pb1")] + t_scope, inc=sp)
        def _(p, j=j):
            for c in range(4):
                p.transpose(pb[0][:, c * 128:(c + 1) * 128], xstage[0][:, c * 128:(c + 1) * 128], idf[:])
            i = None
            for c in range(4):
                i = p.transpose(pb[1][:, c * 128:(c + 1) * 128], xstage[0][:, (4 + c) * 128:(5 + c) * 128], idf[:])
            return i
        t_ptr = sp.tok()

        @emit("V", deps=[t_ptr] + t_scope + [t_xn[c][0] for c in range(NCH)], inc=sv)
        def _(v, j=j):
            xv = xp256[:].rearrange("p (c t) -> p c t", c=NCH)[:, :, j * 128:(j + 1) * 128]
            xnv = xn1[:].rearrange("p (c t) -> p c t", c=NCH)[:, :, j * 128:(j + 1) * 128]
            sv.inc(v.tensor_tensor(out=xv[:, 0:4, :], in0=xnv[:, 0:4, :],
                                   in1=pb[0][:].rearrange("p (c t) -> p c t", c=4), op=ALU.add), 1)
            chain(v, sv)
            return v.tensor_tensor(out=xv[:, 4:8, :], in0=xnv[:, 4:8, :],
                                   in1=pb[1][:].rearrange("p (c t) -> p c t", c=4), op=ALU.add)
        t_x256[j] = sv.tok()
        setread("pb0", t_x256[j])
        setread("pb1", t_x256[j])
    t_xp256 = t_x256[1]

    if True:
        esC = ExitStack()
        cvs = esC.enter_context(nc.sbuf_tensor("cvs", [128, 4096], F32))
        gmb = esC.enter_context(nc.sbuf_tensor("gmb", [G, D], BF16))
        cvb = esC.enter_context(nc.sbuf_tensor("cvb", [128, 4096], BF16))
        wcT = esC.enter_context(nc.sbuf_tensor("wcT", [128, 32 * 128], BF16))
        t_comp_prev = None
        for oc in range(NCH):
            @emit("S", deps=[t_comp_prev], inc=scv, k=16)
            def _(s, oc=oc):
                return s.dma_start(out=cvs[:], in_=cw_in[oc * 128:(oc + 1) * 128, :, :]
                                   .rearrange("p i k -> p i k"))
            t_cl = scv.tok()

            @emit("V", deps=[t_cl], inc=sv)
            def _(v):
                return v.tensor_copy(out=cvb[:], in_=cvs[:])
            t_cv = sv.tok()

            t_wctr = []
            for kk in range(4):
                for ic in range(NCH):
                    pf = pbf if (kk * NCH + ic) % 2 == 0 else pbf2
                    pfk = "pbf" if (kk * NCH + ic) % 2 == 0 else "pbf2"

                    @emit("P", deps=[t_cv, war(pfk)], inc=sp)
                    def _(p, kk=kk, ic=ic, pf=pf):
                        iv = cvb[:].rearrange("p (i k) -> p k i", k=4)[:, kk, ic * 128:(ic + 1) * 128]
                        return p.transpose(pf[:, 0:128], iv, idb[:])
                    t_tr = sp.tok()

                    @emit("V", deps=[t_tr], inc=sv)
                    def _(v, kk=kk, ic=ic, pf=pf):
                        return v.tensor_copy(out=wcT[:, (kk * NCH + ic) * 128:(kk * NCH + ic + 1) * 128],
                                             in_=pf[:, 0:128])
                    tk2 = sv.tok()
                    setread(pfk, tk2)
                    t_wctr.append(tk2)

            @emit("P", deps=t_wctr + [t_xp256, t_rows, war("pb4")], inc=sp)
            def _(p, oc=oc):
                for kk in range(4):
                    for ic in range(NCH):
                        rv = xp256[:].rearrange("p (c t) -> p c t", c=NCH)[:, ic, 0:256] \
                            .rearrange("p (g k) -> p k g", k=4)[:, kk, :]
                        p.matmul(pb[4][:, 0:64], wcT[:, (kk * NCH + ic) * 128:(kk * NCH + ic + 1) * 128],
                                 rv, start=(kk == 0 and ic == 0), stop=False)
                return p.matmul(pb[4][:, 0:64], cbrow[0:1, oc * 128:(oc + 1) * 128], onesrb[0:1, 0:64],
                                start=False, stop=True)
            t_cmp = sp.tok()

            @emit("V", deps=[t_cmp], inc=sv)
            def _(v, oc=oc):
                return v.tensor_copy(out=gt_s[:, oc * 128: oc * 128 + 64], in_=pb[4][:, 0:64])
            t_comp_prev = sv.tok()
            setread("pb4", t_comp_prev)

        @emit("S", deps=[t_comp_prev], inc=scv, k=16)
        def _(s):
            return s.dma_start(out=cvs[0:G, 0:D], in_=gm_in)
        t_gml = scv.tok()

        @emit("V", deps=[t_gml], inc=sv)
        def _(v):
            return v.tensor_copy(out=gmb[:], in_=cvs[0:G, 0:D])
        t_gmc = sv.tok()

        for oc in range(NCH):
            pf = pbf if oc % 2 == 0 else pbf2
            pfk = "pbf" if oc % 2 == 0 else "pbf2"

            @emit("P", deps=[t_gmc, war(pfk)], inc=sp)
            def _(p, oc=oc, pf=pf):
                return p.transpose(pf[:, 0:64], gmb[:, oc * 128:(oc + 1) * 128], idb[0:G, 0:G])
            t_tr = sp.tok()

            @emit("V", deps=[t_tr], inc=sv)
            def _(v, oc=oc, pf=pf):
                return v.tensor_copy(out=gt_s[:, oc * 128 + 64:(oc + 1) * 128], in_=pf[:, 0:64])
            tk2 = sv.tok()
            setread(pfk, tk2)
        t_gt_done = sv.tok()

        for h in range(NH):
            pf = pbf if h % 2 == 0 else pbf2
            pfk = "pbf" if h % 2 == 0 else "pbf2"

            @emit("P", deps=[t_gt_done, war(pfk)], inc=sp)
            def _(p, h=h, pf=pf):
                sl = gt_s[64 * (h % 2): 64 * (h % 2) + 64, (h // 2) * 128:(h // 2) * 128 + 128]
                return p.transpose(pf[:, 0:64], sl, idb[64 * (h % 2):64 * (h % 2) + 64,
                                                       64 * (h % 2):64 * (h % 2) + 64])
            t_tr = sp.tok()

            @emit("V", deps=[t_tr], inc=sv)
            def _(v, h=h, pf=pf):
                return v.tensor_copy(out=gtT[:, h * HD:(h + 1) * HD], in_=pf[:, 0:64])
            tk2 = sv.tok()
            setread(pfk, tk2)
        t_gtT = sv.tok()
        esC.close()


    esC0.close()

    esH = ExitStack()

    def sbH(name, shape, dtp):
        return esH.enter_context(nc.sbuf_tensor(name, shape, dtp))

    xp_h = sbH("xp_h", [128, NCH * TH], BF16)
    local_h = sbH("local_h", [128, NCH * TH], BF16)
    glob_h = sbH("glob_h", [128, NCH * TH], BF16)
    abuf = [sbH("abuf0", [128, 4 * 512], BF16), sbH("abuf1", [128, 4 * 512], BF16)]
    vtb = [sbH("vtb0", [128, 512], BF16), sbH("vtb1", [128, 512], BF16)]
    rbc2 = sbH("rbc2", [128, 512], BF16)
    recips = sbH("recips", [128, 512], F32)
    nrow2 = sbH("nrow2", [1, TH], F32)
    negm8 = sbH("negm8", [1, TH], BF16)
    mcell = sbH("mcell", [1, 1], F32)
    mixt = sbH("mixt", [128, 512], BF16)
    t1t = sbH("t1t", [128, 512], BF16)
    _gwb0 = sbH("gwb0", [128, 2048], BF16)
    gwb = [_gwb0, _gwb0]
    xpsq = sbH("xpsq", [128, TH], BF16)

    t_gt_done = None
    t_gtT = None
    t_xp_reader = None
    t_loc_reader = None
    t_half_done = None
    t_store = {}
    t_slab_reader = {}
    t_out = [[None] * NCH for _ in range(2)]

    for hf in range(2):
        TOFF = hf * TH
        t_xp = [None] * 8
        for j in range(8):
            u = hf * 8 + j
            pp = j % 2

            @emit("S", deps=[t_store.get(("pos", 0)), t_store.get(("pos", 1)), t_xp256,
                             t_xp_reader if hf else None], inc=sx[pp], k=16)
            def _(s, u=u, pp=pp):
                return s.dma_start(out=xstage[pp][:], in_=pos_in[u * 128:(u + 1) * 128, :])
            t_pl = sx[pp].tok()

            bk_ = "pb%d" % (2 * pp)
            bk2_ = "pb%d" % (2 * pp + 1)

            @emit("P", deps=[t_pl, war(bk_), war(bk2_)], inc=sp)
            def _(p, pp=pp):
                b0, b1 = pb[2 * pp], pb[2 * pp + 1]
                for c in range(4):
                    p.transpose(b0[:, c * 128:(c + 1) * 128], xstage[pp][:, c * 128:(c + 1) * 128], idf[:])
                i = None
                for c in range(4):
                    i = p.transpose(b1[:, c * 128:(c + 1) * 128], xstage[pp][:, (4 + c) * 128:(5 + c) * 128], idf[:])
                return i
            t_ptr = sp.tok()
            t_store[("pos", pp)] = t_ptr

            @emit("V", deps=[t_ptr] + t_scope + [t_xn[c][u // 4] for c in range(NCH)], inc=sv)
            def _(v, u=u, j=j, pp=pp):
                b0, b1 = pb[2 * pp], pb[2 * pp + 1]
                xv = xp_h[:].rearrange("p (c t) -> p c t", c=NCH)[:, :, j * 128:(j + 1) * 128]
                xnv = xn1[:].rearrange("p (c t) -> p c t", c=NCH)[:, :, u * 128:(u + 1) * 128]
                sv.inc(v.tensor_tensor(out=xv[:, 0:4, :], in0=xnv[:, 0:4, :],
                                       in1=b0[:].rearrange("p (c t) -> p c t", c=4), op=ALU.add), 1)
                chain(v, sv)
                return v.tensor_tensor(out=xv[:, 4:8, :], in0=xnv[:, 4:8, :],
                                       in1=b1[:].rearrange("p (c t) -> p c t", c=4), op=ALU.add)
            t_xp[j] = sv.tok()
            setread(bk_, t_xp[j])
            setread(bk2_, t_xp[j])

        t_xp_all = t_xp[7]

        NW = 2
        NTC = TH // 128
        for hp in range(NCH):
            @emit("V", deps=[t_xp_all, t_loc_reader, war_a_sq(pr, hp)], inc=sv)
            def _(v, hp=hp):
                return v.tensor_tensor(out=xpsq[:], in0=xp_h[:, hp * TH:(hp + 1) * TH],
                                       in1=xp_h[:, hp * TH:(hp + 1) * TH], op=ALU.mult)
            t_sqh = sv.tok()

            for sub in range(2):
                h = 2 * hp + sub
                base = 64 * sub
                @emit("P", deps=[t_sqh, war("pb3"), war("pb4")], inc=sp)
                def _(p, base=base):
                    p.matmul(pb[3][0:1, 0:512], onescol[base:base + 64, :], xpsq[base:base + 64, 0:512],
                             start=True, stop=True)
                    return p.matmul(pb[4][0:1, 0:512], onescol[base:base + 64, :], xpsq[base:base + 64, 512:1024],
                                    start=True, stop=True)
                t_nr = sp.tok()
                pr[("xpsq",)] = t_nr

                @emit("V", deps=[t_nr], inc=sv)
                def _(v):
                    sv.inc(v.tensor_copy(out=nrow2[0:1, 0:512], in_=pb[3][0:1, 0:512]), 1)
                    sv.inc(v.tensor_copy(out=nrow2[0:1, 512:1024], in_=pb[4][0:1, 0:512]), 1)
                    chain(v, sv)
                    return v.tensor_reduce(out=mcell[:], in_=nrow2[:], op=ALU.max, axis=AX.X)
                t_nr2 = sv.tok()
                setread("pb3", t_nr2)
                setread("pb4", t_nr2)

                @emit("V", deps=[t_nr2], inc=sv)
                def _(v):
                    return v.tensor_scalar(out=negm8[:], in0=nrow2[:], scalar1=mcell[0:1, 0:1],
                                           scalar2=-0.5, op0=ALU.add, op1=ALU.mult)
                t_nm = sv.tok()

                for w in range(NW):
                    woff = w * 512
                    for kc in range(4):
                        pf, pfk = (pbf, "pbf") if (w * 4 + kc) % 2 == 0 else (pbf2, "pbf2")

                        @emit("P", deps=[t_xp_all, war(pfk)], inc=sp)
                        def _(p, hp=hp, base=base, woff=woff, kc=kc, pf=pf):
                            iv = xp_h[base:base + 64, hp * TH + woff + kc * 128: hp * TH + woff + (kc + 1) * 128]
                            return p.transpose(pf[:, 0:64], iv, idb[base:base + 64, base:base + 64])
                        t_tr = sp.tok()

                        @emit("V", deps=[t_tr, pr.get(("av", sub))], inc=sv)
                        def _(v, w=w, kc=kc, sub=sub, pf=pf):
                            return v.tensor_copy(out=vtb[sub][:, (w * 4 + kc) * 64:(w * 4 + kc + 1) * 64],
                                                 in_=pf[:, 0:64])
                        tk2 = sv.tok()
                        setread(pfk, tk2)
                t_vtb = sv.tok()

                for w in range(NW):
                    woff = w * 512
                    bs, bsk = (pb[0], "pb0") if sub == 0 else (pb[1], "pb1")
                    for kc in range(4):
                        @emit("P", deps=[t_nm, t_xp_all, war(bsk)], inc=sp)
                        def _(p, hp=hp, base=base, woff=woff, kc=kc, bs=bs):
                            p.matmul(bs[:, 0:512],
                                     xp_h[base:base + 64, hp * TH + woff + kc * 128: hp * TH + woff + (kc + 1) * 128],
                                     xp_h[base:base + 64, hp * TH + woff: hp * TH + woff + 512],
                                     start=True, stop=False)
                            return p.matmul(bs[:, 0:512], onescb[:], negm8[0:1, woff:woff + 512],
                                            start=False, stop=True)
                        t_sc = sp.tok()

                        @emit("A", deps=[t_sc, war_ab(pr, sub)], inc=sa)
                        def _(a, kc=kc, sub=sub, bs=bs):
                            return a.activation(out=abuf[sub][:, kc * 512:(kc + 1) * 512], in_=bs[:, 0:512],
                                                func=AF.Exp, scale=np.float32(0.125))
                        t_ex = sa.tok()
                        setread(bsk, t_ex)

                        @emit("P", deps=[t_ex, war("pb3")], inc=sp)
                        def _(p, kc=kc, sub=sub, base=base):
                            return p.matmul(pb[3][base:base + 1, 0:512], onescol[:],
                                            abuf[sub][:, kc * 512:(kc + 1) * 512],
                                            start=(kc == 0), stop=(kc == 3))
                        t_sum = sp.tok()

                    @emit("V", deps=[t_sum, pr.get(("recr", sub))], inc=sv)
                    def _(v, base=base):
                        return v.reciprocal(out=recips[base:base + 1, 0:512], in_=pb[3][base:base + 1, 0:512])
                    t_rc = sv.tok()
                    setread("pb3", t_rc)

                    @emit("P", deps=[t_rc, war("pb4")], inc=sp)
                    def _(p, base=base, sub=sub):
                        if sub == 0:
                            return p.matmul(pb[4][0:64, 0:512], onesc[0:1, 0:64], recips[0:1, 0:512],
                                            start=True, stop=True)
                        return p.matmul(pb[4][64:128, 0:512], ones2d[64:65, 0:64], recips[64:65, 0:512],
                                        start=True, stop=True)
                    t_bcm = sp.tok()
                    pr[("recr", sub)] = t_bcm

                    @emit("V", deps=[t_bcm, pr.get(("rbcr", sub))], inc=sv)
                    def _(v, base=base):
                        return v.tensor_copy(out=rbc2[base:base + 64, :], in_=pb[4][base:base + 64, 0:512])
                    t_rb = sv.tok()
                    setread("pb4", t_rb)

                    @emit("P", deps=[t_vtb, sa.tok(), war("pb2")], inc=sp)
                    def _(p, w=w, sub=sub, base=base):
                        i = None
                        for kc in range(4):
                            i = p.matmul(pb[2][base:base + 64, 0:512],
                                         vtb[sub][:, (w * 4 + kc) * 64:(w * 4 + kc + 1) * 64],
                                         abuf[sub][:, kc * 512:(kc + 1) * 512],
                                         start=(kc == 0), stop=(kc == 3))
                        return i
                    t_av = sp.tok()
                    pr[("av", sub)] = t_av
                    pr[("abr", sub)] = t_av

                    @emit("V", deps=[t_av, t_rb], inc=sv)
                    def _(v, hp=hp, w=w, base=base):
                        return v.tensor_tensor(out=local_h[base:base + 64, hp * TH + w * 512: hp * TH + (w + 1) * 512],
                                               in0=pb[2][base:base + 64, 0:512],
                                               in1=rbc2[base:base + 64, :], op=ALU.mult)
                    t_lw = sv.tok()
                    setread("pb2", t_lw)
                    pr[("rbcr", sub)] = t_lw

            for tc in range(NTC):
                for sub in range(2):
                    h = 2 * hp + sub
                    base = 64 * sub

                    @emit("P", deps=[t_xp_all, t_gt_done, war("pb0")], inc=sp)
                    def _(p, hp=hp, base=base, tc=tc):
                        return p.matmul(pb[0][:, 0:128],
                                        xp_h[base:base + 64, hp * TH + tc * 128: hp * TH + (tc + 1) * 128],
                                        gt_s[base:base + 64, hp * 128:(hp + 1) * 128],
                                        start=True, stop=True)
                    t_gs = sp.tok()

                    @emit("V", deps=[t_gs], inc=sv)
                    def _(v):
                        return v.tensor_reduce(out=rc["nmax"][:], in_=pb[0][:, 0:128], op=ALU.max,
                                               axis=AX.X, negate=True)
                    t_gm2 = sv.tok()

                    @emit("A", deps=[t_gm2, pr.get(("mixr",))], inc=sa)
                    def _(a):
                        return a.activation(out=mixt[:, 0:128], in_=pb[0][:, 0:128], func=AF.Exp,
                                            bias=rc["nmax"][:], scale=np.float32(0.125),
                                            accum_out=rc["sum"][:])
                    t_ge = sa.tok()
                    setread("pb0", t_ge)

                    @emit("V", deps=[t_ge, pr.get(("t1tr",))], inc=sv)
                    def _(v):
                        sv.inc(v.reciprocal(out=rc["rec"][:], in_=rc["sum"][:]), 1)
                        chain(v, sv)
                        return v.tensor_scalar(out=t1t[:, 0:128], in0=mixt[:, 0:128],
                                               scalar1=rc["rec"][:], scalar2=None, op0=ALU.mult)
                    t_gn = sv.tok()
                    pr[("mixr",)] = t_gn

                    @emit("P", deps=[t_gn, war("pbf")], inc=sp)
                    def _(p):
                        return p.transpose(pbf[:, 0:128], t1t[:, 0:128], idb[:])
                    t_gt2 = sp.tok()
                    pr[("t1tr",)] = t_gt2

                    @emit("V", deps=[t_gt2, pr.get(("gatr",))], inc=sv)
                    def _(v):
                        return v.tensor_copy(out=mixt[:, 128:256], in_=pbf[:, 0:128])
                    t_gaT = sv.tok()
                    setread("pbf", t_gaT)

                    @emit("P", deps=[t_gaT, t_gtT, war("pb2") if sub == 0 else None], inc=sp)
                    def _(p, h=h, base=base):
                        return p.matmul(pb[2][base:base + 64, 0:128], gtT[:, h * HD:(h + 1) * HD],
                                        mixt[:, 128:256], start=True, stop=True)
                    t_gav = sp.tok()
                    pr[("gav", sub)] = t_gav
                    pr[("gatr",)] = t_gav

                @emit("V", deps=[pr[("gav", 0)], pr[("gav", 1)]], inc=sv)
                def _(v, hp=hp, tc=tc):
                    return v.tensor_copy(out=glob_h[:, hp * TH + tc * 128: hp * TH + (tc + 1) * 128],
                                         in_=pb[2][:, 0:128])
                t_gw2 = sv.tok()
                setread("pb2", t_gw2)
                pr[("globc", hp, tc)] = t_gw2

        t_attn_done = sv.tok()

        t_dbgh = None
        if "xp" in dbg:
            @emit("G", deps=[t_attn_done])
            def _(g, TOFF=TOFF):
                for c in range(NCH):
                    sdbg.inc(g.dma_start(out=dbg["xp"][c * 128:(c + 1) * 128, TOFF:TOFF + TH],
                                        in_=xp_h[:, c * TH:(c + 1) * TH]), 16)
                    sdbg.inc(g.dma_start(out=dbg["local"][c * 128:(c + 1) * 128, TOFF:TOFF + TH],
                                        in_=local_h[:, c * TH:(c + 1) * TH]), 16)
                    sdbg.inc(g.dma_start(out=dbg["glob"][c * 128:(c + 1) * 128, TOFF:TOFF + TH],
                                        in_=glob_h[:, c * TH:(c + 1) * TH]), 16)
                return None
            t_dbgh = sdbg.tok()

        def mixsl(dc):
            return (abuf[0] if dc < 4 else abuf[1])[:, (dc % 4) * 512:(dc % 4) * 512 + 512]

        for vt in range(2):
            t_mix = [None] * NCH
            for dc in range(NCH):
                @emit("G", deps=[t_slab_reader.get(("gwb", 0))], inc=sgd, k=16)
                def _(g, dc=dc):
                    srcv = gw_in[:, dc * 128:(dc + 1) * 128].rearrange("(c p) d -> p c d", p=128)
                    return g.dma_start(out=gwb[0][:].rearrange("p (c d) -> p c d", c=16), in_=srcv)
                t_gc = sgd.tok()

                bs, bsk = (pb[0], "pb0") if dc % 2 == 0 else (pb[1], "pb1")

                @emit("P", deps=[t_gc, t_attn_done, t_gbc, war(bsk)], inc=sp)
                def _(p, dc=dc, vt=vt, bs=bs):
                    i = None
                    for cc in range(16):
                        rhs = (local_h if cc < 8 else glob_h)[:, (cc % 8) * TH + vt * 512: (cc % 8) * TH + vt * 512 + 512]
                        i = p.matmul(bs[:, 0:512], gwb[0][:, cc * 128:(cc + 1) * 128], rhs,
                                     start=(cc == 0), stop=(cc == 15))
                    return i
                t_gmm = sp.tok()
                t_slab_reader[("gwb", 0)] = t_gmm

                @emit("A", deps=[t_gmm, t_dbgh, pr.get(("aout", dc, vt - 1)) if vt > 0 else None], inc=sa)
                def _(a, dc=dc, bs=bs):
                    return a.activation(out=mixsl(dc), in_=bs[:, 0:512], func=AF.Sigmoid, bias=gbcol[:, dc:dc + 1])
                t_mix[dc] = sa.tok()
                setread(bsk, t_mix[dc])

            for dc in range(NCH):
                @emit("V", deps=[t_mix[dc], t_mix[NCH - 1], pr.get(("aoutprev",))], inc=sv)
                def _(v, dc=dc, vt=vt):
                    lsl = local_h[:, dc * TH + vt * 512: dc * TH + vt * 512 + 512]
                    gsl = glob_h[:, dc * TH + vt * 512: dc * TH + vt * 512 + 512]
                    sv.inc(v.tensor_tensor(out=t1t[:], in0=lsl, in1=gsl, op=ALU.subtract), 1)
                    chain(v, sv)
                    sv.inc(v.tensor_tensor(out=t1t[:], in0=mixsl(dc), in1=t1t[:], op=ALU.mult), 1)
                    chain(v, sv)
                    return v.tensor_tensor(out=lsl, in0=gsl, in1=t1t[:], op=ALU.add)
                pr[("aout", dc, vt)] = sv.tok()
                pr[("aoutprev",)] = pr[("aout", dc, vt)]
        t_aout = sv.tok()

        owb = gwb
        for dc in range(NCH):
            pp = dc % 2

            @emit("G", deps=[t_slab_reader.get(("gwb", 0))], inc=sgd, k=16)
            def _(g, dc=dc, pp=pp):
                srcv = ow_in[:, dc * 128:(dc + 1) * 128].rearrange("(c p) d -> p c d", p=128)
                return g.dma_start(out=owb[pp][:, 0:1024].rearrange("p (c d) -> p c d", c=NCH), in_=srcv)
            t_oc = sgd.tok()

            for vt in range(2):
                bs, bsk = (pb[0], "pb0") if vt == 0 else (pb[1], "pb1")

                @emit("P", deps=[t_oc, t_aout, t_gbc, war(bsk)], inc=sp)
                def _(p, dc=dc, vt=vt, pp=pp, bs=bs):
                    i = None
                    for cc in range(NCH):
                        i = p.matmul(bs[:, 0:512], owb[pp][:, cc * 128:(cc + 1) * 128],
                                     local_h[:, cc * TH + vt * 512: cc * TH + vt * 512 + 512],
                                     start=(cc == 0), stop=(cc == NCH - 1))
                    return i
                t_omm = sp.tok()
                t_slab_reader[("gwb", 0)] = t_omm

                @emit("V", deps=[t_omm, pr.get(("opev",))], inc=sv)
                def _(v, dc=dc, vt=vt, bs=bs, TOFF=TOFF):
                    sl = slice(dc * T + TOFF + vt * 512, dc * T + TOFF + vt * 512 + 512)
                    sv.inc(v.tensor_scalar(out=recips[:], in0=bs[:, 0:512],
                                           scalar1=obcol[:, dc:dc + 1], scalar2=None, op0=ALU.add), 1)
                    chain(v, sv)
                    return v.tensor_tensor(out=resid[:, sl], in0=resid[:, sl], in1=recips[:], op=ALU.add)
                t_out[hf][dc] = sv.tok()
                pr[("opev",)] = t_out[hf][dc]
                setread(bsk, t_out[hf][dc])
        t_xp_reader = sp.tok()
        t_loc_reader = t_xp_reader
        t_half_done = t_xp_reader

    esH.close()

    esF = ExitStack()

    def sbF(name, shape, dtp):
        return esF.enter_context(nc.sbuf_tensor(name, shape, dtp))

    h1b = sbF("h1b", [128, NCH * TH], BF16)
    h2b = sbF("h2b", [128, NCH * TH], BF16)
    fws = [sbF("fws0", [128, 1024], F32), sbF("fws1", [128, 1024], F32)]
    fwb = [sbF("fwb0", [128, 1024], BF16), sbF("fwb1", [128, 1024], BF16)]
    cwb = sbF("cwb", [128, TH], BF16)
    b1r = sbF("b1r", [1, D], BF16)
    b2r = sbF("b2r", [1, D], BF16)
    b1cL = sbF("b1cL", [128, NCH], F32)
    b2cL = sbF("b2cL", [128, NCH], F32)
    fsc = sbF("fsc", [128, 512], F32)

    rtok = {}
    for hf in range(2):
        for dc in range(NCH):
            for vt in range(2):
                rtok[(hf, dc, vt)] = t_out[hf][dc]

    t_join = sv.tok()
    t_joinp = sp.tok()
    t_joina = sa.tok()
    t_h1_reader = t_joinp
    t_h2_reader = t_joinp
    t_cwb_reader = t_joinp
    t_br_reader = t_joinp
    t_sl_reader = {("f", 0): (sv, max(t_join[1], 0)), ("f", 1): (sv, max(t_join[1], 0))}
    for kk in (("f", 0), ("f", 1)):
        pass
    t_sl_reader = {("f", 0): t_join, ("f", 1): t_join, ("fb", 0): t_joinp, ("fb", 1): t_joinp}
    for hf in range(2):
        TOFF = hf * TH
        for e in range(E):
            @emit("P", deps=[t_cw_done, t_erows, t_cwb_reader, war("pb5"), war("pb4")], inc=sp)
            def _(p, e=e, TOFF=TOFF):
                p.matmul(pb[5][:, 0:512], erowS[:, e * 128:(e + 1) * 128], cwTb[0:E, TOFF: TOFF + 512],
                         start=True, stop=True)
                return p.matmul(pb[4][:, 0:512], erowS[:, e * 128:(e + 1) * 128],
                                cwTb[0:E, TOFF + 512: TOFF + 1024], start=True, stop=True)
            t_cwm = sp.tok()

            @emit("V", deps=[t_cwm, war("pb5"), war("pb4")], inc=sv)
            def _(v):
                sv.inc(v.tensor_copy(out=cwb[:, 0:512], in_=pb[5][:, 0:512]), 1)
                chain(v, sv)
                return v.tensor_copy(out=cwb[:, 512:1024], in_=pb[4][:, 0:512])
            t_cwb = sv.tok()
            setread("pb5", t_cwb)
            setread("pb4", t_cwb)

            @emit("G", deps=[t_br_reader])
            def _(g, e=e):
                sgd.inc(g.dma_start(out=b1r[:], in_=fb1[e:e + 1, :]), 16)
                sgd.inc(g.dma_start(out=b2r[:], in_=fb2[e:e + 1, :]), 16)
                return None
            t_br0 = sgd.tok()

            @emit("P", deps=[t_br0, war("pb4")], inc=sp)
            def _(p, e=e):
                for hc in range(NCH):
                    p.matmul(pb[4][:, hc:hc + 1], b1r[0:1, hc * 128:(hc + 1) * 128],
                             onesrb[0:1, 0:1], start=True, stop=True)
                i = None
                for dc in range(NCH):
                    i = p.matmul(pb[4][:, NCH + dc:NCH + dc + 1], b2r[0:1, dc * 128:(dc + 1) * 128],
                                 onesrb[0:1, 0:1], start=True, stop=True)
                return i
            t_brm = sp.tok()

            @emit("V", deps=[t_brm], inc=sv)
            def _(v):
                sv.inc(v.tensor_copy(out=b1cL[:], in_=pb[4][:, 0:NCH]), 1)
                chain(v, sv)
                return v.tensor_copy(out=b2cL[:], in_=pb[4][:, NCH:2 * NCH])
            t_br = sv.tok()
            setread("pb4", t_br)

            t_h1 = {}
            for hc in range(NCH):
                pp = hc % 2

                @emit("S", deps=[t_sl_reader.get(("f", pp))], inc=swl[pp], k=16)
                def _(s, e=e, hc=hc, pp=pp):
                    srcv = fw1[e:e + 1, :, hc * 128:(hc + 1) * 128].rearrange("o (c p) h -> p o c h", p=128)
                    return s.dma_start(out=fws[pp][:].rearrange("p (o c h) -> p o c h", o=1, c=NCH), in_=srcv)
                t_wl = swl[pp].tok()

                @emit("V", deps=[t_wl, t_sl_reader.get(("fb", pp))], inc=sv)
                def _(v, pp=pp):
                    return v.tensor_copy(out=fwb[pp][:], in_=fws[pp][:])
                t_wc = sv.tok()
                t_sl_reader[("f", pp)] = t_wc

                for vt in range(2):
                    bi = (hc % 2) * 2 + vt
                    bs, bsk = pb[bi], "pb%d" % bi

                    @emit("P", deps=[t_wc, t_br, war(bsk)] + [t_xn[c][(TOFF + vt * 512) // 512] for c in range(NCH)], inc=sp)
                    def _(p, hc=hc, vt=vt, pp=pp, bs=bs, TOFF=TOFF):
                        i = None
                        for c in range(NCH):
                            i = p.matmul(bs[:, 0:512], fwb[pp][:, c * 128:(c + 1) * 128],
                                         xn1[:, c * T + TOFF + vt * 512: c * T + TOFF + vt * 512 + 512],
                                         start=(c == 0), stop=(c == NCH - 1))
                        return i
                    t_mm = sp.tok()
                    t_sl_reader[("fb", pp)] = t_mm

                    @emit("V", deps=[t_mm, t_h1_reader], inc=sv)
                    def _(v, hc=hc, vt=vt, bs=bs):
                        return v.tensor_scalar(out=h1b[:, hc * TH + vt * 512: hc * TH + vt * 512 + 512],
                                               in0=bs[:, 0:512], scalar1=b1cL[:, hc:hc + 1],
                                               scalar2=0.0, op0=ALU.add, op1=ALU.max)
                    t_h1[(hc, vt)] = sv.tok()
                    setread(bsk, t_h1[(hc, vt)])

            t_h2 = {}
            for hc in range(NCH):
                pp = hc % 2

                @emit("S", deps=[t_sl_reader.get(("f", pp))], inc=swl[pp], k=16)
                def _(s, e=e, hc=hc, pp=pp):
                    srcv = fw2[e:e + 1, :, hc * 128:(hc + 1) * 128].rearrange("o (c p) h -> p o c h", p=128)
                    return s.dma_start(out=fws[pp][:].rearrange("p (o c h) -> p o c h", o=1, c=NCH), in_=srcv)
                t_wl = swl[pp].tok()

                @emit("V", deps=[t_wl, t_sl_reader.get(("fb", pp))], inc=sv)
                def _(v, pp=pp):
                    return v.tensor_copy(out=fwb[pp][:], in_=fws[pp][:])
                t_wc = sv.tok()
                t_sl_reader[("f", pp)] = t_wc

                for vt in range(2):
                    bi = (hc % 2) * 2 + vt
                    bs, bsk = pb[bi], "pb%d" % bi
                    hdeps = [t_h1[(c, vt)] for c in range(NCH)]

                    @emit("P", deps=[t_wc, t_br, war(bsk)] + hdeps, inc=sp)
                    def _(p, hc=hc, vt=vt, pp=pp, bs=bs):
                        i = None
                        for c in range(NCH):
                            i = p.matmul(bs[:, 0:512], fwb[pp][:, c * 128:(c + 1) * 128],
                                         h1b[:, c * TH + vt * 512: c * TH + vt * 512 + 512],
                                         start=(c == 0), stop=(c == NCH - 1))
                        return i
                    t_mm = sp.tok()
                    t_sl_reader[("fb", pp)] = t_mm

                    @emit("V", deps=[t_mm, t_h2_reader], inc=sv)
                    def _(v, hc=hc, vt=vt, bs=bs):
                        return v.tensor_scalar(out=h2b[:, hc * TH + vt * 512: hc * TH + vt * 512 + 512],
                                               in0=bs[:, 0:512], scalar1=b2cL[:, hc:hc + 1],
                                               scalar2=0.0, op0=ALU.add, op1=ALU.max)
                    t_h2[(hc, vt)] = sv.tok()
                    setread(bsk, t_h2[(hc, vt)])
            t_h1_reader = t_mm

            for dc in range(NCH):
                pp = dc % 2

                @emit("S", deps=[t_sl_reader.get(("f", pp))], inc=swl[pp], k=16)
                def _(s, e=e, dc=dc, pp=pp):
                    srcv = fw3[e:e + 1, :, dc * 128:(dc + 1) * 128].rearrange("o (c p) h -> p o c h", p=128)
                    return s.dma_start(out=fws[pp][:].rearrange("p (o c h) -> p o c h", o=1, c=NCH), in_=srcv)
                t_wl = swl[pp].tok()

                @emit("V", deps=[t_wl, t_sl_reader.get(("fb", pp))], inc=sv)
                def _(v, pp=pp):
                    return v.tensor_copy(out=fwb[pp][:], in_=fws[pp][:])
                t_wc = sv.tok()
                t_sl_reader[("f", pp)] = t_wc

                for vt in range(2):
                    bs, bsk = (pb[2], "pb2") if vt == 0 else (pb[3], "pb3")
                    hdeps = [t_h2[(c, vt)] for c in range(NCH)]

                    @emit("P", deps=[t_wc, war(bsk)] + hdeps, inc=sp)
                    def _(p, dc=dc, vt=vt, pp=pp, bs=bs):
                        i = None
                        for c in range(NCH):
                            i = p.matmul(bs[:, 0:512], fwb[pp][:, c * 128:(c + 1) * 128],
                                         h2b[:, c * TH + vt * 512: c * TH + vt * 512 + 512],
                                         start=(c == 0), stop=(c == NCH - 1))
                        return i
                    t_mm3 = sp.tok()
                    t_sl_reader[("fb", pp)] = t_mm3

                    @emit("V", deps=[t_mm3, t_cwb, rtok[(hf, dc, vt)], pr.get(("fscprev",))], inc=sv)
                    def _(v, dc=dc, vt=vt, bs=bs, TOFF=TOFF):
                        sl = slice(dc * T + TOFF + vt * 512, dc * T + TOFF + vt * 512 + 512)
                        sv.inc(v.tensor_tensor(out=fsc[:], in0=bs[:, 0:512],
                                               in1=cwb[:, vt * 512:(vt + 1) * 512], op=ALU.mult), 1)
                        chain(v, sv)
                        return v.tensor_tensor(out=resid[:, sl], in0=resid[:, sl], in1=fsc[:], op=ALU.add)
                    rtok[(hf, dc, vt)] = sv.tok()
                    pr[("fscprev",)] = rtok[(hf, dc, vt)]
                    setread(bsk, rtok[(hf, dc, vt)])
            t_h2_reader = t_mm3
            t_cwb_reader = t_mm3 if False else rtok[(hf, NCH - 1, 1)]
            t_br_reader = t_mm

        for dc in range(NCH):
            for vt in range(2):
                bs, bsk = (pb[2], "pb2") if vt == 0 else (pb[3], "pb3")

                @emit("P", deps=[t_cw_done, war(bsk), t_rows], inc=sp)
                def _(p, dc=dc, vt=vt, bs=bs, TOFF=TOFF):
                    return p.matmul(bs[:, 0:512], b3S[:, dc * 128:(dc + 1) * 128],
                                    cwTb[0:E, TOFF + vt * 512: TOFF + vt * 512 + 512],
                                    start=True, stop=True)
                t_b3 = sp.tok()

                @emit("V", deps=[t_b3, rtok[(hf, dc, vt)]], inc=sv)
                def _(v, dc=dc, vt=vt, bs=bs, TOFF=TOFF):
                    sl = slice(dc * T + TOFF + vt * 512, dc * T + TOFF + vt * 512 + 512)
                    return v.tensor_tensor(out=resid[:, sl], in0=resid[:, sl], in1=bs[:, 0:512], op=ALU.add)
                rtok[(hf, dc, vt)] = sv.tok()
                setread(bsk, rtok[(hf, dc, vt)])

    esF.close()

    t_sst = {}
    for u in range(NU):
        hf = u // 8
        vt = (u % 8) // 4
        pp = u % 2
        deps = [rtok[(hf, dc, vt)] for dc in range(NCH)] + [war("pb4"), war("pb5")]

        @emit("P", deps=deps, inc=sp)
        def _(p, u=u):
            for c in range(4):
                p.transpose(pb[4][:, c * 128:(c + 1) * 128], resid[:, c * T + u * 128: c * T + (u + 1) * 128], idf[:])
            i = None
            for c in range(4):
                i = p.transpose(pb[5][:, c * 128:(c + 1) * 128],
                                resid[:, (4 + c) * T + u * 128: (4 + c) * T + (u + 1) * 128], idf[:])
            return i
        t_str = sp.tok()

        @emit("V", deps=[t_str, t_sst.get(("x2", 0)), t_sst.get(("x2", 1))], inc=sv)
        def _(v, pp=pp):
            sv.inc(v.tensor_copy(out=xstage[pp][:, 0:512], in_=pb[4][:, 0:512]), 1)
            chain(v, sv)
            return v.tensor_copy(out=xstage[pp][:, 512:1024], in_=pb[5][:, 0:512])
        t_cp = sv.tok()
        setread("pb4", t_cp)
        setread("pb5", t_cp)

        @emit("S", deps=[t_cp], inc=st, k=16)
        def _(s, u=u, pp=pp):
            return s.dma_start(out=x2_out[u * 128:(u + 1) * 128, :], in_=xstage[pp][:])
        t_sst[("x2", pp)] = st.tok()

    toks = [s_.tok() for s_ in ALLSEMS]
    for engname in ENG:
        @emit(engname, deps=toks)
        def _(e):
            return None

    es.close()
    return nc


def build_full(dbgkeys=()):
    import os
    upto = 99
    nc = bass.Bass("TRN2", target_bir_lowering=False)
    dt = nc.dram_tensor
    x_in = dt("x", [T, D], F32, kind="ExternalInput").ap()
    Wq = dt("Wq", [DS, D], F32, kind="ExternalInput").ap()
    bq = dt("bq", [DS], F32, kind="ExternalInput").ap()
    Wk = dt("Wk", [DS, D], F32, kind="ExternalInput").ap()
    bk = dt("bk", [DS], F32, kind="ExternalInput").ap()
    Wv = dt("Wv", [DS, D], F32, kind="ExternalInput").ap()
    bv = dt("bv", [DS], F32, kind="ExternalInput").ap()
    dec = dt("dec", [DS], F32, kind="ExternalInput").ap()
    Wo = dt("Wo", [D, DS], F32, kind="ExternalInput").ap()
    bo = dt("bo", [D], F32, kind="ExternalInput").ap()
    Wr = dt("Wr", [D, E], F32, kind="ExternalInput").ap()
    w1 = dt("w1", [E, D, H0], F32, kind="ExternalInput").ap()
    b1 = dt("b1", [E, H0], F32, kind="ExternalInput").ap()
    w2 = dt("w2", [E, H0, D], F32, kind="ExternalInput").ap()
    b2 = dt("b2", [E, D], F32, kind="ExternalInput").ap()
    identf = dt("identf", [128, 128], F32, kind="ExternalInput").ap()
    identb = dt("identb", [128, 128], BF16, kind="ExternalInput").ap()
    tril_s = dt("tril_s", [128, 128], BF16, kind="ExternalInput").ap()
    mask_ul = dt("mask_ul", [LSC, LSC], F32, kind="ExternalInput").ap()
    ecb_in = dt("ecb", [128, E], F32, kind="ExternalInput").ap()
    cpref = dt("cpref", [E], F32, kind="ExternalInput").ap()
    x1_out = dt("x1", [D, T], F32, kind="ExternalOutput").ap()
    dbg = {}
    if "xn" in dbgkeys:
        dbg["xn"] = dt("dbg_xn", [D, T], F32, kind="ExternalOutput").ap()
    if "qkv" in dbgkeys:
        for nm in ("q", "k", "v", "scan"):
            dbg[nm] = dt("dbg_" + nm, [DS, T], F32, kind="ExternalOutput").ap()
    if "route" in dbgkeys:
        for nm in ("e1", "e2", "g1", "g2", "lp1", "lp2", "id1", "id2"):
            dbg[nm] = dt("dbg_" + nm, [128, NU], F32, kind="ExternalOutput").ap()

    buckets = dt("buckets", [E * CLOC, D], BF16)
    exp_out_d = dt("exp_out_d", [E * CLOC, D], BF16)
    cc_in = dt("cc_in", [1, E], F32)
    cc_out = dt("cc_out", [8, E], F32)

    es = ExitStack()

    def sb(name, shape, dtp):
        return es.enter_context(nc.sbuf_tensor(name, shape, dtp))

    def pst(name, shape, dtp=F32):
        return es.enter_context(nc.psum_tensor(name, shape, dtp))

    resid = sb("resid", [128, NCH * T], F32)
    idf = sb("idf", [128, 128], F32)
    idb = sb("idb", [128, 128], BF16)
    trilS = sb("trilS", [128, 128], BF16)
    mUL = sb("mUL", [LSC, LSC], F32)
    ecbS = sb("ecbS", [128, E], F32)
    onesc = sb("onesc", [1, 128], F32)
    onescb = sb("onescb", [1, 128], BF16)
    onesrb = sb("onesrb", [1, 512], BF16)
    onesr = sb("onesr", [1, 512], F32)
    onescol = sb("onescol", [128, 1], BF16)
    srow_m = sb("srow_m", [1, 512], F32)
    srow_r = sb("srow_r", [1, 512], F32)
    vrow = sb("vrow", [1, 512], F32)
    vrow2 = sb("vrow2", [1, 512], F32)
    offs = sb("offs", [1, E], F32)
    coffs = sb("coffs", [1, E], F32)
    coffb = sb("coffb", [128, E], F32)
    cmask = sb("cmask", [E, 1], F32)
    cntrow = sb("cntrow", [1, E], F32)
    agrows = sb("agrows", [8, E], F32)
    g1c = sb("g1c", [128, NU], F32)
    g2c = sb("g2c", [128, NU], F32)
    e1c = sb("e1c", [128, NU], F32)
    e2c = sb("e2c", [128, NU], F32)
    mcolT = sb("mcolT", [128, NU], F32)
    rcolT = sb("rcolT", [128, NU], F32)
    csb = sb("csb", [128, E], F32)
    onescolf = sb("onescolf", [128, 1], F32)
    idx1 = sb("idx1", [128, NU], U32)
    idx2 = sb("idx2", [128, NU], U32)
    dbgstage = sb("dbgstage", [DS, 512], F32) if dbgkeys else None
    dbg_cols = {}
    if "route" in dbgkeys:
        for nm in ("e1", "e2", "g1", "g2", "lp1", "lp2"):
            dbg_cols[nm] = sb("dbgc_" + nm, [128, NU], F32)

    pb = [pst(f"pb{i}", [128, 512], F32) for i in range(6)]
    pbf = pst("pbf", [128, 1024], BF16)
    pbf2 = pst("pbf2", [128, 1024], BF16)

    sd = Sem(es, nc, "sd")
    sw = Sem(es, nc, "sw")
    sx = [Sem(es, nc, "sx0"), Sem(es, nc, "sx1")]
    sgt = [Sem(es, nc, "sgt0"), Sem(es, nc, "sgt1")]
    sw1 = [Sem(es, nc, "sw1a"), Sem(es, nc, "sw1b")]
    sw2 = [Sem(es, nc, "sw2a"), Sem(es, nc, "sw2b")]
    setm = [Sem(es, nc, "set0"), Sem(es, nc, "set1")]
    scc = Sem(es, nc, "scc")
    sgd = Sem(es, nc, "sgd")
    sv = Sem(es, nc, "sv")
    sp = Sem(es, nc, "sp")
    sa = Sem(es, nc, "sa")
    sg = Sem(es, nc, "sg")
    st = Sem(es, nc, "st")
    ALLSEMS = [sd, sw, sx[0], sx[1], sgt[0], sgt[1], sw1[0], sw1[1], sw2[0], sw2[1],
               setm[0], setm[1], scc, sgd, sv, sp, sa, sg, st]

    block = es.enter_context(nc.Block())
    ENG = {"S": block.sync, "V": block.vector, "A": block.scalar,
           "P": block.tensor, "G": block.gpsimd}

    def emit(eng, deps=(), inc=None, k=1):
        reg = ENG[eng]
        deps = [d for d in deps if d is not None]

        def deco(body):
            def _f(e):
                seen = {}
                for sem, val in deps:
                    if seen.get(id(sem), -1) < val:
                        seen[id(sem)] = val
                        e.wait_ge(sem.h, val)
                last = body(e)
                if inc is not None and last is not None:
                    inc.inc(last, k)
            reg(_f)
        return deco

    def chain(e, sem):
        e.wait_ge(sem.h, sem.n)

    pr = {}

    def war(key):
        return pr.get(key)

    def setread(key, tok):
        pr[key] = tok

    x_t = x_in.rearrange("(n p) d -> n p d", p=128)

    esA = ExitStack()

    def sbA(name, shape, dtp):
        return esA.enter_context(nc.sbuf_tensor(name, shape, dtp))

    xstage = [sbA("xstage0", [128, D], F32), sbA("xstage1", [128, D], F32)]
    wq_f = sbA("wq_f", [DS, D], F32)
    wk_f = sbA("wk_f", [DS, D], F32)
    wv_f = sbA("wv_f", [DS, D], F32)
    wo_f = sbA("wo_f", [128, NCH * DS], F32)
    wr_f = sbA("wr_f", [128, NCH * E], F32)
    wq_b = sbA("wq_b", [DS, D], BF16)
    wk_b = sbA("wk_b", [DS, D], BF16)
    wv_b = sbA("wv_b", [DS, D], BF16)
    wo_b = sbA("wo_b", [128, NCH * DS], BF16)
    wqT = sbA("wqT", [128, NCH * DS], BF16)
    wkT = sbA("wkT", [128, NCH * DS], BF16)
    wvT = sbA("wvT", [128, NCH * DS], BF16)
    woT = sbA("woT", [DS, D], BF16)
    wrS = sbA("wrS", [128, NCH * E], BF16)
    brow_f = sbA("brow_f", [1, 3 * DS + D], F32)
    brow = sbA("brow", [1, 3 * DS + D], BF16)
    bqkvc = sbA("bqkvc", [DS, 3], F32)
    dcol = sbA("dcol", [DS, 1], F32)
    dsig = sbA("dsig", [DS, 1], F32)
    lndc = sbA("lndc", [DS, 1], F32)
    lndr = sbA("lndr", [1, DS], F32)
    iotar = sbA("iotar", [1, LSC], F32)
    iotarr = sbA("iotarr", [1, LSC], F32)
    iotai = sbA("iotai", [1, LSC], I32)
    dpm = sbA("dpm", [LSC, DS], BF16)
    dpp = sbA("dpp", [LSC, DS], F32)
    dpr = sbA("dpr", [LSC, DS], BF16)
    drow = sbA("drow", [DS, DS], F32)
    dLrow = sbA("dLrow", [DS, DS], F32)
    xn = sbA("xn", [128, NCH * T], BF16)
    xbp = [sbA("xb0", [128, 512], BF16), sbA("xb1", [128, 512], BF16)]
    xsqp = [sbA("xsq0", [128, 512], BF16), sbA("xsq1", [128, 512], BF16)]
    xtmp = sbA("xtmp", [128, 512], F32)
    qf = sbA("qf", [DS, T], BF16)
    kf = sbA("kf", [DS, T], BF16)
    vf = sbA("vf", [DS, T], BF16)
    kT = sbA("kT", [DS, NSC * DS], BF16)
    vT = sbA("vT", [DS, NSC * DS], BF16)
    of = sbA("of", [DS, T], BF16)
    St = sbA("St", [DS, DS], F32)
    St_b = sbA("St_b", [DS, DS], BF16)
    uta = sbA("uta", [LSC, DS], BF16)
    utb = sbA("utb", [LSC, DS], BF16)
    Ata = sbA("Ata", [LSC, LSC], BF16)
    otok = sbA("otok", [LSC, DS], BF16)
    lgs = sbA("lgs", [E, 512], F32)
    lgT = sbA("lgT", [128, NU * E], F32)
    qexp = sbA("qexp", [128, E], F32)
    oh1f = sbA("oh1f", [128, E], F32)
    oh2f = sbA("oh2f", [128, E], F32)
    ohs_all = sbA("ohs_all", [128, NU * E], BF16)
    oh1_all = sbA("oh1_all", [128, NU * E], F32)
    oh2_all = sbA("oh2_all", [128, NU * E], F32)
    cum_u = sbA("cum_u", [128, E], F32)
    rc = {nm: sbA("rc_" + nm, [128, 1], F32) for nm in
          ("nmax", "sum", "rec", "m1", "m2", "e1", "e2", "lp1", "lp2",
           "gp1", "gp2", "k1", "k2", "ob1", "ob2", "t1", "t2", "if1", "if2")}
    rtmp = sbA("rtmp", [128, E], F32)
    rtmp2 = sbA("rtmp2", [128, E], F32)
    xtok = [sbA("xtok0", [128, D], BF16), sbA("xtok1", [128, D], BF16)]

    @emit("S")
    def _(s):
        sw.inc(s.dma_start(out=idf[:], in_=identf), 16)
        sw.inc(s.dma_start(out=idb[:], in_=identb), 16)
        sw.inc(s.dma_start(out=trilS[:], in_=tril_s), 16)
        sw.inc(s.dma_start(out=mUL[:], in_=mask_ul), 16)
        sw.inc(s.dma_start(out=ecbS[:], in_=ecb_in), 16)
        sw.inc(s.dma_start(out=cmask[:], in_=cpref[:, None]), 16)
        sw.inc(s.dma_start(out=dcol[:], in_=dec[:, None]), 16)
        sw.inc(s.dma_start(out=bqkvc[:, 0:1], in_=bq[:, None]), 16)
        sw.inc(s.dma_start(out=bqkvc[:, 1:2], in_=bk[:, None]), 16)
        sw.inc(s.dma_start(out=bqkvc[:, 2:3], in_=bv[:, None]), 16)
        sw.inc(s.dma_start(out=wq_f[:], in_=Wq), 16)
        sw.inc(s.dma_start(out=wk_f[:], in_=Wk), 16)
        sw.inc(s.dma_start(out=wv_f[:], in_=Wv), 16)
        sw.inc(s.dma_start(out=wo_f[:].rearrange("p (c s) -> p c s", c=NCH), in_=Wo.rearrange("(c p) s -> p c s", p=128)), 16)
        sw.inc(s.dma_start(out=wr_f[:].rearrange("p (c e) -> p c e", c=NCH), in_=Wr.rearrange("(c p) e -> p c e", p=128)), 16)
        sw.inc(s.dma_start(out=brow_f[0:1, 0:DS], in_=bq[None, :]), 16)
        sw.inc(s.dma_start(out=brow_f[0:1, DS:2 * DS], in_=bk[None, :]), 16)
        sw.inc(s.dma_start(out=brow_f[0:1, 2 * DS:3 * DS], in_=bv[None, :]), 16)
        sw.inc(s.dma_start(out=brow_f[0:1, 3 * DS:], in_=bo[None, :]), 16)
        return None
    t_wload = sw.tok()

    @emit("G", inc=sg)
    def _(g):
        g.memset(onesc[:], 1.0)
        g.memset(onescb[:], 1.0)
        g.memset(onesrb[:], 1.0)
        g.memset(onesr[:], 1.0)
        g.memset(onescol[:], 1.0)
        g.memset(onescolf[:], 1.0)
        g.memset(offs[:], 0.0)
        return g.iota(out=iotai[:], base=0, pattern=[[1, LSC]], channel_multiplier=0)
    t_ones = sg.tok()

    @emit("V", deps=[t_wload, t_ones], inc=sv)
    def _(v):
        v.tensor_copy(out=iotar[:], in_=iotai[:])
        v.tensor_copy(out=wq_b[:], in_=wq_f[:])
        v.tensor_copy(out=wk_b[:], in_=wk_f[:])
        v.tensor_copy(out=wv_b[:], in_=wv_f[:])
        v.tensor_copy(out=wo_b[:], in_=wo_f[:])
        v.tensor_copy(out=wrS[:], in_=wr_f[:])
        return v.tensor_copy(out=brow[:], in_=brow_f[:])
    t_wcast = sv.tok()

    @emit("G", deps=[t_wcast], inc=sg)
    def _(g):
        return g.iota(out=iotai[:], base=LSC - 1, pattern=[[-1, LSC]], channel_multiplier=0)
    t_iota2 = sg.tok()

    @emit("V", deps=[t_iota2], inc=sv)
    def _(v):
        return v.tensor_copy(out=iotarr[:], in_=iotai[:])
    t_iotarr = sv.tok()

    @emit("P", deps=[t_wcast], inc=sp)
    def _(p):
        for c in range(NCH):
            p.transpose(pbf[:, c * 64:(c + 1) * 64], wq_b[:, c * 128:(c + 1) * 128], idb[0:DS, 0:DS])
        i = None
        for c in range(NCH):
            i = p.transpose(pbf[:, 512 + c * 64: 512 + (c + 1) * 64], wk_b[:, c * 128:(c + 1) * 128], idb[0:DS, 0:DS])
        return i
    t_trqk = sp.tok()

    @emit("V", deps=[t_trqk], inc=sv)
    def _(v):
        v.tensor_copy(out=wqT[:], in_=pbf[:, 0:512])
        return v.tensor_copy(out=wkT[:], in_=pbf[:, 512:1024])
    t_evqk = sv.tok()
    setread("pbf", t_evqk)

    @emit("P", deps=[t_wcast, war("pbf")], inc=sp)
    def _(p):
        i = None
        for c in range(NCH):
            i = p.transpose(pbf[:, c * 64:(c + 1) * 64], wv_b[:, c * 128:(c + 1) * 128], idb[0:DS, 0:DS])
        return i
    t_trv = sp.tok()

    @emit("V", deps=[t_trv], inc=sv)
    def _(v):
        return v.tensor_copy(out=wvT[:], in_=pbf[:, 0:512])
    t_evv = sv.tok()
    setread("pbf", t_evv)

    @emit("P", deps=[t_evv], inc=sp)
    def _(p):
        i = None
        for c in range(NCH):
            i = p.transpose(pbf[0:DS, c * 128:(c + 1) * 128], wo_b[:, c * DS:(c + 1) * DS], idb[:])
        return i
    t_tro = sp.tok()

    @emit("V", deps=[t_tro], inc=sv)
    def _(v):
        return v.tensor_copy(out=woT[:], in_=pbf[0:DS, 0:1024])
    t_evvo = sv.tok()
    setread("pbf", t_evvo)

    @emit("A", deps=[t_wload], inc=sa)
    def _(a):
        sa.inc(a.activation(out=dsig[:], in_=dcol[:], func=AF.Sigmoid), 1)
        chain(a, sa)
        return a.activation(out=lndc[:], in_=dsig[:], func=AF.Ln)
    t_lnd = sa.tok()

    @emit("P", deps=[t_lnd, war("pb4")], inc=sp)
    def _(p):
        return p.transpose(pb[4][0:1, 0:DS], lndc[:], idf[0:DS, 0:DS])
    t_lndT = sp.tok()

    @emit("V", deps=[t_lndT], inc=sv)
    def _(v):
        return v.tensor_copy(out=lndr[:], in_=pb[4][0:1, 0:DS])
    t_lndr = sv.tok()
    setread("pb4", t_lndr)

    @emit("P", deps=[t_lndr, t_iotarr, war("pb4")], inc=sp)
    def _(p):
        p.matmul(pb[4][0:LSC, 0:DS], iotar[:], lndr[:], start=True, stop=True)
        p.matmul(pb[4][0:LSC, DS:2 * DS], iotarr[:], lndr[:], start=True, stop=True)
        return p.matmul(pb[4][0:DS, 2 * DS:3 * DS], onesc[0:1, 0:DS], lndr[:], start=True, stop=True)
    t_outer = sp.tok()

    @emit("A", deps=[t_outer], inc=sa)
    def _(a):
        a.activation(out=dpm[:], in_=pb[4][0:LSC, 0:DS], func=AF.Exp, scale=np.float32(-1.0), bias=np.float32(-PSCL))
        a.activation(out=dpp[:], in_=pb[4][0:LSC, 0:DS], func=AF.Exp, scale=np.float32(1.0), bias=np.float32(PSCL))
        a.activation(out=dpr[:], in_=pb[4][0:LSC, DS:2 * DS], func=AF.Exp)
        a.activation(out=drow[:], in_=pb[4][0:DS, 2 * DS:3 * DS], func=AF.Exp, scale=np.float32(1.0), bias=np.float32(-PSCL))
        return a.activation(out=dLrow[:], in_=pb[4][0:DS, 2 * DS:3 * DS], func=AF.Exp, scale=np.float32(LSC))
    t_tables = sa.tok()
    setread("pb4", t_tables)

    t_xtr = [None] * 16
    t_xev = [None] * 16
    for i in range(16):
        @emit("S", deps=([t_xtr[i - 2]] if i >= 2 else []), inc=sx[i % 2], k=16)
        def _(s, i=i):
            return s.dma_start(out=xstage[i % 2][:], in_=x_t[i])
        t_xload = sx[i % 2].tok()

        b0k, b1k = "pb%d" % (2 * (i % 2)), "pb%d" % (2 * (i % 2) + 1)

        @emit("P", deps=[t_xload, war(b0k), war(b1k)], inc=sp)
        def _(p, i=i):
            b0, b1 = pb[2 * (i % 2)], pb[2 * (i % 2) + 1]
            for c in range(4):
                p.transpose(b0[:, c * 128:(c + 1) * 128], xstage[i % 2][:, c * 128:(c + 1) * 128], idf[:])
            j = None
            for c in range(4):
                j = p.transpose(b1[:, c * 128:(c + 1) * 128], xstage[i % 2][:, (4 + c) * 128:(5 + c) * 128], idf[:])
            return j
        t_xtr[i] = sp.tok()

        @emit("V", deps=[t_xtr[i]], inc=sv)
        def _(v, i=i):
            b0, b1 = pb[2 * (i % 2)], pb[2 * (i % 2) + 1]
            rv = resid[:].rearrange("p (c t) -> p c t", c=NCH)[:, :, i * 128:(i + 1) * 128]
            v.tensor_copy(out=rv[:, 0:4, :], in_=b0[:].rearrange("p (c t) -> p c t", c=4))
            return v.tensor_copy(out=rv[:, 4:8, :], in_=b1[:].rearrange("p (c t) -> p c t", c=4))
        t_xev[i] = sv.tok()
        setread(b0k, t_xev[i])
        setread(b1k, t_xev[i])

    if upto >= 2:
        t_xn = [[None] * NV for _ in range(NCH)]
        t_residln = [[None] * NV for _ in range(NCH)]
        t_norm_prev = None
        for v_i in range(NV):
            dep_x = [t_xev[4 * v_i + c] for c in range(4)]
            t_mmprev = None
            t_cast = [None] * NCH
            for c in range(NCH):
                pp = c % 2

                @emit("V", deps=dep_x + [t_mmprev], inc=sv)
                def _(v, c=c, v_i=v_i, pp=pp):
                    return v.tensor_copy(out=xbp[pp][:], in_=resid[:, c * T + v_i * 512: c * T + v_i * 512 + 512])
                t_cast[c] = sv.tok()

                @emit("V", deps=[t_cast[c]], inc=sv)
                def _(v, pp=pp):
                    return v.tensor_tensor(out=xsqp[pp][:], in0=xbp[pp][:], in1=xbp[pp][:], op=ALU.mult)
                t_sq = sv.tok()

                @emit("P", deps=[t_cast[c], t_sq, t_ones, war("pb4"), war("pb5")], inc=sp)
                def _(p, c=c, pp=pp):
                    p.matmul(pb[4][0:1, 0:512], onescol[:], xbp[pp][:], start=(c == 0), stop=(c == NCH - 1))
                    return p.matmul(pb[5][0:1, 0:512], onescol[:], xsqp[pp][:], start=(c == 0), stop=(c == NCH - 1))
                t_mmprev = sp.tok()
            t_stats = t_mmprev

            @emit("V", deps=[t_stats], inc=sv)
            def _(v, v_i=v_i):
                v.tensor_scalar(out=srow_m[:], in0=pb[4][0:1, 0:512], scalar1=1.0 / D, scalar2=None, op0=ALU.mult)
                sv.inc(v.tensor_scalar(out=vrow2[:], in0=pb[5][0:1, 0:512], scalar1=1.0 / D, scalar2=None, op0=ALU.mult), 1)
                chain(v, sv)
                sv.inc(v.tensor_tensor(out=vrow[:], in0=srow_m[:], in1=srow_m[:], op=ALU.mult), 1)
                chain(v, sv)
                return v.tensor_tensor(out=vrow[:], in0=vrow2[:], in1=vrow[:], op=ALU.subtract)
            t_var = sv.tok()
            setread("pb4", t_var)
            setread("pb5", t_var)

            @emit("A", deps=[t_var], inc=sa)
            def _(a, v_i=v_i):
                return a.activation(out=vrow[:], in_=vrow[:], func=AF.Sqrt, bias=np.float32(1e-5))
            t_sqrt = sa.tok()

            @emit("V", deps=[t_sqrt], inc=sv)
            def _(v, v_i=v_i):
                return v.reciprocal(out=srow_r[:], in_=vrow[:])
            t_rstd = sv.tok()

            @emit("P", deps=[t_rstd, t_ones, war("pb4"), war("pb5")], inc=sp)
            def _(p, v_i=v_i):
                p.matmul(pb[4][:, 0:512], onesc[:], srow_m[:], start=True, stop=True)
                return p.matmul(pb[5][:, 0:512], onesc[:], srow_r[:], start=True, stop=True)
            t_bcast = sp.tok()

            @emit("P", deps=dep_x + [t_wload, war("pb3")], inc=sp)
            def _(p, v_i=v_i):
                i = None
                for c in range(NCH):
                    i = p.matmul(pb[3][0:E, :], wr_f[:, c * E:(c + 1) * E],
                                 resid[:, c * T + v_i * 512: c * T + v_i * 512 + 512],
                                 start=(c == 0), stop=(c == NCH - 1))
                return i
            t_Ax = sp.tok()

            @emit("V", deps=[t_Ax], inc=sv)
            def _(v):
                return v.tensor_copy(out=lgs[:], in_=pb[3][0:E, :])
            t_Axs = sv.tok()
            setread("pb3", t_Axs)

            @emit("P", deps=[t_Axs, t_rstd, war("pb2")], inc=sp)
            def _(p, v_i=v_i):
                for j in range(4):
                    p.transpose(pb[2][:, j * E:(j + 1) * E], lgs[:, j * 128:(j + 1) * 128], idf[0:E, 0:E])
                for j in range(4):
                    p.matmul(pb[2][:, 32 + j: 33 + j], srow_m[0:1, j * 128:(j + 1) * 128],
                             onesr[0:1, 0:1], start=True, stop=True)
                i = None
                for j in range(4):
                    i = p.matmul(pb[2][:, 40 + j: 41 + j], srow_r[0:1, j * 128:(j + 1) * 128],
                                 onesr[0:1, 0:1], start=True, stop=True)
                return i
            t_Axt = sp.tok()

            @emit("V", deps=[t_Axt], inc=sv)
            def _(v, v_i=v_i):
                v.tensor_copy(out=lgT[:, v_i * 4 * E:(v_i + 1) * 4 * E], in_=pb[2][:, 0:4 * E])
                v.tensor_copy(out=mcolT[:, v_i * 4:(v_i + 1) * 4], in_=pb[2][:, 32:36])
                return v.tensor_copy(out=rcolT[:, v_i * 4:(v_i + 1) * 4], in_=pb[2][:, 40:44])
            t_AxT = sv.tok()
            setread("pb2", t_AxT)

            for c in range(NCH):
                @emit("V", deps=[t_bcast, t_norm_prev], inc=sv)
                def _(v, c=c, v_i=v_i):
                    sl = slice(c * T + v_i * 512, c * T + v_i * 512 + 512)
                    sv.inc(v.tensor_tensor(out=xtmp[:], in0=resid[:, sl], in1=pb[4][:, 0:512], op=ALU.subtract), 1)
                    chain(v, sv)
                    return v.tensor_tensor(out=xn[:, sl], in0=xtmp[:], in1=pb[5][:, 0:512], op=ALU.mult)
                t_xn[c][v_i] = sv.tok()
                t_norm_prev = t_xn[c][v_i]

                @emit("V", deps=[t_xn[c][v_i], t_Ax], inc=sv)
                def _(v, c=c, v_i=v_i):
                    sl = slice(c * T + v_i * 512, c * T + v_i * 512 + 512)
                    return v.tensor_tensor(out=resid[:, sl], in0=resid[:, sl], in1=xn[:, sl], op=ALU.add)
                t_residln[c][v_i] = sv.tok()
            setread("pb4", t_norm_prev)
            setread("pb5", t_norm_prev)

        if "xn" in dbg:
            for c in range(NCH):
                @emit("G", deps=[t_xn[c][v] for v in range(NV)], inc=st, k=16)
                def _(g, c=c):
                    return g.dma_start(out=dbg["xn"][c * 128:(c + 1) * 128, :], in_=xn[:, c * T:(c + 1) * T])

    if upto >= 3:
        t_q = [None] * NV
        t_k = [None] * NV
        t_v = [None] * NV
        for v_i in range(NV):
            dep_xn = [t_xn[c][v_i] for c in range(NCH)]
            for pi, (wT, broff, dst, tlist) in enumerate(
                    ((wqT, 0, qf, t_q), (wkT, DS, kf, t_k), (wvT, 2 * DS, vf, t_v))[:int(os.environ.get("L0QKV", "3"))]):
                bk_ = "pb%d" % (pi % 2)

                @emit("P", deps=dep_xn + [t_evqk, t_evvo, war(bk_)], inc=sp)
                def _(p, v_i=v_i, wT=wT, broff=broff, pi=pi):
                    bank = pb[pi % 2]
                    i = None
                    for c in range(NCH):
                        i = p.matmul(bank[0:DS, :], wT[:, c * DS:(c + 1) * DS],
                                     xn[:, c * T + v_i * 512: c * T + v_i * 512 + 512],
                                     start=(c == 0), stop=(c == NCH - 1))
                    return i
                t_mm = sp.tok()

                @emit("V", deps=[t_mm, t_wload], inc=sv)
                def _(v, v_i=v_i, dst=dst, pi=pi):
                    return v.tensor_scalar(out=dst[:, v_i * 512:(v_i + 1) * 512], in0=pb[pi % 2][0:DS, :],
                                           scalar1=bqkvc[:, pi:pi + 1], scalar2=None, op0=ALU.add)
                tlist[v_i] = sv.tok()
                setread(bk_, tlist[v_i])

        t_kvT = [None] * NSC
        for ci in range(int(os.environ.get("L0KVT", str(NSC)))):
            v_i = ci // 8
            bk = pbf if ci % 2 == 0 else pbf2
            bkk = "pbf" if ci % 2 == 0 else "pbf2"

            @emit("P", deps=[t_k[v_i], t_v[v_i], war(bkk)], inc=sp)
            def _(p, ci=ci, bk=bk):
                p.transpose(bk[0:DS, 0:64], kf[:, ci * 64:(ci + 1) * 64], idb[0:DS, 0:DS])
                return p.transpose(bk[0:DS, 64:128], vf[:, ci * 64:(ci + 1) * 64], idb[0:DS, 0:DS])
            t_tr = sp.tok()

            @emit("V", deps=[t_tr], inc=sv)
            def _(v, ci=ci, bk=bk):
                v.tensor_copy(out=kT[:, ci * DS:(ci + 1) * DS], in_=bk[0:DS, 0:64])
                return v.tensor_copy(out=vT[:, ci * DS:(ci + 1) * DS], in_=bk[0:DS, 64:128])
            t_kvT[ci] = sv.tok()
            setread(bkk, t_kvT[ci])

    if upto >= 4:
        @emit("G", deps=[t_tables], inc=sg)
        def _(g):
            return g.memset(St[:], 0.0)
        t_St = sg.tok()

        t_Ata_prev = None
        t_ops_prev = None
        t_otok_prev = None
        t_otr_prev = None
        t_oev_prev = None
        t_G_prev = None
        t_of = [None] * NSC
        for ci in range(NSC):
            u = ci // 2
            half = ci % 2
            v_i = ci // 8
            q_sl = qf[:, ci * LSC:(ci + 1) * LSC]
            k_sl = kf[:, ci * LSC:(ci + 1) * LSC]
            kT_sl = kT[:, ci * DS:(ci + 1) * DS]
            vT_sl = vT[:, ci * DS:(ci + 1) * DS]

            @emit("P", deps=[t_q[v_i], t_k[v_i], t_Ata_prev or t_tables, war("pb0")], inc=sp)
            def _(p, q_sl=q_sl, k_sl=k_sl):
                return p.matmul(pb[0][0:LSC, 0:LSC], k_sl, q_sl, start=True, stop=True)
            t_A = sp.tok()

            @emit("V", deps=[t_A, t_ops_prev], inc=sv)
            def _(v):
                return v.tensor_tensor(out=Ata[:], in0=pb[0][0:LSC, 0:LSC], in1=mUL[:], op=ALU.mult)
            t_Ata = sv.tok()
            setread("pb0", t_Ata)

            @emit("V", deps=[t_kvT[ci], t_St, t_ops_prev, t_G_prev, t_tables], inc=sv)
            def _(v, vT_sl=vT_sl):
                v.tensor_tensor(out=uta[:], in0=vT_sl, in1=dpm[:], op=ALU.mult)
                v.tensor_tensor(out=utb[:], in0=vT_sl, in1=dpr[:], op=ALU.mult)
                return v.tensor_tensor(out=St_b[:], in0=St[:], in1=drow[:], op=ALU.mult)
            t_u = sv.tok()

            @emit("P", deps=[t_Ata, t_u, t_otok_prev, war("pb1")], inc=sp)
            def _(p, q_sl=q_sl):
                p.matmul(pb[1][0:LSC, 0:DS], Ata[:], uta[:], start=True, stop=False)
                return p.matmul(pb[1][0:LSC, 0:DS], q_sl, St_b[:], start=False, stop=True)
            t_ops = sp.tok()

            @emit("V", deps=[t_ops, t_otr_prev], inc=sv)
            def _(v):
                return v.tensor_tensor(out=otok[:], in0=pb[1][0:LSC, 0:DS], in1=dpp[:], op=ALU.mult)
            t_otok = sv.tok()
            setread("pb1", t_otok)

            @emit("P", deps=[t_otok, t_oev_prev, war("pbf2")], inc=sp)
            def _(p):
                return p.transpose(pbf2[0:DS, 896:960], otok[:], idb[0:LSC, 0:LSC])
            t_otr = sp.tok()

            @emit("A", deps=[t_otr], inc=sa)
            def _(a, ci=ci):
                return a.copy(out=of[:, ci * LSC:(ci + 1) * LSC], in_=pbf2[0:DS, 896:960])
            t_oev = sa.tok()
            setread("pbf2", t_oev)
            t_of[ci] = t_oev

            @emit("P", deps=[t_u, t_kvT[ci], t_St, war("pb2")], inc=sp)
            def _(p, kT_sl=kT_sl):
                return p.matmul(pb[2][0:DS, 0:DS], kT_sl, utb[:], start=True, stop=True)
            t_G = sp.tok()
            t_G_prev = t_G

            @emit("V", deps=[t_G, t_u], inc=sv)
            def _(v):
                return v.tensor_tensor(out=St[:], in0=St[:], in1=dLrow[:], op=ALU.mult)
            t_sd1 = sv.tok()

            @emit("V", deps=[t_sd1], inc=sv)
            def _(v):
                return v.tensor_tensor(out=St[:], in0=St[:], in1=pb[2][0:DS, 0:DS], op=ALU.add)
            t_St = sv.tok()
            setread("pb2", t_St)

            t_Ata_prev, t_ops_prev, t_otok_prev = t_Ata, t_ops, t_otok
            t_otr_prev, t_oev_prev = t_otr, t_oev

        if "q" in dbg:
            t_dprev = None
            for nm, src in (("scan", of), ("q", qf), ("k", kf), ("v", vf)):
                for v_i in range(NV):
                    dep = [t_of[NSC - 1]] if nm == "scan" else [t_q[v_i], t_k[v_i], t_v[v_i]]

                    @emit("V", deps=dep + [t_dprev], inc=sv)
                    def _(v, src=src, v_i=v_i):
                        return v.tensor_copy(out=dbgstage[:], in_=src[:, v_i * 512:(v_i + 1) * 512])
                    t_c = sv.tok()

                    @emit("S", deps=[t_c], inc=st, k=16)
                    def _(s, nm=nm, v_i=v_i):
                        return s.dma_start(out=dbg[nm][:, v_i * 512:(v_i + 1) * 512], in_=dbgstage[:])
                    t_dprev = st.tok()

    if upto >= 5:
        t_op = [[None] * NV for _ in range(NCH)]
        for v_i in range(NV):
            for c in range(NCH):
                bk_ = "pb%d" % (c % 2)

                @emit("P", deps=[t_of[8 * v_i + 7], t_evvo, war(bk_)], inc=sp)
                def _(p, c=c, v_i=v_i):
                    bank = pb[c % 2]
                    p.matmul(bank[:, :], woT[:, c * 128:(c + 1) * 128], of[:, v_i * 512:(v_i + 1) * 512],
                             start=True, stop=False)
                    return p.matmul(bank[:, :], brow[0:1, 3 * DS + c * 128: 3 * DS + (c + 1) * 128],
                                    onesrb[:], start=False, stop=True)
                t_mm = sp.tok()

                @emit("V", deps=[t_mm, t_residln[c][v_i]], inc=sv)
                def _(v, c=c, v_i=v_i):
                    sl = slice(c * T + v_i * 512, c * T + v_i * 512 + 512)
                    return v.tensor_tensor(out=resid[:, sl], in0=resid[:, sl], in1=pb[c % 2][:, :], op=ALU.add)
                t_op[c][v_i] = sv.tok()
                setread(bk_, t_op[c][v_i])

    if upto >= 6:
        @emit("P", deps=[t_wload, t_ones, war("pb3")], inc=sp)
        def _(p):
            i = None
            for c in range(NCH):
                i = p.matmul(pb[3][0:E, 0:1], wr_f[:, c * E:(c + 1) * E], onescolf[:],
                             start=(c == 0), stop=(c == NCH - 1))
            return i
        t_cs1 = sp.tok()

        @emit("V", deps=[t_cs1], inc=sv)
        def _(v):
            return v.tensor_copy(out=lgs[0:E, 0:1], in_=pb[3][0:E, 0:1])
        t_cs2 = sv.tok()
        setread("pb3", t_cs2)

        @emit("P", deps=[t_cs2, war("pb3")], inc=sp)
        def _(p):
            return p.transpose(pb[3][0:1, 0:E], lgs[0:E, 0:1], idf[0:E, 0:E])
        t_cstr = sp.tok()

        @emit("V", deps=[t_cstr], inc=sv)
        def _(v):
            return v.tensor_copy(out=cntrow[:], in_=pb[3][0:1, 0:E])
        t_cs3 = sv.tok()
        setread("pb3", t_cs3)

        @emit("P", deps=[t_cs3, war("pb2")], inc=sp)
        def _(p):
            return p.matmul(pb[2][:, 0:E], onesc[:], cntrow[:], start=True, stop=True)
        t_cs4 = sp.tok()

        @emit("V", deps=[t_cs4], inc=sv)
        def _(v):
            return v.tensor_copy(out=csb[:], in_=pb[2][:, 0:E])
        t_csb = sv.tok()
        setread("pb2", t_csb)

        t_lgT = [None] * NU
        t_lgfix_prev = None
        for u in range(NU):
            v_i = u // 4

            @emit("V", deps=[t_AxT, t_csb, t_lgfix_prev], inc=sv)
            def _(v, u=u):
                sv.inc(v.tensor_scalar(out=rtmp[:], in0=csb[:], scalar1=mcolT[:, u:u + 1],
                                       scalar2=None, op0=ALU.mult), 1)
                chain(v, sv)
                sv.inc(v.tensor_tensor(out=lgT[:, u * E:(u + 1) * E], in0=lgT[:, u * E:(u + 1) * E],
                                       in1=rtmp[:], op=ALU.subtract), 1)
                chain(v, sv)
                return v.tensor_scalar(out=lgT[:, u * E:(u + 1) * E], in0=lgT[:, u * E:(u + 1) * E],
                                       scalar1=rcolT[:, u:u + 1], scalar2=None, op0=ALU.mult)
            t_lgT[u] = sv.tok()
            t_lgfix_prev = t_lgT[u]

        t_route = [None] * NU
        t_ru_prev = None
        for u in range(NU):
            @emit("V", deps=[t_lgT[u], t_ru_prev], inc=sv)
            def _(v, u=u):
                sl = lgT[:, u * E:(u + 1) * E]
                return v.tensor_reduce(out=rc["nmax"][:], in_=sl, op=ALU.max, axis=AX.X, negate=True)
            t1 = sv.tok()

            @emit("A", deps=[t1], inc=sa)
            def _(a, u=u):
                return a.activation(out=qexp[:], in_=lgT[:, u * E:(u + 1) * E], func=AF.Exp, bias=rc["nmax"][:])
            t2 = sa.tok()

            @emit("V", deps=[t2], inc=sv)
            def _(v, u=u):
                sv.inc(v.tensor_reduce(out=rc["m1"][:], in_=qexp[:], op=ALU.max, axis=AX.X), 1)
                chain(v, sv)
                sv.inc(v.tensor_scalar(out=oh1f[:], in0=qexp[:], scalar1=rc["m1"][:], scalar2=None, op0=ALU.is_ge), 1)
                chain(v, sv)
                sv.inc(v.tensor_scalar(out=rtmp[:], in0=oh1f[:], scalar1=1e30, scalar2=None, op0=ALU.mult), 1)
                chain(v, sv)
                sv.inc(v.tensor_tensor(out=rtmp[:], in0=qexp[:], in1=rtmp[:], op=ALU.subtract), 1)
                chain(v, sv)
                sv.inc(v.tensor_reduce(out=rc["m2"][:], in_=rtmp[:], op=ALU.max, axis=AX.X), 1)
                chain(v, sv)
                sv.inc(v.tensor_scalar(out=oh2f[:], in0=rtmp[:], scalar1=rc["m2"][:], scalar2=None, op0=ALU.is_ge), 1)
                chain(v, sv)
                sv.inc(v.tensor_tensor(out=rc["sum"][:], in0=rc["m1"][:], in1=rc["m2"][:], op=ALU.add), 1)
                chain(v, sv)
                sv.inc(v.reciprocal(out=rc["rec"][:], in_=rc["sum"][:]), 1)
                chain(v, sv)
                sv.inc(v.tensor_tensor(out=rc["t1"][:], in0=rc["m1"][:], in1=rc["rec"][:], op=ALU.mult), 1)
                chain(v, sv)
                sv.inc(v.tensor_tensor(out=rc["t2"][:], in0=rc["m2"][:], in1=rc["rec"][:], op=ALU.mult), 1)
                chain(v, sv)
                sv.inc(v.tensor_tensor(out=rtmp2[:], in0=oh1f[:], in1=ecbS[:], op=ALU.mult), 1)
                chain(v, sv)
                sv.inc(v.tensor_reduce(out=rc["e1"][:], in_=rtmp2[:], op=ALU.add, axis=AX.X), 1)
                chain(v, sv)
                sv.inc(v.tensor_tensor(out=rtmp2[:], in0=oh2f[:], in1=ecbS[:], op=ALU.mult), 1)
                chain(v, sv)
                sv.inc(v.tensor_reduce(out=rc["e2"][:], in_=rtmp2[:], op=ALU.add, axis=AX.X), 1)
                chain(v, sv)
                sv.inc(v.tensor_copy(out=oh1_all[:, u * E:(u + 1) * E], in_=oh1f[:]), 1)
                chain(v, sv)
                sv.inc(v.tensor_copy(out=oh2_all[:, u * E:(u + 1) * E], in_=oh2f[:]), 1)
                chain(v, sv)
                sv.inc(v.tensor_tensor(out=rtmp[:], in0=oh1f[:], in1=oh2f[:], op=ALU.add), 1)
                chain(v, sv)
                sv.inc(v.tensor_copy(out=ohs_all[:, u * E:(u + 1) * E], in_=rtmp[:]), 1)
                chain(v, sv)
                v.tensor_copy(out=g1c[:, u:u + 1], in_=rc["t1"][:])
                v.tensor_copy(out=e1c[:, u:u + 1], in_=rc["e1"][:])
                v.tensor_copy(out=e2c[:, u:u + 1], in_=rc["e2"][:])
                return v.tensor_copy(out=g2c[:, u:u + 1], in_=rc["t2"][:])
            t_route[u] = sv.tok()
            t_ru_prev = t_route[u]

        @emit("P", deps=[t_route[u] for u in range(NU)] + [war("pb3")], inc=sp)
        def _(p):
            i = None
            for u in range(NU):
                i = p.matmul(pb[3][0:1, 0:E], onescol[:], ohs_all[:, u * E:(u + 1) * E],
                             start=(u == 0), stop=(u == NU - 1))
            return i
        t_cnt = sp.tok()

        @emit("V", deps=[t_cnt, t_cs4], inc=sv)
        def _(v):
            return v.tensor_copy(out=cntrow[:], in_=pb[3][0:1, 0:E])
        t_cntrow = sv.tok()
        setread("pb3", t_cntrow)

        @emit("S", deps=[t_cntrow], inc=scc, k=16)
        def _(s):
            return s.dma_start(out=cc_in.ap(), in_=cntrow[:])
        t_ccin = scc.tok()

        @emit("G", deps=[t_ccin], inc=sg)
        def _(g):
            return g.collective_compute("AllGather", ALU.bypass,
                                        replica_groups=[[0, 1, 2, 3, 4, 5, 6, 7]],
                                        ins=[cc_in.ap().opt()], outs=[cc_out.ap().opt()])
        t_ag = sg.tok()

        @emit("G", deps=[t_ag], inc=sgd, k=16)
        def _(g):
            return g.dma_start(out=agrows[:], in_=cc_out.ap())
        t_agl = sgd.tok()

        @emit("P", deps=[t_agl, t_wload, war("pb3")], inc=sp)
        def _(p):
            return p.matmul(pb[3][0:1, 0:E], cmask[:], agrows[:], start=True, stop=True)
        t_coffmm = sp.tok()

        @emit("V", deps=[t_coffmm], inc=sv)
        def _(v):
            return v.tensor_copy(out=coffs[:], in_=pb[3][0:1, 0:E])
        t_coffs = sv.tok()
        setread("pb3", t_coffs)

        @emit("P", deps=[t_coffs, war("pb3")], inc=sp)
        def _(p):
            return p.matmul(pb[3][:, 0:E], onesc[:], coffs[:], start=True, stop=True)
        t_cb = sp.tok()

        @emit("V", deps=[t_cb], inc=sv)
        def _(v):
            return v.tensor_copy(out=coffb[:], in_=pb[3][:, 0:E])
        t_coffb = sv.tok()
        setread("pb3", t_coffb)

        t_scat = [None] * NU
        t_idx_prev = None
        t_xtok_prev = [None, None]
        for u in range(NU):
            @emit("P", deps=[t_route[u], t_idx_prev, war("pb3")], inc=sp)
            def _(p, u=u):
                p.matmul(pb[3][:, 0:E], trilS[:], ohs_all[:, u * E:(u + 1) * E], start=True, stop=False)
                p.matmul(pb[3][:, 0:E], onesc[:], offs[:], start=False, stop=True)
                return p.matmul(pb[3][0:1, E:2 * E], onescol[:], ohs_all[:, u * E:(u + 1) * E],
                                start=True, stop=True)
            t_cum = sp.tok()

            @emit("V", deps=[t_cum], inc=sv)
            def _(v, u=u):
                v.tensor_copy(out=cum_u[:], in_=pb[3][:, 0:E])
                return v.tensor_tensor(out=offs[:], in0=offs[:], in1=pb[3][0:1, E:2 * E], op=ALU.add)
            t_cumev = sv.tok()
            setread("pb3", t_cumev)

            @emit("V", deps=[t_cumev, t_coffb], inc=sv)
            def _(v, u=u):
                sv.inc(v.tensor_tensor(out=rtmp[:], in0=oh1_all[:, u * E:(u + 1) * E], in1=cum_u[:], op=ALU.mult), 1)
                chain(v, sv)
                sv.inc(v.tensor_reduce(out=rc["lp1"][:], in_=rtmp[:], op=ALU.add, axis=AX.X), 1)
                chain(v, sv)
                sv.inc(v.tensor_tensor(out=rtmp[:], in0=oh2_all[:, u * E:(u + 1) * E], in1=cum_u[:], op=ALU.mult), 1)
                chain(v, sv)
                sv.inc(v.tensor_reduce(out=rc["lp2"][:], in_=rtmp[:], op=ALU.add, axis=AX.X), 1)
                chain(v, sv)
                sv.inc(v.tensor_tensor(out=rtmp[:], in0=oh1_all[:, u * E:(u + 1) * E], in1=coffb[:], op=ALU.mult), 1)
                chain(v, sv)
                sv.inc(v.tensor_reduce(out=rc["gp1"][:], in_=rtmp[:], op=ALU.add, axis=AX.X), 1)
                chain(v, sv)
                sv.inc(v.tensor_tensor(out=rc["gp1"][:], in0=rc["gp1"][:], in1=rc["lp1"][:], op=ALU.add), 1)
                chain(v, sv)
                sv.inc(v.tensor_tensor(out=rtmp[:], in0=oh2_all[:, u * E:(u + 1) * E], in1=coffb[:], op=ALU.mult), 1)
                chain(v, sv)
                sv.inc(v.tensor_reduce(out=rc["gp2"][:], in_=rtmp[:], op=ALU.add, axis=AX.X), 1)
                chain(v, sv)
                sv.inc(v.tensor_tensor(out=rc["gp2"][:], in0=rc["gp2"][:], in1=rc["lp2"][:], op=ALU.add), 1)
                chain(v, sv)
                sv.inc(v.tensor_scalar(out=rc["k1"][:], in0=rc["gp1"][:], scalar1=float(CAP), scalar2=None, op0=ALU.is_lt), 1)
                chain(v, sv)
                sv.inc(v.tensor_scalar(out=rc["k2"][:], in0=rc["gp2"][:], scalar1=float(CAP), scalar2=None, op0=ALU.is_lt), 1)
                chain(v, sv)
                sv.inc(v.tensor_scalar(out=rc["ob1"][:], in0=rc["lp1"][:], scalar1=float(CLOC), scalar2=None, op0=ALU.is_lt), 1)
                chain(v, sv)
                sv.inc(v.tensor_scalar(out=rc["ob2"][:], in0=rc["lp2"][:], scalar1=float(CLOC), scalar2=None, op0=ALU.is_lt), 1)
                chain(v, sv)
                sv.inc(v.tensor_tensor(out=rc["k1"][:], in0=rc["k1"][:], in1=rc["ob1"][:], op=ALU.mult), 1)
                chain(v, sv)
                sv.inc(v.tensor_tensor(out=rc["k2"][:], in0=rc["k2"][:], in1=rc["ob2"][:], op=ALU.mult), 1)
                chain(v, sv)
                sv.inc(v.tensor_tensor(out=g1c[:, u:u + 1], in0=g1c[:, u:u + 1], in1=rc["k1"][:], op=ALU.mult), 1)
                chain(v, sv)
                sv.inc(v.tensor_tensor(out=g2c[:, u:u + 1], in0=g2c[:, u:u + 1], in1=rc["k2"][:], op=ALU.mult), 1)
                chain(v, sv)
                sv.inc(v.tensor_scalar(out=rc["if1"][:], in0=e1c[:, u:u + 1], scalar1=float(CLOC), scalar2=None, op0=ALU.mult), 1)
                chain(v, sv)
                sv.inc(v.tensor_tensor(out=rc["if1"][:], in0=rc["if1"][:], in1=rc["lp1"][:], op=ALU.add), 1)
                chain(v, sv)
                sv.inc(v.tensor_scalar(out=rc["t1"][:], in0=rc["ob1"][:], scalar1=-1e9, scalar2=1e9, op0=ALU.mult, op1=ALU.add), 1)
                chain(v, sv)
                sv.inc(v.tensor_tensor(out=rc["if1"][:], in0=rc["if1"][:], in1=rc["t1"][:], op=ALU.add), 1)
                chain(v, sv)
                sv.inc(v.tensor_scalar(out=rc["if2"][:], in0=e2c[:, u:u + 1], scalar1=float(CLOC), scalar2=None, op0=ALU.mult), 1)
                chain(v, sv)
                sv.inc(v.tensor_tensor(out=rc["if2"][:], in0=rc["if2"][:], in1=rc["lp2"][:], op=ALU.add), 1)
                chain(v, sv)
                sv.inc(v.tensor_scalar(out=rc["t2"][:], in0=rc["ob2"][:], scalar1=-1e9, scalar2=1e9, op0=ALU.mult, op1=ALU.add), 1)
                chain(v, sv)
                sv.inc(v.tensor_tensor(out=rc["if2"][:], in0=rc["if2"][:], in1=rc["t2"][:], op=ALU.add), 1)
                chain(v, sv)
                sv.inc(v.tensor_copy(out=idx1[:, u:u + 1], in_=rc["if1"][:]), 1)
                chain(v, sv)
                return v.tensor_copy(out=idx2[:, u:u + 1], in_=rc["if2"][:])
            t_idx = sv.tok()
            t_idx_prev = t_idx

            if "e1" in dbg:
                @emit("V", deps=[t_idx], inc=sv)
                def _(v, u=u):
                    v.tensor_copy(out=dbg_cols["e1"][:, u:u + 1], in_=e1c[:, u:u + 1])
                    v.tensor_copy(out=dbg_cols["e2"][:, u:u + 1], in_=e2c[:, u:u + 1])
                    v.tensor_copy(out=dbg_cols["lp1"][:, u:u + 1], in_=rc["lp1"][:])
                    v.tensor_copy(out=dbg_cols["lp2"][:, u:u + 1], in_=rc["lp2"][:])
                    v.tensor_copy(out=dbg_cols["g1"][:, u:u + 1], in_=rc["gp1"][:])
                    return v.tensor_copy(out=dbg_cols["g2"][:, u:u + 1], in_=rc["gp2"][:])
                t_idx = sv.tok()
                t_idx_prev = t_idx

            pp = u % 2

            @emit("P", deps=[t_idx] + [t_xn[c][u // 4] for c in range(NCH)] +
                           [t_xtok_prev[pp], war("pbf")], inc=sp)
            def _(p, u=u):
                i = None
                for c in range(NCH):
                    i = p.transpose(pbf[:, c * 128:(c + 1) * 128],
                                    xn[:, c * T + u * 128: c * T + (u + 1) * 128], idb[:])
                return i
            t_xtr2 = sp.tok()

            @emit("V", deps=[t_xtr2], inc=sv)
            def _(v, pp=pp):
                return v.tensor_copy(out=xtok[pp][:], in_=pbf[:])
            t_xtokc = sv.tok()
            setread("pbf", t_xtokc)

            @emit("G", deps=[t_xtokc, t_scat[u - 1] if u > 0 else None])
            def _(g, u=u, pp=pp):
                with g.register("bcs%d" % u) as r:
                    g.reg_mov(r, E * CLOC - 1)
                    sgd.inc(g.indirect_dma_start(
                        out=buckets.ap(), out_offset=bass.IndirectOffsetOnAxis(ap=idx1[:, u:u + 1], axis=0),
                        in_=xtok[pp][:], in_offset=None,
                        bounds_check=r, oob_is_err=False), 16)
                    chain(g, sgd)
                    sgd.inc(g.indirect_dma_start(
                        out=buckets.ap(), out_offset=bass.IndirectOffsetOnAxis(ap=idx2[:, u:u + 1], axis=0),
                        in_=xtok[pp][:], in_offset=None,
                        bounds_check=r, oob_is_err=False), 16)
                return None
            t_scat[u] = sgd.tok()
            t_xtok_prev[pp] = t_scat[u]

        if "e1" in dbg:
            for nm in ("e1", "e2", "g1", "g2", "lp1", "lp2"):
                @emit("S", deps=[t_idx_prev], inc=st, k=16)
                def _(s, nm=nm):
                    return s.dma_start(out=dbg[nm], in_=dbg_cols[nm][:])
            @emit("G", deps=[t_idx_prev])
            def _(g):
                sgd.inc(g.dma_start(out=dbg["id1"], in_=idx1[:]), 16)
                sgd.inc(g.dma_start(out=dbg["id2"], in_=idx2[:]), 16)
                return None

    esA.close()

    toks = [s_.tok() for s_ in ALLSEMS]
    for engname in ENG:
        @emit(engname, deps=toks)
        def _(e):
            return None

    if upto >= 7:
        esB = ExitStack()

        def sbB(name, shape, dtp):
            return esB.enter_context(nc.sbuf_tensor(name, shape, dtp))

        ein = sbB("ein", [128, NCH * CLOC], BF16)
        hbuf = sbB("hbuf", [128, NH0 * CLOC], BF16)
        eoutF = sbB("eoutF", [128, NCH * CLOC], BF16)
        gtok = [sbB("gtok0", [128, D], BF16), sbB("gtok1", [128, D], BF16)]
        w1c = [sbB("w1c0", [128, 1024], F32), sbB("w1c1", [128, 1024], F32)]
        w1cb = [sbB("w1cb0", [128, 1024], BF16), sbB("w1cb1", [128, 1024], BF16)]
        w2c = [sbB("w2c0", [128, 2048], F32), sbB("w2c1", [128, 2048], F32)]
        w2cb = [sbB("w2cb0", [128, 2048], BF16), sbB("w2cb1", [128, 2048], BF16)]
        b1r = sbB("b1r", [1, H0], BF16)
        b2r = sbB("b2r", [1, D], BF16)
        b1c = sbB("b1c", [128, NH0], F32)
        b2c = sbB("b2c", [128, NCH], F32)
        eout_t = [sbB("eout_t0", [128, D], BF16), sbB("eout_t1", [128, D], BF16)]
        gat1 = sbB("gat1", [128, D], BF16)
        gat2 = sbB("gat2", [128, D], BF16)
        ycomb = sbB("ycomb", [128, D], F32)
        yc2 = sbB("yc2", [128, D], F32)

        NTT = [(0, 512), (512, 128)]

        @emit("G", inc=sg)
        def _(g):
            g.memset(gat1[:], 0.0)
            return g.memset(gat2[:], 0.0)
        t_gz = sg.tok()

        t_scat_all = t_scat[NU - 1]
        t_wprev = {}
        t_bias_reader = None
        t_ein_reader = None
        t_h_reader = None
        t_eoutF_reader = None
        t_store_prev = None
        t_estore = [None] * E
        for e in range(E):
            t_eintr = []
            for j in range(5):
                pp = j % 2

                @emit("S", deps=[t_scat_all, t_wprev.get(("gt", pp))], inc=sgt[pp], k=16)
                def _(s, e=e, j=j, pp=pp):
                    return s.dma_start(out=gtok[pp][:], in_=buckets.ap()[e * CLOC + j * 128: e * CLOC + (j + 1) * 128, :])
                t_gl = sgt[pp].tok()

                @emit("P", deps=[t_gl, war("pbf"), t_ein_reader], inc=sp)
                def _(p, pp=pp):
                    i = None
                    for c in range(NCH):
                        i = p.transpose(pbf[:, c * 128:(c + 1) * 128], gtok[pp][:, c * 128:(c + 1) * 128], idb[:])
                    return i
                t_tr = sp.tok()
                t_wprev[("gt", pp)] = t_tr

                @emit("V", deps=[t_tr, t_ein_reader], inc=sv)
                def _(v, j=j):
                    ev = ein[:].rearrange("p (c t) -> p c t", c=NCH)[:, :, j * 128:(j + 1) * 128]
                    return v.tensor_copy(out=ev[:, 0:NCH, :], in_=pbf[:].rearrange("p (c t) -> p c t", c=NCH))
                tk2 = sv.tok()
                setread("pbf", tk2)
                t_eintr.append(tk2)

            @emit("G", deps=[t_bias_reader])
            def _(g, e=e):
                sgd.inc(g.dma_start(out=b1r[:], in_=b1[e:e + 1, :]), 16)
                sgd.inc(g.dma_start(out=b2r[:], in_=b2[e:e + 1, :]), 16)
                return None
            t_bc0 = sgd.tok()

            @emit("P", deps=[t_bc0, war("pb4")], inc=sp)
            def _(p, e=e):
                for hc in range(NH0):
                    p.matmul(pb[4][:, hc:hc + 1], b1r[0:1, hc * 128:(hc + 1) * 128],
                             onesrb[0:1, 0:1], start=True, stop=True)
                i = None
                for dc in range(NCH):
                    i = p.matmul(pb[4][:, NH0 + dc:NH0 + dc + 1], b2r[0:1, dc * 128:(dc + 1) * 128],
                                 onesrb[0:1, 0:1], start=True, stop=True)
                return i
            t_bcm = sp.tok()

            @emit("V", deps=[t_bcm], inc=sv)
            def _(v):
                sv.inc(v.tensor_copy(out=b1c[:], in_=pb[4][:, 0:NH0]), 1)
                chain(v, sv)
                return v.tensor_copy(out=b2c[:], in_=pb[4][:, NH0:NH0 + NCH])
            t_bc = sv.tok()
            setread("pb4", t_bc)

            t_hev = [[None, None] for _ in range(NH0)]
            t_hmm_last = None
            for hc in range(NH0):
                pp = hc % 2

                @emit("S", deps=[t_wprev.get(("w1", pp))], inc=sw1[pp], k=16)
                def _(s, e=e, hc=hc, pp=pp):
                    src = w1[e:e + 1, :, hc * 128:(hc + 1) * 128].rearrange("o (c p) h -> p o c h", p=128)
                    return s.dma_start(out=w1c[pp][:].rearrange("p (o c h) -> p o c h", o=1, c=NCH), in_=src)
                t_wl = sw1[pp].tok()

                @emit("V", deps=[t_wl, t_wprev.get(("w1b", pp))], inc=sv)
                def _(v, pp=pp):
                    return v.tensor_copy(out=w1cb[pp][:], in_=w1c[pp][:])
                t_wb = sv.tok()
                t_wprev[("w1", pp)] = t_wb

                for ti, (toff, tlen) in enumerate(NTT):
                    bi = (hc % 2) * 2 + ti
                    bk_ = "pb%d" % bi

                    @emit("P", deps=t_eintr + [t_wb, t_bc, war(bk_)], inc=sp)
                    def _(p, hc=hc, pp=pp, toff=toff, tlen=tlen, bi=bi):
                        bank = pb[bi]
                        i = None
                        for c in range(NCH):
                            i = p.matmul(bank[:, 0:tlen], w1cb[pp][:, c * 128:(c + 1) * 128],
                                         ein[:, c * CLOC + toff: c * CLOC + toff + tlen],
                                         start=(c == 0), stop=(c == NCH - 1))
                        return i
                    t_hmm = sp.tok()
                    t_wprev[("w1b", pp)] = t_hmm
                    t_hmm_last = t_hmm

                    @emit("V", deps=[t_hmm, t_h_reader], inc=sv)
                    def _(v, hc=hc, toff=toff, tlen=tlen, bi=bi):
                        return v.tensor_scalar(out=hbuf[:, hc * CLOC + toff: hc * CLOC + toff + tlen],
                                               in0=pb[bi][:, 0:tlen], scalar1=b1c[:, hc:hc + 1],
                                               scalar2=0.0, op0=ALU.add, op1=ALU.max)
                    t_hev[hc][ti] = sv.tok()
                    setread(bk_, t_hev[hc][ti])
            t_ein_reader = t_hmm_last

            t_omm_last = None
            t_oev = [[None, None] for _ in range(NCH)]
            for dc in range(NCH):
                t_whalf = []
                for half in range(2):
                    pp = half

                    @emit("S", deps=[t_wprev.get(("w2", pp))], inc=sw2[pp], k=16)
                    def _(s, e=e, dc=dc, half=half, pp=pp):
                        src = w2[e:e + 1, half * 2048:(half + 1) * 2048, dc * 128:(dc + 1) * 128] \
                            .rearrange("o (c p) d -> p o c d", p=128)
                        return s.dma_start(out=w2c[pp][:].rearrange("p (o c d) -> p o c d", o=1, c=16), in_=src)
                    t_w2l = sw2[pp].tok()

                    @emit("V", deps=[t_w2l, t_wprev.get(("w2b", pp))], inc=sv)
                    def _(v, pp=pp):
                        return v.tensor_copy(out=w2cb[pp][:], in_=w2c[pp][:])
                    t_w2b = sv.tok()
                    t_wprev[("w2", pp)] = t_w2b
                    t_whalf.append(t_w2b)

                for ti, (toff, tlen) in enumerate(NTT):
                    bk_ = "pb%d" % (2 + ti % 2)
                    hdeps = [t_hev[hc][ti] for hc in range(NH0)]

                    @emit("P", deps=hdeps + t_whalf + [t_bc, war(bk_)], inc=sp)
                    def _(p, dc=dc, toff=toff, tlen=tlen, ti=ti):
                        bank = pb[2 + ti % 2]
                        i = None
                        for hc in range(NH0):
                            i = p.matmul(bank[:, 0:tlen], w2cb[hc // 16][:, (hc % 16) * 128:(hc % 16 + 1) * 128],
                                         hbuf[:, hc * CLOC + toff: hc * CLOC + toff + tlen],
                                         start=(hc == 0), stop=(hc == NH0 - 1))
                        return i
                    t_omm = sp.tok()
                    t_wprev[("w2b", 0)] = t_omm
                    t_wprev[("w2b", 1)] = t_omm
                    t_omm_last = t_omm

                    @emit("V", deps=[t_omm, t_eoutF_reader], inc=sv)
                    def _(v, dc=dc, toff=toff, tlen=tlen, ti=ti):
                        return v.tensor_scalar(out=eoutF[:, dc * CLOC + toff: dc * CLOC + toff + tlen],
                                               in0=pb[2 + ti % 2][:, 0:tlen],
                                               scalar1=b2c[:, dc:dc + 1], scalar2=None, op0=ALU.add)
                    t_oev[dc][ti] = sv.tok()
                    setread(bk_, t_oev[dc][ti])
            t_h_reader = t_omm_last
            t_bias_reader = t_omm_last

            t_str_tr_last = None
            for j in range(5):
                pp = j % 2
                ti = 0 if j < 4 else 1
                deps_ev = [t_oev[dc][0] for dc in range(NCH)] + [t_oev[dc][1] for dc in range(NCH)]

                @emit("P", deps=deps_ev + [war("pbf"), t_store_prev if pp == (t_store_prev is not None and pp) else t_store_prev], inc=sp)
                def _(p, j=j):
                    i = None
                    for c in range(NCH):
                        i = p.transpose(pbf[:, c * 128:(c + 1) * 128],
                                        eoutF[:, c * CLOC + j * 128: c * CLOC + (j + 1) * 128], idb[:])
                    return i
                t_str_tr = sp.tok()
                t_str_tr_last = t_str_tr

                @emit("V", deps=[t_str_tr, t_wprev.get(("eot", pp))], inc=sv)
                def _(v, pp=pp):
                    return v.tensor_copy(out=eout_t[pp][:], in_=pbf[:])
                t_eot = sv.tok()
                setread("pbf", t_eot)

                @emit("S", deps=[t_eot], inc=setm[pp], k=16)
                def _(s, e=e, j=j, pp=pp):
                    return s.dma_start(out=exp_out_d.ap()[e * CLOC + j * 128: e * CLOC + (j + 1) * 128, :],
                                       in_=eout_t[pp][:])
                t_wprev[("eot", pp)] = setm[pp].tok()
            t_eoutF_reader = t_str_tr_last
            t_estore[e] = t_wprev[("eot", (5 - 1) % 2)]
            t_store_prev = t_estore[e]

    if upto >= 8:
        t_comb_prev = None
        for u in range(NU):
            @emit("G", deps=[t_estore[ee] for ee in range(E)] + [t_gz, t_comb_prev])
            def _(g, u=u):
                with g.register("bcg%d" % u) as r:
                    g.reg_mov(r, E * CLOC - 1)
                    sg.inc(g.indirect_dma_start(
                        out=gat1[:], out_offset=None, in_=exp_out_d.ap(),
                        in_offset=bass.IndirectOffsetOnAxis(ap=idx1[:, u:u + 1], axis=0),
                        bounds_check=r, oob_is_err=False), 16)
                    sg.inc(g.indirect_dma_start(
                        out=gat2[:], out_offset=None, in_=exp_out_d.ap(),
                        in_offset=bass.IndirectOffsetOnAxis(ap=idx2[:, u:u + 1], axis=0),
                        bounds_check=r, oob_is_err=False), 16)
                return None
            t_gath = sg.tok()

            @emit("V", deps=[t_gath], inc=sv)
            def _(v, u=u):
                v.tensor_scalar(out=ycomb[:], in0=gat1[:], scalar1=g1c[:, u:u + 1], scalar2=None, op0=ALU.mult)
                sv.inc(v.tensor_scalar(out=yc2[:], in0=gat2[:], scalar1=g2c[:, u:u + 1], scalar2=None, op0=ALU.mult), 1)
                chain(v, sv)
                return v.tensor_tensor(out=ycomb[:], in0=ycomb[:], in1=yc2[:], op=ALU.add)
            t_yc = sv.tok()

            @emit("P", deps=[t_yc, war("pb4"), war("pb5")], inc=sp)
            def _(p, u=u):
                for c in range(4):
                    p.transpose(pb[4][:, c * 128:(c + 1) * 128], ycomb[:, c * 128:(c + 1) * 128], idf[:])
                i = None
                for c in range(4):
                    i = p.transpose(pb[5][:, c * 128:(c + 1) * 128], ycomb[:, (4 + c) * 128:(5 + c) * 128], idf[:])
                return i
            t_ytr = sp.tok()

            @emit("V", deps=[t_ytr, t_op[NCH - 1][u // 4]] + [t_op[c][u // 4] for c in range(NCH)], inc=sv)
            def _(v, u=u):
                rv = resid[:].rearrange("p (c t) -> p c t", c=NCH)[:, :, u * 128:(u + 1) * 128]
                v.tensor_tensor(out=rv[:, 0:4, :], in0=rv[:, 0:4, :],
                                in1=pb[4][:].rearrange("p (c t) -> p c t", c=4), op=ALU.add)
                return v.tensor_tensor(out=rv[:, 4:8, :], in0=rv[:, 4:8, :],
                                       in1=pb[5][:].rearrange("p (c t) -> p c t", c=4), op=ALU.add)
            t_comb = sv.tok()
            setread("pb4", t_comb)
            setread("pb5", t_comb)
            t_comb_prev = t_comb

        esB.close()


    toks = [s_.tok() for s_ in ALLSEMS]
    for engname in ENG:
        @emit(engname, deps=toks)
        def _(e):
            return None
    t_xload = sv.tok()
    dt = nc.dram_tensor

    import os
    NH, HD, WIN, G = 16, 64, 512, 64
    TH = T // 2
    pos_in = dt("pos", [8192, D], F32, kind="ExternalInput").ap()
    gm_in = dt("gm", [G, D], F32, kind="ExternalInput").ap()
    cw_in = dt("convw", [D, D, 4], F32, kind="ExternalInput").ap()
    cb_in = dt("convb", [D], F32, kind="ExternalInput").ap()
    gw_in = dt("gatew", [2 * D, D], F32, kind="ExternalInput").ap()
    gb_in = dt("gateb", [D], F32, kind="ExternalInput").ap()
    ow_in = dt("outw", [D, D], F32, kind="ExternalInput").ap()
    ob_in = dt("outb", [D], F32, kind="ExternalInput").ap()
    fgw = dt("fgw", [E, D], F32, kind="ExternalInput").ap()
    fgb = dt("fgb", [E], F32, kind="ExternalInput").ap()
    fw1 = dt("fw1", [E, D, D], F32, kind="ExternalInput").ap()
    fb1 = dt("fb1", [E, D], F32, kind="ExternalInput").ap()
    fw2 = dt("fw2", [E, D, D], F32, kind="ExternalInput").ap()
    fb2 = dt("fb2", [E, D], F32, kind="ExternalInput").ap()
    fw3 = dt("fw3", [E, D, D], F32, kind="ExternalInput").ap()
    fb3 = dt("fb3", [E, D], F32, kind="ExternalInput").ap()
    erows_in = dt("erows", [E, E * 128], F32, kind="ExternalInput").ap()
    x2_out = dt("x2", [T, D], F32, kind="ExternalOutput").ap()
    dbg = {}
    if "l1" in dbgkeys:
        for nm in ("xn", "xp", "local", "glob"):
            dbg[nm] = dt("qdbg_" + nm, [D, T], F32, kind="ExternalOutput").ap()


    xn1 = sb("qxn1", [128, NCH * T], BF16)
    ones2d = sb("qones2d", [128, 128], F32)
    erowS = sb("qerowS", [E, E * 128], BF16)
    gt_s = sb("qgt_s", [128, NCH * 128], BF16)
    gtT = sb("qgtT", [128, NH * HD], BF16)
    cwTb = sb("qcwTb", [E, T], BF16)
    b3S = sb("qb3S", [E, D], BF16)
    gbrow = sb("qgbrow", [1, D], BF16)
    obrow = sb("qobrow", [1, D], BF16)
    gbcol = sb("qgbcol", [128, NCH], F32)
    obcol = sb("qobcol", [128, NCH], F32)
    cbrow = sb("qcbrow", [1, D], BF16)
    fgbrow = sb("qfgbrow", [1, E], BF16)
    _xs0 = sb("qxstage0", [128, D], F32)
    xstage = [_xs0, _xs0]
    rc = {nm: sb("qrc_" + nm, [128, 1], F32) for nm in
          ("nmax", "sum", "rec", "m1", "m2", "t1", "t2")}
    gwT = sb("qgwT", [128, NCH * E], BF16)
    esG = ExitStack()

    def sbG(name, shape, dtp):
        return esG.enter_context(nc.sbuf_tensor(name, shape, dtp))

    xbp = [sbG("qxb0", [128, 512], BF16), sbG("qxb1", [128, 512], BF16)]
    xsqp = [sbG("qxsq0", [128, 512], BF16), sbG("qxsq1", [128, 512], BF16)]
    xtmp = sbG("qxtmp", [128, 512], F32)
    lgs = sbG("qlgs", [E, 512], F32)
    lgT = sbG("qlgT", [128, NU * E], F32)
    qexp = sbG("qqexp", [128, E], F32)
    oh1f = sbG("qoh1f", [128, E], F32)
    oh2f = sbG("qoh2f", [128, E], F32)
    cwmat = sbG("qcwmat", [128, E], F32)
    rtmp = sbG("qrtmp", [128, E], F32)


    swl = [Sem(es, nc, "qswla"), Sem(es, nc, "qswlb")]
    sgr = Sem(es, nc, "qsgr")
    scv = Sem(es, nc, "qscv")
    sdbg = Sem(es, nc, "qsdbg")
    ALLSEMS = [sd, sx[0], sx[1], swl[0], swl[1], sgd, sgr, scv, sdbg, sv, sp, sa, sg, st]



    @emit("S")
    def _(s):
        sd.inc(s.dma_start(out=idf[:], in_=identf), 16)
        sd.inc(s.dma_start(out=idb[:], in_=identb), 16)
        sd.inc(s.dma_start(out=xstage[0][0:E, 0:E * 128], in_=erows_in), 16)
        return None
    t_xload = sd.tok()

    @emit("G")
    def _(g):
        g.memset(onesc[:], 1.0)
        g.memset(onescb[:], 1.0)
        g.memset(onesrb[:], 1.0)
        g.memset(onescol[:], 1.0)
        sg.inc(g.memset(ones2d[:], 1.0), 1)
        sgr.inc(g.dma_start(out=gbrow[:], in_=gb_in[None, :]), 16)
        sgr.inc(g.dma_start(out=obrow[:], in_=ob_in[None, :]), 16)
        sgr.inc(g.dma_start(out=fgbrow[:], in_=fgb[None, :]), 16)
        sgr.inc(g.dma_start(out=b3S[:], in_=fb3), 16)
        sgr.inc(g.dma_start(out=cbrow[:], in_=cb_in[None, :]), 16)
        return None
    t_ones = sg.tok()
    t_rows = sgr.tok()

    @emit("V", deps=[t_xload], inc=sv)
    def _(v):
        return v.tensor_copy(out=erowS[:], in_=xstage[0][0:E, 0:E * 128])
    t_erows = sv.tok()

    @emit("P", deps=[t_rows, t_ones, war("pb4")], inc=sp)
    def _(p):
        for dc in range(NCH):
            p.matmul(pb[4][:, dc:dc + 1], gbrow[0:1, dc * 128:(dc + 1) * 128],
                     onesrb[0:1, 0:1], start=True, stop=True)
        i = None
        for dc in range(NCH):
            i = p.matmul(pb[4][:, NCH + dc:NCH + dc + 1], obrow[0:1, dc * 128:(dc + 1) * 128],
                         onesrb[0:1, 0:1], start=True, stop=True)
        return i
    t_gbm = sp.tok()

    @emit("V", deps=[t_gbm], inc=sv)
    def _(v):
        sv.inc(v.tensor_copy(out=gbcol[:], in_=pb[4][:, 0:NCH]), 1)
        chain(v, sv)
        return v.tensor_copy(out=obcol[:], in_=pb[4][:, NCH:2 * NCH])
    t_gbc = sv.tok()
    setread("pb4", t_gbc)

    t_xn = [[None] * NV for _ in range(NCH)]
    t_norm_prev = None
    for v_i in range(NV):
        t_mmprev = None
        t_cast = [None] * NCH
        for c in range(NCH):
            pp = c % 2

            @emit("V", deps=[t_xload, t_mmprev], inc=sv)
            def _(v, c=c, v_i=v_i, pp=pp):
                return v.tensor_copy(out=xbp[pp][:], in_=resid[:, c * T + v_i * 512: c * T + v_i * 512 + 512])
            t_cast[c] = sv.tok()

            @emit("V", deps=[t_cast[c]], inc=sv)
            def _(v, pp=pp):
                return v.tensor_tensor(out=xsqp[pp][:], in0=xbp[pp][:], in1=xbp[pp][:], op=ALU.mult)
            t_sq = sv.tok()

            @emit("P", deps=[t_cast[c], t_sq, t_ones, war("pb4"), war("pb5")], inc=sp)
            def _(p, c=c, pp=pp):
                p.matmul(pb[4][0:1, 0:512], onescol[:], xbp[pp][:], start=(c == 0), stop=(c == NCH - 1))
                return p.matmul(pb[5][0:1, 0:512], onescol[:], xsqp[pp][:], start=(c == 0), stop=(c == NCH - 1))
            t_mmprev = sp.tok()
        t_stats = t_mmprev

        @emit("V", deps=[t_stats], inc=sv)
        def _(v, v_i=v_i):
            v.tensor_scalar(out=srow_m[:], in0=pb[4][0:1, 0:512], scalar1=1.0 / D, scalar2=None, op0=ALU.mult)
            sv.inc(v.tensor_scalar(out=vrow2[:], in0=pb[5][0:1, 0:512], scalar1=1.0 / D, scalar2=None, op0=ALU.mult), 1)
            chain(v, sv)
            sv.inc(v.tensor_tensor(out=vrow[:], in0=srow_m[:], in1=srow_m[:], op=ALU.mult), 1)
            chain(v, sv)
            return v.tensor_tensor(out=vrow[:], in0=vrow2[:], in1=vrow[:], op=ALU.subtract)
        t_var = sv.tok()
        setread("pb4", t_var)
        setread("pb5", t_var)

        @emit("A", deps=[t_var], inc=sa)
        def _(a):
            return a.activation(out=vrow[:], in_=vrow[:], func=AF.Sqrt, bias=np.float32(1e-5))
        t_sqrt = sa.tok()

        @emit("V", deps=[t_sqrt], inc=sv)
        def _(v):
            return v.reciprocal(out=srow_r[:], in_=vrow[:])
        t_rstd = sv.tok()

        @emit("P", deps=[t_rstd, t_ones, war("pb4"), war("pb5")], inc=sp)
        def _(p):
            p.matmul(pb[4][:, 0:512], onesc[:], srow_m[:], start=True, stop=True)
            return p.matmul(pb[5][:, 0:512], onesc[:], srow_r[:], start=True, stop=True)
        t_bcast = sp.tok()

        for c in range(NCH):
            @emit("V", deps=[t_bcast, t_norm_prev], inc=sv)
            def _(v, c=c, v_i=v_i):
                sl = slice(c * T + v_i * 512, c * T + v_i * 512 + 512)
                sv.inc(v.tensor_tensor(out=xtmp[:], in0=resid[:, sl], in1=pb[4][:, 0:512], op=ALU.subtract), 1)
                chain(v, sv)
                return v.tensor_tensor(out=xn1[:, sl], in0=xtmp[:], in1=pb[5][:, 0:512], op=ALU.mult)
            t_xn[c][v_i] = sv.tok()
            t_norm_prev = t_xn[c][v_i]
        setread("pb4", t_norm_prev)
        setread("pb5", t_norm_prev)

    if "xn" in dbg:
        for c in range(NCH):
            @emit("G", deps=[t_xn[c][v] for v in range(NV)], inc=sdbg, k=16)
            def _(g, c=c):
                return g.dma_start(out=dbg["xn"][c * 128:(c + 1) * 128, :], in_=xn1[:, c * T:(c + 1) * T])

    @emit("S", deps=[t_erows], inc=sx[0], k=16)
    def _(s):
        return s.dma_start(out=xstage[0][0:E, 0:D], in_=fgw)
    t_fgw = sx[0].tok()

    @emit("P", deps=[t_fgw, war("pb5")], inc=sp)
    def _(p):
        i = None
        for c in range(NCH):
            i = p.transpose(pb[5][:, c * E:(c + 1) * E], xstage[0][0:E, c * 128:(c + 1) * 128], idf[0:E, 0:E])
        return i
    t_gwT = sp.tok()

    @emit("V", deps=[t_gwT], inc=sv)
    def _(v):
        return v.tensor_copy(out=gwT[:], in_=pb[5][:, 0:NCH * E])
    t_gwTc = sv.tok()
    setread("pb5", t_gwTc)

    t_cwT = [None] * NV
    t_ru_prev = None
    for v_i in range(NV):
        @emit("P", deps=[t_xn[c][v_i] for c in range(NCH)] + [t_gwTc, t_rows, war("pb3")], inc=sp)
        def _(p, v_i=v_i):
            for c in range(NCH):
                p.matmul(pb[3][0:E, :], gwT[:, c * E:(c + 1) * E],
                         xn1[:, c * T + v_i * 512: c * T + v_i * 512 + 512],
                         start=(c == 0), stop=False)
            return p.matmul(pb[3][0:E, :], fgbrow[:], onesrb[:], start=False, stop=True)
        t_lg = sp.tok()

        @emit("V", deps=[t_lg], inc=sv)
        def _(v):
            return v.tensor_copy(out=lgs[:], in_=pb[3][0:E, :])
        t_lgs = sv.tok()
        setread("pb3", t_lgs)

        @emit("P", deps=[t_lgs, war("pb5")], inc=sp)
        def _(p):
            i = None
            for j in range(4):
                i = p.transpose(pb[5][:, j * E:(j + 1) * E], lgs[:, j * 128:(j + 1) * 128], idf[0:E, 0:E])
            return i
        t_lgtr = sp.tok()

        @emit("V", deps=[t_lgtr], inc=sv)
        def _(v, v_i=v_i):
            return v.tensor_copy(out=lgT[:, v_i * 4 * E:(v_i + 1) * 4 * E], in_=pb[5][:, 0:4 * E])
        tk = sv.tok()
        setread("pb5", tk)

        for j in range(4):
            u = 4 * v_i + j

            @emit("V", deps=[tk, t_ru_prev], inc=sv)
            def _(v, u=u):
                return v.tensor_reduce(out=rc["nmax"][:], in_=lgT[:, u * E:(u + 1) * E], op=ALU.max, axis=AX.X, negate=True)
            t1 = sv.tok()

            @emit("A", deps=[t1], inc=sa)
            def _(a, u=u):
                return a.activation(out=qexp[:], in_=lgT[:, u * E:(u + 1) * E], func=AF.Exp, bias=rc["nmax"][:],
                                    accum_out=rc["sum"][:])
            t2 = sa.tok()

            @emit("V", deps=[t2], inc=sv)
            def _(v, u=u):
                def q(i):
                    sv.inc(i, 1)
                    chain(v, sv)
                    return i
                q(v.reciprocal(out=rc["rec"][:], in_=rc["sum"][:]))
                q(v.tensor_scalar(out=qexp[:], in0=qexp[:], scalar1=rc["rec"][:], scalar2=None, op0=ALU.mult))
                q(v.tensor_reduce(out=rc["m1"][:], in_=qexp[:], op=ALU.max, axis=AX.X))
                q(v.tensor_scalar(out=oh1f[:], in0=qexp[:], scalar1=rc["m1"][:], scalar2=None, op0=ALU.is_ge))
                q(v.tensor_scalar(out=rtmp[:], in0=oh1f[:], scalar1=1e30, scalar2=None, op0=ALU.mult))
                q(v.tensor_tensor(out=rtmp[:], in0=qexp[:], in1=rtmp[:], op=ALU.subtract))
                q(v.tensor_reduce(out=rc["m2"][:], in_=rtmp[:], op=ALU.max, axis=AX.X))
                q(v.tensor_scalar(out=oh2f[:], in0=rtmp[:], scalar1=rc["m2"][:], scalar2=None, op0=ALU.is_ge))
                q(v.tensor_copy(out=rc["t1"][:], in_=qexp[:, 0:1]))
                q(v.tensor_copy(out=rc["t2"][:], in_=qexp[:, 1:2]))
                q(v.tensor_scalar(out=oh1f[:], in0=oh1f[:], scalar1=rc["t1"][:], scalar2=None, op0=ALU.mult))
                q(v.tensor_scalar(out=oh2f[:], in0=oh2f[:], scalar1=rc["t2"][:], scalar2=None, op0=ALU.mult))
                return v.tensor_tensor(out=cwmat[:], in0=oh1f[:], in1=oh2f[:], op=ALU.add)
            t3 = sv.tok()

            @emit("P", deps=[t3, war("pb2")], inc=sp)
            def _(p, u=u):
                return p.transpose(pb[2][0:E, 0:128], cwmat[:], idf[:])
            t4 = sp.tok()

            @emit("V", deps=[t4], inc=sv)
            def _(v, u=u):
                return v.tensor_copy(out=cwTb[0:E, u * 128:(u + 1) * 128], in_=pb[2][0:E, 0:128])
            t_ru_prev = sv.tok()
            setread("pb2", t_ru_prev)
    t_cw_done = t_ru_prev
    esG.close()
    t_scope = [sp.tok(), sv.tok(), sa.tok()]


    esC0 = ExitStack()
    xp256 = esC0.enter_context(nc.sbuf_tensor("qxp256", [128, NCH * 256], BF16))
    t_x256 = [None, None]
    for j in range(2):
        @emit("S", deps=[t_x256[0], t_gwT], inc=sx[0], k=16)
        def _(s, j=j):
            return s.dma_start(out=xstage[0][:], in_=pos_in[j * 128:(j + 1) * 128, :])
        t_pl = sx[0].tok()

        @emit("P", deps=[t_pl, war("pb0"), war("pb1")] + t_scope, inc=sp)
        def _(p, j=j):
            for c in range(4):
                p.transpose(pb[0][:, c * 128:(c + 1) * 128], xstage[0][:, c * 128:(c + 1) * 128], idf[:])
            i = None
            for c in range(4):
                i = p.transpose(pb[1][:, c * 128:(c + 1) * 128], xstage[0][:, (4 + c) * 128:(5 + c) * 128], idf[:])
            return i
        t_ptr = sp.tok()

        @emit("V", deps=[t_ptr] + t_scope + [t_xn[c][0] for c in range(NCH)], inc=sv)
        def _(v, j=j):
            xv = xp256[:].rearrange("p (c t) -> p c t", c=NCH)[:, :, j * 128:(j + 1) * 128]
            xnv = xn1[:].rearrange("p (c t) -> p c t", c=NCH)[:, :, j * 128:(j + 1) * 128]
            sv.inc(v.tensor_tensor(out=xv[:, 0:4, :], in0=xnv[:, 0:4, :],
                                   in1=pb[0][:].rearrange("p (c t) -> p c t", c=4), op=ALU.add), 1)
            chain(v, sv)
            return v.tensor_tensor(out=xv[:, 4:8, :], in0=xnv[:, 4:8, :],
                                   in1=pb[1][:].rearrange("p (c t) -> p c t", c=4), op=ALU.add)
        t_x256[j] = sv.tok()
        setread("pb0", t_x256[j])
        setread("pb1", t_x256[j])
    t_xp256 = t_x256[1]

    if True:
        esC = ExitStack()
        cvs = esC.enter_context(nc.sbuf_tensor("qcvs", [128, 4096], F32))
        gmb = esC.enter_context(nc.sbuf_tensor("qgmb", [G, D], BF16))
        cvb = esC.enter_context(nc.sbuf_tensor("qcvb", [128, 4096], BF16))
        wcT = esC.enter_context(nc.sbuf_tensor("qwcT", [128, 32 * 128], BF16))
        t_comp_prev = None
        for oc in range(NCH):
            @emit("S", deps=[t_comp_prev], inc=scv, k=16)
            def _(s, oc=oc):
                return s.dma_start(out=cvs[:], in_=cw_in[oc * 128:(oc + 1) * 128, :, :]
                                   .rearrange("p i k -> p i k"))
            t_cl = scv.tok()

            @emit("V", deps=[t_cl], inc=sv)
            def _(v):
                return v.tensor_copy(out=cvb[:], in_=cvs[:])
            t_cv = sv.tok()

            t_wctr = []
            for kk in range(4):
                for ic in range(NCH):
                    pf = pbf if (kk * NCH + ic) % 2 == 0 else pbf2
                    pfk = "pbf" if (kk * NCH + ic) % 2 == 0 else "pbf2"

                    @emit("P", deps=[t_cv, war(pfk)], inc=sp)
                    def _(p, kk=kk, ic=ic, pf=pf):
                        iv = cvb[:].rearrange("p (i k) -> p k i", k=4)[:, kk, ic * 128:(ic + 1) * 128]
                        return p.transpose(pf[:, 0:128], iv, idb[:])
                    t_tr = sp.tok()

                    @emit("V", deps=[t_tr], inc=sv)
                    def _(v, kk=kk, ic=ic, pf=pf):
                        return v.tensor_copy(out=wcT[:, (kk * NCH + ic) * 128:(kk * NCH + ic + 1) * 128],
                                             in_=pf[:, 0:128])
                    tk2 = sv.tok()
                    setread(pfk, tk2)
                    t_wctr.append(tk2)

            @emit("P", deps=t_wctr + [t_xp256, t_rows, war("pb4")], inc=sp)
            def _(p, oc=oc):
                for kk in range(4):
                    for ic in range(NCH):
                        rv = xp256[:].rearrange("p (c t) -> p c t", c=NCH)[:, ic, 0:256] \
                            .rearrange("p (g k) -> p k g", k=4)[:, kk, :]
                        p.matmul(pb[4][:, 0:64], wcT[:, (kk * NCH + ic) * 128:(kk * NCH + ic + 1) * 128],
                                 rv, start=(kk == 0 and ic == 0), stop=False)
                return p.matmul(pb[4][:, 0:64], cbrow[0:1, oc * 128:(oc + 1) * 128], onesrb[0:1, 0:64],
                                start=False, stop=True)
            t_cmp = sp.tok()

            @emit("V", deps=[t_cmp], inc=sv)
            def _(v, oc=oc):
                return v.tensor_copy(out=gt_s[:, oc * 128: oc * 128 + 64], in_=pb[4][:, 0:64])
            t_comp_prev = sv.tok()
            setread("pb4", t_comp_prev)

        @emit("S", deps=[t_comp_prev], inc=scv, k=16)
        def _(s):
            return s.dma_start(out=cvs[0:G, 0:D], in_=gm_in)
        t_gml = scv.tok()

        @emit("V", deps=[t_gml], inc=sv)
        def _(v):
            return v.tensor_copy(out=gmb[:], in_=cvs[0:G, 0:D])
        t_gmc = sv.tok()

        for oc in range(NCH):
            pf = pbf if oc % 2 == 0 else pbf2
            pfk = "pbf" if oc % 2 == 0 else "pbf2"

            @emit("P", deps=[t_gmc, war(pfk)], inc=sp)
            def _(p, oc=oc, pf=pf):
                return p.transpose(pf[:, 0:64], gmb[:, oc * 128:(oc + 1) * 128], idb[0:G, 0:G])
            t_tr = sp.tok()

            @emit("V", deps=[t_tr], inc=sv)
            def _(v, oc=oc, pf=pf):
                return v.tensor_copy(out=gt_s[:, oc * 128 + 64:(oc + 1) * 128], in_=pf[:, 0:64])
            tk2 = sv.tok()
            setread(pfk, tk2)
        t_gt_done = sv.tok()

        for h in range(NH):
            pf = pbf if h % 2 == 0 else pbf2
            pfk = "pbf" if h % 2 == 0 else "pbf2"

            @emit("P", deps=[t_gt_done, war(pfk)], inc=sp)
            def _(p, h=h, pf=pf):
                sl = gt_s[64 * (h % 2): 64 * (h % 2) + 64, (h // 2) * 128:(h // 2) * 128 + 128]
                return p.transpose(pf[:, 0:64], sl, idb[64 * (h % 2):64 * (h % 2) + 64,
                                                       64 * (h % 2):64 * (h % 2) + 64])
            t_tr = sp.tok()

            @emit("V", deps=[t_tr], inc=sv)
            def _(v, h=h, pf=pf):
                return v.tensor_copy(out=gtT[:, h * HD:(h + 1) * HD], in_=pf[:, 0:64])
            tk2 = sv.tok()
            setread(pfk, tk2)
        t_gtT = sv.tok()
        esC.close()


    esC0.close()

    esH = ExitStack()

    def sbH(name, shape, dtp):
        return esH.enter_context(nc.sbuf_tensor(name, shape, dtp))

    xp_h = sbH("qxp_h", [128, NCH * TH], BF16)
    local_h = sbH("qlocal_h", [128, NCH * TH], BF16)
    glob_h = sbH("qglob_h", [128, NCH * TH], BF16)
    abuf = [sbH("qabuf0", [128, 4 * 512], BF16), sbH("qabuf1", [128, 4 * 512], BF16)]
    vtb = [sbH("qvtb0", [128, 512], BF16), sbH("qvtb1", [128, 512], BF16)]
    rbc2 = sbH("qrbc2", [128, 512], BF16)
    recips = sbH("qrecips", [128, 512], F32)
    nrow2 = sbH("qnrow2", [1, TH], F32)
    negm8 = sbH("qnegm8", [1, TH], BF16)
    mcell = sbH("qmcell", [1, 1], F32)
    mixt = sbH("qmixt", [128, 512], BF16)
    t1t = sbH("qt1t", [128, 512], BF16)
    _gwb0 = sbH("qgwb0", [128, 2048], BF16)
    gwb = [_gwb0, _gwb0]
    xpsq = sbH("qxpsq", [128, TH], BF16)

    t_gt_done = None
    t_gtT = None
    t_xp_reader = None
    t_loc_reader = None
    t_half_done = None
    t_store = {}
    t_slab_reader = {}
    t_out = [[None] * NCH for _ in range(2)]

    for hf in range(2):
        TOFF = hf * TH
        t_xp = [None] * 8
        for j in range(8):
            u = hf * 8 + j
            pp = j % 2

            @emit("S", deps=[t_store.get(("pos", 0)), t_store.get(("pos", 1)), t_xp256,
                             t_xp_reader if hf else None], inc=sx[pp], k=16)
            def _(s, u=u, pp=pp):
                return s.dma_start(out=xstage[pp][:], in_=pos_in[u * 128:(u + 1) * 128, :])
            t_pl = sx[pp].tok()

            bk_ = "pb%d" % (2 * pp)
            bk2_ = "pb%d" % (2 * pp + 1)

            @emit("P", deps=[t_pl, war(bk_), war(bk2_)], inc=sp)
            def _(p, pp=pp):
                b0, b1 = pb[2 * pp], pb[2 * pp + 1]
                for c in range(4):
                    p.transpose(b0[:, c * 128:(c + 1) * 128], xstage[pp][:, c * 128:(c + 1) * 128], idf[:])
                i = None
                for c in range(4):
                    i = p.transpose(b1[:, c * 128:(c + 1) * 128], xstage[pp][:, (4 + c) * 128:(5 + c) * 128], idf[:])
                return i
            t_ptr = sp.tok()
            t_store[("pos", pp)] = t_ptr

            @emit("V", deps=[t_ptr] + t_scope + [t_xn[c][u // 4] for c in range(NCH)], inc=sv)
            def _(v, u=u, j=j, pp=pp):
                b0, b1 = pb[2 * pp], pb[2 * pp + 1]
                xv = xp_h[:].rearrange("p (c t) -> p c t", c=NCH)[:, :, j * 128:(j + 1) * 128]
                xnv = xn1[:].rearrange("p (c t) -> p c t", c=NCH)[:, :, u * 128:(u + 1) * 128]
                sv.inc(v.tensor_tensor(out=xv[:, 0:4, :], in0=xnv[:, 0:4, :],
                                       in1=b0[:].rearrange("p (c t) -> p c t", c=4), op=ALU.add), 1)
                chain(v, sv)
                return v.tensor_tensor(out=xv[:, 4:8, :], in0=xnv[:, 4:8, :],
                                       in1=b1[:].rearrange("p (c t) -> p c t", c=4), op=ALU.add)
            t_xp[j] = sv.tok()
            setread(bk_, t_xp[j])
            setread(bk2_, t_xp[j])

        t_xp_all = t_xp[7]

        NW = 2
        NTC = TH // 128
        for hp in range(NCH):
            @emit("V", deps=[t_xp_all, t_loc_reader, war_a_sq(pr, hp)], inc=sv)
            def _(v, hp=hp):
                return v.tensor_tensor(out=xpsq[:], in0=xp_h[:, hp * TH:(hp + 1) * TH],
                                       in1=xp_h[:, hp * TH:(hp + 1) * TH], op=ALU.mult)
            t_sqh = sv.tok()

            for sub in range(2):
                h = 2 * hp + sub
                base = 64 * sub
                @emit("P", deps=[t_sqh, war("pb3"), war("pb4")], inc=sp)
                def _(p, base=base):
                    p.matmul(pb[3][0:1, 0:512], onescol[base:base + 64, :], xpsq[base:base + 64, 0:512],
                             start=True, stop=True)
                    return p.matmul(pb[4][0:1, 0:512], onescol[base:base + 64, :], xpsq[base:base + 64, 512:1024],
                                    start=True, stop=True)
                t_nr = sp.tok()
                pr[("xpsq",)] = t_nr

                @emit("V", deps=[t_nr], inc=sv)
                def _(v):
                    sv.inc(v.tensor_copy(out=nrow2[0:1, 0:512], in_=pb[3][0:1, 0:512]), 1)
                    sv.inc(v.tensor_copy(out=nrow2[0:1, 512:1024], in_=pb[4][0:1, 0:512]), 1)
                    chain(v, sv)
                    return v.tensor_reduce(out=mcell[:], in_=nrow2[:], op=ALU.max, axis=AX.X)
                t_nr2 = sv.tok()
                setread("pb3", t_nr2)
                setread("pb4", t_nr2)

                @emit("V", deps=[t_nr2], inc=sv)
                def _(v):
                    return v.tensor_scalar(out=negm8[:], in0=nrow2[:], scalar1=mcell[0:1, 0:1],
                                           scalar2=-0.5, op0=ALU.add, op1=ALU.mult)
                t_nm = sv.tok()

                for w in range(NW):
                    woff = w * 512
                    for kc in range(4):
                        pf, pfk = (pbf, "pbf") if (w * 4 + kc) % 2 == 0 else (pbf2, "pbf2")

                        @emit("P", deps=[t_xp_all, war(pfk)], inc=sp)
                        def _(p, hp=hp, base=base, woff=woff, kc=kc, pf=pf):
                            iv = xp_h[base:base + 64, hp * TH + woff + kc * 128: hp * TH + woff + (kc + 1) * 128]
                            return p.transpose(pf[:, 0:64], iv, idb[base:base + 64, base:base + 64])
                        t_tr = sp.tok()

                        @emit("V", deps=[t_tr, pr.get(("av", sub))], inc=sv)
                        def _(v, w=w, kc=kc, sub=sub, pf=pf):
                            return v.tensor_copy(out=vtb[sub][:, (w * 4 + kc) * 64:(w * 4 + kc + 1) * 64],
                                                 in_=pf[:, 0:64])
                        tk2 = sv.tok()
                        setread(pfk, tk2)
                t_vtb = sv.tok()

                for w in range(NW):
                    woff = w * 512
                    bs, bsk = (pb[0], "pb0") if sub == 0 else (pb[1], "pb1")
                    for kc in range(4):
                        @emit("P", deps=[t_nm, t_xp_all, war(bsk)], inc=sp)
                        def _(p, hp=hp, base=base, woff=woff, kc=kc, bs=bs):
                            p.matmul(bs[:, 0:512],
                                     xp_h[base:base + 64, hp * TH + woff + kc * 128: hp * TH + woff + (kc + 1) * 128],
                                     xp_h[base:base + 64, hp * TH + woff: hp * TH + woff + 512],
                                     start=True, stop=False)
                            return p.matmul(bs[:, 0:512], onescb[:], negm8[0:1, woff:woff + 512],
                                            start=False, stop=True)
                        t_sc = sp.tok()

                        @emit("A", deps=[t_sc, war_ab(pr, sub)], inc=sa)
                        def _(a, kc=kc, sub=sub, bs=bs):
                            return a.activation(out=abuf[sub][:, kc * 512:(kc + 1) * 512], in_=bs[:, 0:512],
                                                func=AF.Exp, scale=np.float32(0.125))
                        t_ex = sa.tok()
                        setread(bsk, t_ex)

                        @emit("P", deps=[t_ex, war("pb3")], inc=sp)
                        def _(p, kc=kc, sub=sub, base=base):
                            return p.matmul(pb[3][base:base + 1, 0:512], onescol[:],
                                            abuf[sub][:, kc * 512:(kc + 1) * 512],
                                            start=(kc == 0), stop=(kc == 3))
                        t_sum = sp.tok()

                    @emit("V", deps=[t_sum, pr.get(("recr", sub))], inc=sv)
                    def _(v, base=base):
                        return v.reciprocal(out=recips[base:base + 1, 0:512], in_=pb[3][base:base + 1, 0:512])
                    t_rc = sv.tok()
                    setread("pb3", t_rc)

                    @emit("P", deps=[t_rc, war("pb4")], inc=sp)
                    def _(p, base=base, sub=sub):
                        if sub == 0:
                            return p.matmul(pb[4][0:64, 0:512], onesc[0:1, 0:64], recips[0:1, 0:512],
                                            start=True, stop=True)
                        return p.matmul(pb[4][64:128, 0:512], ones2d[64:65, 0:64], recips[64:65, 0:512],
                                        start=True, stop=True)
                    t_bcm = sp.tok()
                    pr[("recr", sub)] = t_bcm

                    @emit("V", deps=[t_bcm, pr.get(("rbcr", sub))], inc=sv)
                    def _(v, base=base):
                        return v.tensor_copy(out=rbc2[base:base + 64, :], in_=pb[4][base:base + 64, 0:512])
                    t_rb = sv.tok()
                    setread("pb4", t_rb)

                    @emit("P", deps=[t_vtb, sa.tok(), war("pb2")], inc=sp)
                    def _(p, w=w, sub=sub, base=base):
                        i = None
                        for kc in range(4):
                            i = p.matmul(pb[2][base:base + 64, 0:512],
                                         vtb[sub][:, (w * 4 + kc) * 64:(w * 4 + kc + 1) * 64],
                                         abuf[sub][:, kc * 512:(kc + 1) * 512],
                                         start=(kc == 0), stop=(kc == 3))
                        return i
                    t_av = sp.tok()
                    pr[("av", sub)] = t_av
                    pr[("abr", sub)] = t_av

                    @emit("V", deps=[t_av, t_rb], inc=sv)
                    def _(v, hp=hp, w=w, base=base):
                        return v.tensor_tensor(out=local_h[base:base + 64, hp * TH + w * 512: hp * TH + (w + 1) * 512],
                                               in0=pb[2][base:base + 64, 0:512],
                                               in1=rbc2[base:base + 64, :], op=ALU.mult)
                    t_lw = sv.tok()
                    setread("pb2", t_lw)
                    pr[("rbcr", sub)] = t_lw

            for tc in range(NTC):
                for sub in range(2):
                    h = 2 * hp + sub
                    base = 64 * sub

                    @emit("P", deps=[t_xp_all, t_gt_done, war("pb0")], inc=sp)
                    def _(p, hp=hp, base=base, tc=tc):
                        return p.matmul(pb[0][:, 0:128],
                                        xp_h[base:base + 64, hp * TH + tc * 128: hp * TH + (tc + 1) * 128],
                                        gt_s[base:base + 64, hp * 128:(hp + 1) * 128],
                                        start=True, stop=True)
                    t_gs = sp.tok()

                    @emit("V", deps=[t_gs], inc=sv)
                    def _(v):
                        return v.tensor_reduce(out=rc["nmax"][:], in_=pb[0][:, 0:128], op=ALU.max,
                                               axis=AX.X, negate=True)
                    t_gm2 = sv.tok()

                    @emit("A", deps=[t_gm2, pr.get(("mixr",))], inc=sa)
                    def _(a):
                        return a.activation(out=mixt[:, 0:128], in_=pb[0][:, 0:128], func=AF.Exp,
                                            bias=rc["nmax"][:], scale=np.float32(0.125),
                                            accum_out=rc["sum"][:])
                    t_ge = sa.tok()
                    setread("pb0", t_ge)

                    @emit("V", deps=[t_ge, pr.get(("t1tr",))], inc=sv)
                    def _(v):
                        sv.inc(v.reciprocal(out=rc["rec"][:], in_=rc["sum"][:]), 1)
                        chain(v, sv)
                        return v.tensor_scalar(out=t1t[:, 0:128], in0=mixt[:, 0:128],
                                               scalar1=rc["rec"][:], scalar2=None, op0=ALU.mult)
                    t_gn = sv.tok()
                    pr[("mixr",)] = t_gn

                    @emit("P", deps=[t_gn, war("pbf")], inc=sp)
                    def _(p):
                        return p.transpose(pbf[:, 0:128], t1t[:, 0:128], idb[:])
                    t_gt2 = sp.tok()
                    pr[("t1tr",)] = t_gt2

                    @emit("V", deps=[t_gt2, pr.get(("gatr",))], inc=sv)
                    def _(v):
                        return v.tensor_copy(out=mixt[:, 128:256], in_=pbf[:, 0:128])
                    t_gaT = sv.tok()
                    setread("pbf", t_gaT)

                    @emit("P", deps=[t_gaT, t_gtT, war("pb2") if sub == 0 else None], inc=sp)
                    def _(p, h=h, base=base):
                        return p.matmul(pb[2][base:base + 64, 0:128], gtT[:, h * HD:(h + 1) * HD],
                                        mixt[:, 128:256], start=True, stop=True)
                    t_gav = sp.tok()
                    pr[("gav", sub)] = t_gav
                    pr[("gatr",)] = t_gav

                @emit("V", deps=[pr[("gav", 0)], pr[("gav", 1)]], inc=sv)
                def _(v, hp=hp, tc=tc):
                    return v.tensor_copy(out=glob_h[:, hp * TH + tc * 128: hp * TH + (tc + 1) * 128],
                                         in_=pb[2][:, 0:128])
                t_gw2 = sv.tok()
                setread("pb2", t_gw2)
                pr[("globc", hp, tc)] = t_gw2

        t_attn_done = sv.tok()

        t_dbgh = None
        if "xp" in dbg:
            @emit("G", deps=[t_attn_done])
            def _(g, TOFF=TOFF):
                for c in range(NCH):
                    sdbg.inc(g.dma_start(out=dbg["xp"][c * 128:(c + 1) * 128, TOFF:TOFF + TH],
                                        in_=xp_h[:, c * TH:(c + 1) * TH]), 16)
                    sdbg.inc(g.dma_start(out=dbg["local"][c * 128:(c + 1) * 128, TOFF:TOFF + TH],
                                        in_=local_h[:, c * TH:(c + 1) * TH]), 16)
                    sdbg.inc(g.dma_start(out=dbg["glob"][c * 128:(c + 1) * 128, TOFF:TOFF + TH],
                                        in_=glob_h[:, c * TH:(c + 1) * TH]), 16)
                return None
            t_dbgh = sdbg.tok()

        def mixsl(dc):
            return (abuf[0] if dc < 4 else abuf[1])[:, (dc % 4) * 512:(dc % 4) * 512 + 512]

        for vt in range(2):
            t_mix = [None] * NCH
            for dc in range(NCH):
                @emit("G", deps=[t_slab_reader.get(("gwb", 0))], inc=sgd, k=16)
                def _(g, dc=dc):
                    srcv = gw_in[:, dc * 128:(dc + 1) * 128].rearrange("(c p) d -> p c d", p=128)
                    return g.dma_start(out=gwb[0][:].rearrange("p (c d) -> p c d", c=16), in_=srcv)
                t_gc = sgd.tok()

                bs, bsk = (pb[0], "pb0") if dc % 2 == 0 else (pb[1], "pb1")

                @emit("P", deps=[t_gc, t_attn_done, t_gbc, war(bsk)], inc=sp)
                def _(p, dc=dc, vt=vt, bs=bs):
                    i = None
                    for cc in range(16):
                        rhs = (local_h if cc < 8 else glob_h)[:, (cc % 8) * TH + vt * 512: (cc % 8) * TH + vt * 512 + 512]
                        i = p.matmul(bs[:, 0:512], gwb[0][:, cc * 128:(cc + 1) * 128], rhs,
                                     start=(cc == 0), stop=(cc == 15))
                    return i
                t_gmm = sp.tok()
                t_slab_reader[("gwb", 0)] = t_gmm

                @emit("A", deps=[t_gmm, t_dbgh, pr.get(("aout", dc, vt - 1)) if vt > 0 else None], inc=sa)
                def _(a, dc=dc, bs=bs):
                    return a.activation(out=mixsl(dc), in_=bs[:, 0:512], func=AF.Sigmoid, bias=gbcol[:, dc:dc + 1])
                t_mix[dc] = sa.tok()
                setread(bsk, t_mix[dc])

            for dc in range(NCH):
                @emit("V", deps=[t_mix[dc], t_mix[NCH - 1], pr.get(("aoutprev",))], inc=sv)
                def _(v, dc=dc, vt=vt):
                    lsl = local_h[:, dc * TH + vt * 512: dc * TH + vt * 512 + 512]
                    gsl = glob_h[:, dc * TH + vt * 512: dc * TH + vt * 512 + 512]
                    sv.inc(v.tensor_tensor(out=t1t[:], in0=lsl, in1=gsl, op=ALU.subtract), 1)
                    chain(v, sv)
                    sv.inc(v.tensor_tensor(out=t1t[:], in0=mixsl(dc), in1=t1t[:], op=ALU.mult), 1)
                    chain(v, sv)
                    return v.tensor_tensor(out=lsl, in0=gsl, in1=t1t[:], op=ALU.add)
                pr[("aout", dc, vt)] = sv.tok()
                pr[("aoutprev",)] = pr[("aout", dc, vt)]
        t_aout = sv.tok()

        owb = gwb
        for dc in range(NCH):
            pp = dc % 2

            @emit("G", deps=[t_slab_reader.get(("gwb", 0))], inc=sgd, k=16)
            def _(g, dc=dc, pp=pp):
                srcv = ow_in[:, dc * 128:(dc + 1) * 128].rearrange("(c p) d -> p c d", p=128)
                return g.dma_start(out=owb[pp][:, 0:1024].rearrange("p (c d) -> p c d", c=NCH), in_=srcv)
            t_oc = sgd.tok()

            for vt in range(2):
                bs, bsk = (pb[0], "pb0") if vt == 0 else (pb[1], "pb1")

                @emit("P", deps=[t_oc, t_aout, t_gbc, war(bsk)], inc=sp)
                def _(p, dc=dc, vt=vt, pp=pp, bs=bs):
                    i = None
                    for cc in range(NCH):
                        i = p.matmul(bs[:, 0:512], owb[pp][:, cc * 128:(cc + 1) * 128],
                                     local_h[:, cc * TH + vt * 512: cc * TH + vt * 512 + 512],
                                     start=(cc == 0), stop=(cc == NCH - 1))
                    return i
                t_omm = sp.tok()
                t_slab_reader[("gwb", 0)] = t_omm

                @emit("V", deps=[t_omm, pr.get(("opev",))], inc=sv)
                def _(v, dc=dc, vt=vt, bs=bs, TOFF=TOFF):
                    sl = slice(dc * T + TOFF + vt * 512, dc * T + TOFF + vt * 512 + 512)
                    sv.inc(v.tensor_scalar(out=recips[:], in0=bs[:, 0:512],
                                           scalar1=obcol[:, dc:dc + 1], scalar2=None, op0=ALU.add), 1)
                    chain(v, sv)
                    return v.tensor_tensor(out=resid[:, sl], in0=resid[:, sl], in1=recips[:], op=ALU.add)
                t_out[hf][dc] = sv.tok()
                pr[("opev",)] = t_out[hf][dc]
                setread(bsk, t_out[hf][dc])
        t_xp_reader = sp.tok()
        t_loc_reader = t_xp_reader
        t_half_done = t_xp_reader

    esH.close()

    esF = ExitStack()

    def sbF(name, shape, dtp):
        return esF.enter_context(nc.sbuf_tensor(name, shape, dtp))

    h1b = sbF("qh1b", [128, NCH * TH], BF16)
    h2b = sbF("qh2b", [128, NCH * TH], BF16)
    fws = [sbF("qfws0", [128, 1024], F32), sbF("qfws1", [128, 1024], F32)]
    fwb = [sbF("qfwb0", [128, 1024], BF16), sbF("qfwb1", [128, 1024], BF16)]
    cwb = sbF("qcwb", [128, TH], BF16)
    b1r = sbF("qb1r", [1, D], BF16)
    b2r = sbF("qb2r", [1, D], BF16)
    b1cL = sbF("qb1cL", [128, NCH], F32)
    b2cL = sbF("qb2cL", [128, NCH], F32)
    fsc = sbF("qfsc", [128, 512], F32)

    rtok = {}
    for hf in range(2):
        for dc in range(NCH):
            for vt in range(2):
                rtok[(hf, dc, vt)] = t_out[hf][dc]

    t_join = sv.tok()
    t_joinp = sp.tok()
    t_joina = sa.tok()
    t_h1_reader = t_joinp
    t_h2_reader = t_joinp
    t_cwb_reader = t_joinp
    t_br_reader = t_joinp
    t_sl_reader = {("f", 0): (sv, max(t_join[1], 0)), ("f", 1): (sv, max(t_join[1], 0))}
    for kk in (("f", 0), ("f", 1)):
        pass
    t_sl_reader = {("f", 0): t_join, ("f", 1): t_join, ("fb", 0): t_joinp, ("fb", 1): t_joinp}
    for hf in range(2):
        TOFF = hf * TH
        for e in range(E):
            @emit("P", deps=[t_cw_done, t_erows, t_cwb_reader, war("pb5"), war("pb4")], inc=sp)
            def _(p, e=e, TOFF=TOFF):
                p.matmul(pb[5][:, 0:512], erowS[:, e * 128:(e + 1) * 128], cwTb[0:E, TOFF: TOFF + 512],
                         start=True, stop=True)
                return p.matmul(pb[4][:, 0:512], erowS[:, e * 128:(e + 1) * 128],
                                cwTb[0:E, TOFF + 512: TOFF + 1024], start=True, stop=True)
            t_cwm = sp.tok()

            @emit("V", deps=[t_cwm, war("pb5"), war("pb4")], inc=sv)
            def _(v):
                sv.inc(v.tensor_copy(out=cwb[:, 0:512], in_=pb[5][:, 0:512]), 1)
                chain(v, sv)
                return v.tensor_copy(out=cwb[:, 512:1024], in_=pb[4][:, 0:512])
            t_cwb = sv.tok()
            setread("pb5", t_cwb)
            setread("pb4", t_cwb)

            @emit("G", deps=[t_br_reader])
            def _(g, e=e):
                sgd.inc(g.dma_start(out=b1r[:], in_=fb1[e:e + 1, :]), 16)
                sgd.inc(g.dma_start(out=b2r[:], in_=fb2[e:e + 1, :]), 16)
                return None
            t_br0 = sgd.tok()

            @emit("P", deps=[t_br0, war("pb4")], inc=sp)
            def _(p, e=e):
                for hc in range(NCH):
                    p.matmul(pb[4][:, hc:hc + 1], b1r[0:1, hc * 128:(hc + 1) * 128],
                             onesrb[0:1, 0:1], start=True, stop=True)
                i = None
                for dc in range(NCH):
                    i = p.matmul(pb[4][:, NCH + dc:NCH + dc + 1], b2r[0:1, dc * 128:(dc + 1) * 128],
                                 onesrb[0:1, 0:1], start=True, stop=True)
                return i
            t_brm = sp.tok()

            @emit("V", deps=[t_brm], inc=sv)
            def _(v):
                sv.inc(v.tensor_copy(out=b1cL[:], in_=pb[4][:, 0:NCH]), 1)
                chain(v, sv)
                return v.tensor_copy(out=b2cL[:], in_=pb[4][:, NCH:2 * NCH])
            t_br = sv.tok()
            setread("pb4", t_br)

            t_h1 = {}
            for hc in range(NCH):
                pp = hc % 2

                @emit("S", deps=[t_sl_reader.get(("f", pp))], inc=swl[pp], k=16)
                def _(s, e=e, hc=hc, pp=pp):
                    srcv = fw1[e:e + 1, :, hc * 128:(hc + 1) * 128].rearrange("o (c p) h -> p o c h", p=128)
                    return s.dma_start(out=fws[pp][:].rearrange("p (o c h) -> p o c h", o=1, c=NCH), in_=srcv)
                t_wl = swl[pp].tok()

                @emit("V", deps=[t_wl, t_sl_reader.get(("fb", pp))], inc=sv)
                def _(v, pp=pp):
                    return v.tensor_copy(out=fwb[pp][:], in_=fws[pp][:])
                t_wc = sv.tok()
                t_sl_reader[("f", pp)] = t_wc

                for vt in range(2):
                    bi = (hc % 2) * 2 + vt
                    bs, bsk = pb[bi], "pb%d" % bi

                    @emit("P", deps=[t_wc, t_br, war(bsk)] + [t_xn[c][(TOFF + vt * 512) // 512] for c in range(NCH)], inc=sp)
                    def _(p, hc=hc, vt=vt, pp=pp, bs=bs, TOFF=TOFF):
                        i = None
                        for c in range(NCH):
                            i = p.matmul(bs[:, 0:512], fwb[pp][:, c * 128:(c + 1) * 128],
                                         xn1[:, c * T + TOFF + vt * 512: c * T + TOFF + vt * 512 + 512],
                                         start=(c == 0), stop=(c == NCH - 1))
                        return i
                    t_mm = sp.tok()
                    t_sl_reader[("fb", pp)] = t_mm

                    @emit("V", deps=[t_mm, t_h1_reader], inc=sv)
                    def _(v, hc=hc, vt=vt, bs=bs):
                        return v.tensor_scalar(out=h1b[:, hc * TH + vt * 512: hc * TH + vt * 512 + 512],
                                               in0=bs[:, 0:512], scalar1=b1cL[:, hc:hc + 1],
                                               scalar2=0.0, op0=ALU.add, op1=ALU.max)
                    t_h1[(hc, vt)] = sv.tok()
                    setread(bsk, t_h1[(hc, vt)])

            t_h2 = {}
            for hc in range(NCH):
                pp = hc % 2

                @emit("S", deps=[t_sl_reader.get(("f", pp))], inc=swl[pp], k=16)
                def _(s, e=e, hc=hc, pp=pp):
                    srcv = fw2[e:e + 1, :, hc * 128:(hc + 1) * 128].rearrange("o (c p) h -> p o c h", p=128)
                    return s.dma_start(out=fws[pp][:].rearrange("p (o c h) -> p o c h", o=1, c=NCH), in_=srcv)
                t_wl = swl[pp].tok()

                @emit("V", deps=[t_wl, t_sl_reader.get(("fb", pp))], inc=sv)
                def _(v, pp=pp):
                    return v.tensor_copy(out=fwb[pp][:], in_=fws[pp][:])
                t_wc = sv.tok()
                t_sl_reader[("f", pp)] = t_wc

                for vt in range(2):
                    bi = (hc % 2) * 2 + vt
                    bs, bsk = pb[bi], "pb%d" % bi
                    hdeps = [t_h1[(c, vt)] for c in range(NCH)]

                    @emit("P", deps=[t_wc, t_br, war(bsk)] + hdeps, inc=sp)
                    def _(p, hc=hc, vt=vt, pp=pp, bs=bs):
                        i = None
                        for c in range(NCH):
                            i = p.matmul(bs[:, 0:512], fwb[pp][:, c * 128:(c + 1) * 128],
                                         h1b[:, c * TH + vt * 512: c * TH + vt * 512 + 512],
                                         start=(c == 0), stop=(c == NCH - 1))
                        return i
                    t_mm = sp.tok()
                    t_sl_reader[("fb", pp)] = t_mm

                    @emit("V", deps=[t_mm, t_h2_reader], inc=sv)
                    def _(v, hc=hc, vt=vt, bs=bs):
                        return v.tensor_scalar(out=h2b[:, hc * TH + vt * 512: hc * TH + vt * 512 + 512],
                                               in0=bs[:, 0:512], scalar1=b2cL[:, hc:hc + 1],
                                               scalar2=0.0, op0=ALU.add, op1=ALU.max)
                    t_h2[(hc, vt)] = sv.tok()
                    setread(bsk, t_h2[(hc, vt)])
            t_h1_reader = t_mm

            for dc in range(NCH):
                pp = dc % 2

                @emit("S", deps=[t_sl_reader.get(("f", pp))], inc=swl[pp], k=16)
                def _(s, e=e, dc=dc, pp=pp):
                    srcv = fw3[e:e + 1, :, dc * 128:(dc + 1) * 128].rearrange("o (c p) h -> p o c h", p=128)
                    return s.dma_start(out=fws[pp][:].rearrange("p (o c h) -> p o c h", o=1, c=NCH), in_=srcv)
                t_wl = swl[pp].tok()

                @emit("V", deps=[t_wl, t_sl_reader.get(("fb", pp))], inc=sv)
                def _(v, pp=pp):
                    return v.tensor_copy(out=fwb[pp][:], in_=fws[pp][:])
                t_wc = sv.tok()
                t_sl_reader[("f", pp)] = t_wc

                for vt in range(2):
                    bs, bsk = (pb[2], "pb2") if vt == 0 else (pb[3], "pb3")
                    hdeps = [t_h2[(c, vt)] for c in range(NCH)]

                    @emit("P", deps=[t_wc, war(bsk)] + hdeps, inc=sp)
                    def _(p, dc=dc, vt=vt, pp=pp, bs=bs):
                        i = None
                        for c in range(NCH):
                            i = p.matmul(bs[:, 0:512], fwb[pp][:, c * 128:(c + 1) * 128],
                                         h2b[:, c * TH + vt * 512: c * TH + vt * 512 + 512],
                                         start=(c == 0), stop=(c == NCH - 1))
                        return i
                    t_mm3 = sp.tok()
                    t_sl_reader[("fb", pp)] = t_mm3

                    @emit("V", deps=[t_mm3, t_cwb, rtok[(hf, dc, vt)], pr.get(("fscprev",))], inc=sv)
                    def _(v, dc=dc, vt=vt, bs=bs, TOFF=TOFF):
                        sl = slice(dc * T + TOFF + vt * 512, dc * T + TOFF + vt * 512 + 512)
                        sv.inc(v.tensor_tensor(out=fsc[:], in0=bs[:, 0:512],
                                               in1=cwb[:, vt * 512:(vt + 1) * 512], op=ALU.mult), 1)
                        chain(v, sv)
                        return v.tensor_tensor(out=resid[:, sl], in0=resid[:, sl], in1=fsc[:], op=ALU.add)
                    rtok[(hf, dc, vt)] = sv.tok()
                    pr[("fscprev",)] = rtok[(hf, dc, vt)]
                    setread(bsk, rtok[(hf, dc, vt)])
            t_h2_reader = t_mm3
            t_cwb_reader = t_mm3 if False else rtok[(hf, NCH - 1, 1)]
            t_br_reader = t_mm

        for dc in range(NCH):
            for vt in range(2):
                bs, bsk = (pb[2], "pb2") if vt == 0 else (pb[3], "pb3")

                @emit("P", deps=[t_cw_done, war(bsk), t_rows], inc=sp)
                def _(p, dc=dc, vt=vt, bs=bs, TOFF=TOFF):
                    return p.matmul(bs[:, 0:512], b3S[:, dc * 128:(dc + 1) * 128],
                                    cwTb[0:E, TOFF + vt * 512: TOFF + vt * 512 + 512],
                                    start=True, stop=True)
                t_b3 = sp.tok()

                @emit("V", deps=[t_b3, rtok[(hf, dc, vt)]], inc=sv)
                def _(v, dc=dc, vt=vt, bs=bs, TOFF=TOFF):
                    sl = slice(dc * T + TOFF + vt * 512, dc * T + TOFF + vt * 512 + 512)
                    return v.tensor_tensor(out=resid[:, sl], in0=resid[:, sl], in1=bs[:, 0:512], op=ALU.add)
                rtok[(hf, dc, vt)] = sv.tok()
                setread(bsk, rtok[(hf, dc, vt)])

    esF.close()

    t_sst = {}
    for u in range(NU):
        hf = u // 8
        vt = (u % 8) // 4
        pp = u % 2
        deps = [rtok[(hf, dc, vt)] for dc in range(NCH)] + [war("pb4"), war("pb5")]

        @emit("P", deps=deps, inc=sp)
        def _(p, u=u):
            for c in range(4):
                p.transpose(pb[4][:, c * 128:(c + 1) * 128], resid[:, c * T + u * 128: c * T + (u + 1) * 128], idf[:])
            i = None
            for c in range(4):
                i = p.transpose(pb[5][:, c * 128:(c + 1) * 128],
                                resid[:, (4 + c) * T + u * 128: (4 + c) * T + (u + 1) * 128], idf[:])
            return i
        t_str = sp.tok()

        @emit("V", deps=[t_str, t_sst.get(("x2", 0)), t_sst.get(("x2", 1))], inc=sv)
        def _(v, pp=pp):
            sv.inc(v.tensor_copy(out=xstage[pp][:, 0:512], in_=pb[4][:, 0:512]), 1)
            chain(v, sv)
            return v.tensor_copy(out=xstage[pp][:, 512:1024], in_=pb[5][:, 0:512])
        t_cp = sv.tok()
        setread("pb4", t_cp)
        setread("pb5", t_cp)

        @emit("S", deps=[t_cp], inc=st, k=16)
        def _(s, u=u, pp=pp):
            return s.dma_start(out=x2_out[u * 128:(u + 1) * 128, :], in_=xstage[pp][:])
        t_sst[("x2", pp)] = st.tok()

    toks = [s_.tok() for s_ in ALLSEMS]
    for engname in ENG:
        @emit(engname, deps=toks)
        def _(e):
            return None

    es.close()
    return nc



def kernel_fused(**inputs):
    consts = host_consts()
    erows = _erows_const()
    z = inputs
    nc = build_full()
    in_maps = []
    for b in range(8):
        m = {
            "x": np.ascontiguousarray(z["L0_attn_x"][b]),
            "Wq": z["L0_attn_Wq"], "bq": z["L0_attn_bq"],
            "Wk": z["L0_attn_Wk"], "bk": z["L0_attn_bk"],
            "Wv": z["L0_attn_Wv"], "bv": z["L0_attn_bv"],
            "dec": z["L0_attn_decay"], "Wo": z["L0_attn_Wo"], "bo": z["L0_attn_bo"],
            "Wr": z["L0_ffn_w_router"], "w1": z["L0_ffn_w1"], "b1": z["L0_ffn_b1"],
            "w2": z["L0_ffn_w2"], "b2": z["L0_ffn_b2"],
            "cpref": cpref_for_core(b),
            "pos": z["L1_attn_pos_emb"], "gm": np.ascontiguousarray(z["L1_attn_global_memory"][0]),
            "convw": z["L1_attn_conv_w"], "convb": z["L1_attn_conv_b"],
            "gatew": z["L1_attn_gate_w"], "gateb": z["L1_attn_gate_b"],
            "outw": z["L1_attn_out_w"], "outb": z["L1_attn_out_b"],
            "fgw": z["L1_ffn_gate_w"], "fgb": z["L1_ffn_gate_b"],
            "fw1": z["L1_ffn_w1"], "fb1": z["L1_ffn_b1"],
            "fw2": z["L1_ffn_w2"], "fb2": z["L1_ffn_b2"],
            "fw3": z["L1_ffn_w3"], "fb3": z["L1_ffn_b3"],
            "erows": erows,
        }
        m.update(consts)
        in_maps.append(m)
    res = run_bass_kernel_spmd(nc, in_maps, core_ids=list(range(8)))
    out = np.stack([np.asarray(res.results[b]["x2"]) for b in range(8)], axis=0)
    return out.astype(np.float32)
